# version 05 (final)
# speedup vs baseline: 1.0151x; 1.0151x over previous
import math
from contextlib import ExitStack

import numpy as np

import concourse.bass as bass
import concourse.mybir as mybir
from concourse.bass_utils import run_bass_kernel_spmd

F32 = mybir.dt.float32
BF16 = mybir.dt.bfloat16
U8 = mybir.dt.uint8
I32 = mybir.dt.int32
AF = mybir.ActivationFunctionType
ALU = mybir.AluOpType
AX = mybir.AxisListType

ENGINES = ("tensor", "vector", "scalar", "gpsimd", "sync")


class Prog:

    def __init__(self, nc: bass.Bass, n_dma_sems=8):
        self.nc = nc
        self.ops = {e: [] for e in ENGINES}
        self.cnt = {e: 0 for e in ENGINES}
        self.mem = {}
        self.water = {e: {} for e in ENGINES}
        self.n_dma_sems = n_dma_sems
        self.dma_use = [0] * n_dma_sems
        self.dma_next = 0
        self._sem_names = [f"S_{e}" for e in ("tensor", "vector", "scalar", "gpsimd")] + [
            f"S_dma{i}" for i in range(n_dma_sems)
        ]

    def _collect(self, eng, reads, writes, extra=()):
        deps = []
        for k in reads:
            m = self.mem.get(k)
            if m and m["w"]:
                deps.append(m["w"])
        for k in writes:
            m = self.mem.get(k)
            if m:
                if m["w"]:
                    deps.append(m["w"])
                deps.extend(m["r"])
        best = {}
        for sem, val in extra:
            if val > self.water[eng].get(sem, 0):
                best[sem] = max(best.get(sem, 0), val)
        for sem, val, src in deps:
            if src == eng == "tensor":
                continue
            if val > self.water[eng].get(sem, 0):
                best[sem] = max(best.get(sem, 0), val)
        waits = sorted(best.items())
        for sem, val in waits:
            self.water[eng][sem] = max(self.water[eng].get(sem, 0), val)
        return waits

    def _update_mem(self, token, reads, writes):
        for k in reads:
            self.mem.setdefault(k, {"w": None, "r": []})["r"].append(token)
        for k in writes:
            self.mem[k] = {"w": token, "r": []}

    def op(self, eng, fn, reads=(), writes=(), tag=""):
        waits = self._collect(eng, reads, writes)
        self.cnt[eng] += 1
        token = (f"S_{eng}", self.cnt[eng], eng)
        self.ops[eng].append((waits, fn, [(f"S_{eng}", 1)], tag))
        self._update_mem(token, reads, writes)
        return token

    def dma(self, fn, reads=(), writes=(), queue="sync", tag=""):
        i = self.dma_next
        self.dma_next = (self.dma_next + 1) % self.n_dma_sems
        sem = f"S_dma{i}"
        extra = [(sem, 16 * self.dma_use[i])] if self.dma_use[i] else []
        waits = self._collect(queue, reads, writes, extra=extra)
        self.dma_use[i] += 1
        token = (sem, 16 * self.dma_use[i], queue)
        self.ops[queue].append((waits, fn, [(sem, 16)], tag))
        self._update_mem(token, reads, writes)
        self.water[queue][sem] = max(self.water[queue].get(sem, 0), 16 * self.dma_use[i] - 16)
        return token

    def matmul_group(self, fns, reads=(), writes=(), tag=""):
        waits = self._collect("tensor", reads, writes)
        self.cnt["tensor"] += 1
        token = ("S_tensor", self.cnt["tensor"], "tensor")
        for j, fn in enumerate(fns):
            w = waits if j == 0 else []
            incs = [("S_tensor", 1)] if j == len(fns) - 1 else []
            self.ops["tensor"].append((w, fn, incs, tag))
        self._update_mem(token, reads, writes)
        return token

    def fence(self, tag="fence"):
        vals = {}
        for e in ("tensor", "vector", "scalar", "gpsimd"):
            if self.cnt[e]:
                vals[f"S_{e}"] = self.cnt[e]
        for i in range(self.n_dma_sems):
            if self.dma_use[i]:
                vals[f"S_dma{i}"] = 16 * self.dma_use[i]
        for e in ENGINES:
            waits = [(s_, v) for s_, v in sorted(vals.items())
                     if self.water[e].get(s_, 0) < v]
            if waits:
                self.ops[e].append((waits, None, [], tag))
                for s_, v in waits:
                    self.water[e][s_] = v

    def build(self, stack: ExitStack):
        nc = self.nc
        sems = {name: stack.enter_context(nc.semaphore(name)) for name in self._sem_names}
        self.sems = sems
        block = stack.enter_context(nc.Block())

        finals = []
        for i in range(self.n_dma_sems):
            if self.dma_use[i]:
                finals.append((f"S_dma{i}", 16 * self.dma_use[i]))
        for e in ("tensor", "vector", "scalar", "gpsimd"):
            if self.cnt[e]:
                finals.append((f"S_{e}", self.cnt[e]))

        def runner(ops_list, final_waits=()):
            def go(eng_obj):
                for waits, fn, incs, tag in ops_list:
                    for sem, val in waits:
                        eng_obj.wait_ge(sems[sem], val)
                    if fn is None:
                        continue
                    inst = fn(nc)
                    for sem, v in incs:
                        inst.then_inc(sems[sem], v)
                for sem, val in final_waits:
                    eng_obj.wait_ge(sems[sem], val)

            return go

        block.tensor(runner(self.ops["tensor"]))
        block.vector(runner(self.ops["vector"]))
        block.scalar(runner(self.ops["scalar"]))
        block.gpsimd(runner(self.ops["gpsimd"]))
        block.sync(runner(self.ops["sync"], final_waits=finals))
        return sems


def make_nc():
    return bass.Bass("TRN2", target_bir_lowering=False, debug=False)


T = 2048
D = 1024
NT = T // 128
ND = D // 128
LN_EPS = 1e-5


class Ctx:

    def __init__(self, nc, P, st):
        self.nc = nc
        self.P = P
        self.st = st
        self.t = {}

    def sb(self, name, shape, dt=F32, scratch=False):
        scr = getattr(self, "_scr", None)
        st = scr if scr is not None else self.st
        used = getattr(self, "_sb_names", None)
        if used is None:
            used = self._sb_names = {}
        n2 = used.get(name, 0)
        used[name] = n2 + 1
        alloc_name = name if n2 == 0 else f"{name}__{n2}"
        ap = st.enter_context(self.nc.sbuf_tensor(alloc_name, shape, dt)).ap()
        self.t[name] = ap
        return ap

    def open_scratch(self):
        if not hasattr(self, "_scrs"):
            self._scrs = []
        self._scrs.append(ExitStack())
        self._scr = self._scrs[-1]

    def close_scratch(self):
        self.P.fence()
        st = self._scrs.pop()
        st.close()
        self._scr = self._scrs[-1] if self._scrs else None

    def ps(self, name, shape, dt=F32):
        ap = self.st.enter_context(self.nc.psum_tensor(name, shape, dt)).ap()
        self.t[name] = ap
        return ap

    def dram(self, name, shape, dt=F32, kind="Internal"):
        cache = getattr(self, "_dram_cache", None)
        if cache is None:
            cache = self._dram_cache = {}
        if name in cache:
            self.t[name] = cache[name]
            return cache[name]
        ap = self.nc.dram_tensor(name, shape, dt, kind=kind).ap()
        cache[name] = ap
        self.t[name] = ap
        return ap


def setup_common(c: Ctx, io_dram=True):
    nc = c.nc
    if io_dram:
        c.dram("xin", [T, D], F32, kind="ExternalInput")
        c.dram("xout", [T, D], F32, kind="ExternalOutput")
    c.xin_name = "xin"
    c.xout_name = "xout"
    c.dram("c_ident", [128, 128], F32, kind="ExternalInput")
    c.sb("IDENT", [128, 128], BF16)
    c.sb("IDENTF", [128, 128], F32)
    c.sb("XT0", [128, D], F32)
    c.sb("XT1", [128, D], F32)
    c.sb("XH0", [128, D], BF16)
    c.sb("XH1", [128, D], BF16)
    c.sb("XHT", [128, ND * T], BF16)
    c.sb("STATS", [128, 4 * NT], F32)
    c.sb("EPSB", [128, 1], F32)

    c.ps("PSA", [128, 512], F32)
    c.ps("PSB", [128, 512], F32)
    c.ps("PSC", [128, 512], F32)
    c.ps("PSD", [128, 512], F32)
    c.ps("PST", [128, 512], BF16)
    c.ps("PST2", [128, 512], BF16)
    c.ps("SQ", [128, D], F32)
    P = c.P
    P.dma(lambda nc: nc.gpsimd.dma_start(out=c.t["IDENT"], in_=c.t["c_ident"]),
          reads=[], writes=["IDENT"], queue="gpsimd")
    P.dma(lambda nc: nc.sync.dma_start(out=c.t["IDENTF"], in_=c.t["c_ident"]),
          reads=[], writes=["IDENTF"], queue="sync")
    P.op("vector", lambda nc: nc.vector.memset(c.t["EPSB"], LN_EPS), writes=["EPSB"])
    c.sb("ONE1", [1, 128], BF16)
    c.sb("ONESCOL", [128, 1], BF16)
    c.sb("ONE1F", [1, 128], F32)
    c.sb("ONESCOLF", [128, 1], F32)
    P.op("vector", lambda nc: nc.vector.memset(c.t["ONE1"], 1.0), writes=["ONE1"])
    P.op("vector", lambda nc: nc.vector.memset(c.t["ONESCOL"], 1.0), writes=["ONESCOL"])
    P.op("vector", lambda nc: nc.vector.memset(c.t["ONE1F"], 1.0), writes=["ONE1F"])
    P.op("vector", lambda nc: nc.vector.memset(c.t["ONESCOLF"], 1.0), writes=["ONESCOLF"])


def emit_ln_fmajor(c: Ctx, out_f32=None, ln_id=0, out_bf16=None):
    P = c.P
    xv = c.t[c.xin_name].rearrange("(n p) d -> p n d", p=128)
    XHT = c.t["XHT"]
    ST = c.t["STATS"]
    SQ = c.t["SQ"]
    rD = 1.0 / D
    for i in range(NT):
        xt = c.t[f"XT{i % 2}"]
        xtk = f"XT{i % 2}"
        P.dma(lambda nc, i=i, xt=xt: nc.sync.dma_start(out=xt, in_=xv[:, i, :]),
              reads=[f"D{c.xin_name}_{i}"], writes=[xtk], queue="sync", tag=f"ln{ln_id}ld")
        o = 4 * i
        s_sum = ST[:, o:o + 1]
        s_ssq = ST[:, o + 1:o + 2]
        s_mean = ST[:, o + 2:o + 3]
        s_rstd = ST[:, o + 3:o + 4]
        k = f"ln{ln_id}_{i}_"
        xh = c.t[f"XH{i % 2}"]
        xhk = f"XH{i % 2}"
        P.op("vector", lambda nc, xt=xt, s=s_sum: nc.vector.reduce_sum(s, xt, axis=AX.X),
             reads=[xtk], writes=[k + "s"])
        P.op("scalar", lambda nc, xt=xt, s=s_ssq: nc.scalar.activation(SQ, xt, AF.Square, accum_out=s),
             reads=[xtk], writes=["SQ", k + "q"])
        P.op("vector", lambda nc, s=s_sum, m=s_mean: nc.vector.tensor_scalar_mul(m, s, rD),
             reads=[k + "s"], writes=[k + "m"])
        P.op("vector", lambda nc, m=s_mean, s=s_sum: nc.vector.tensor_mul(s, m, m),
             reads=[k + "m"], writes=[k + "s"])
        P.op("vector", lambda nc, q=s_ssq, s=s_sum: nc.vector.scalar_tensor_tensor(
            q, q, rD, s, ALU.mult, ALU.subtract),
             reads=[k + "q", k + "s"], writes=[k + "v"])
        P.op("scalar", lambda nc, q=s_ssq, r=s_rstd: nc.scalar.activation(
            r, q, AF.Sqrt, bias=c.t["EPSB"]),
             reads=[k + "v", "EPSB"], writes=[k + "sq"])
        P.op("vector", lambda nc, r=s_rstd: nc.vector.reciprocal(r, r),
             reads=[k + "sq"], writes=[k + "r"])
        P.op("vector", lambda nc, xt=xt, m=s_mean, r=s_rstd, o_=xh:
             nc.vector.tensor_scalar(o_, xt, m, r, ALU.subtract, ALU.mult),
             reads=[xtk, k + "m", k + "r"], writes=[xhk])
        if out_f32 is not None:
            XF = c.t[out_f32]
            P.op("gpsimd", lambda nc, xt=xt, m=s_mean, r=s_rstd, o_=XF[:, i * D:(i + 1) * D]:
                 nc.gpsimd.tensor_scalar(o_, xt, m, r, ALU.subtract, ALU.mult),
                 reads=[xtk, k + "m", k + "r"], writes=[out_f32])
        if out_bf16 is not None:
            XB = c.t[out_bf16]
            P.op("gpsimd", lambda nc, xh=xh, o_=XB[:, i * D:(i + 1) * D]:
                 nc.gpsimd.tensor_copy(o_, xh),
                 reads=[xhk], writes=[out_bf16])
        pairs = []
        for cc in range(ND):
            sap = xh[:, cc * 128:(cc + 1) * 128]
            dap = XHT[:, cc * T + i * 128: cc * T + (i + 1) * 128]
            pairs.append((sap, dap))
        emit_transpose_tiles(c, xhk, "XHT", pairs)


def emit_transpose_tiles(c: Ctx, src_key, dst_key, pairs, psk=("PST", "PST2"),
                         copy_eng="vector"):
    P = c.P
    ID = c.t["IDENT"]
    nb = 0
    for g in range(0, len(pairs), 4):
        batch = pairs[g:g + 4]
        which = (g // 4) % 2
        pst = c.t[psk[which]]
        pk = psk[0] + f"_{which}"
        for j, (sap, _dap) in enumerate(batch):
            P.op("tensor", lambda nc, sap=sap, j=j, pst=pst: nc.tensor.transpose(
                pst[:, j * 128:(j + 1) * 128], sap, ID),
                 reads=[src_key, "IDENT"], writes=[pk], tag="tr")
        for j, (_sap, dap) in enumerate(batch):
            P.op(copy_eng, lambda nc, dap=dap, j=j, pst=pst, eng=copy_eng: getattr(nc, eng).tensor_copy(
                dap, pst[:, j * 128:(j + 1) * 128]),
                 reads=[pk], writes=[dst_key], tag="trcp")
        nb += 1


def bcast_scalar_load(c: Ctx, dram_name, sb_name):
    ap = c.sb(sb_name, [128, 1], F32)
    src = c.t[dram_name][None, :].broadcast_to([128, 1])
    c.P.dma(lambda nc, ap=ap, src=src: nc.sync.dma_start(out=ap, in_=src),
            writes=[sb_name], queue="sync")
    return ap



def stage_L0_ffn(inputs):
    return {
        "L0_ffn_k0": np.ascontiguousarray(inputs["L0_ffn_k0"].view(np.uint8)),
        "L0_ffn_k1": np.ascontiguousarray(inputs["L0_ffn_k1"].view(np.uint8)),
        "L0_ffn_k2": np.ascontiguousarray(inputs["L0_ffn_k2"].view(np.uint8)),
        "L0_ffn_s0": np.asarray(inputs["L0_ffn_s0"], np.float32).reshape(1),
        "L0_ffn_s1": np.asarray(inputs["L0_ffn_s1"], np.float32).reshape(1),
        "L0_ffn_s2": np.asarray(inputs["L0_ffn_s2"], np.float32).reshape(1),
    }


def declare_L0_ffn(c: Ctx):
    c.dram("L0_ffn_k0", [2048, 4096], U8, kind="ExternalInput")
    c.dram("L0_ffn_k1", [4096, 4096], U8, kind="ExternalInput")
    c.dram("L0_ffn_k2", [4096, 1024], U8, kind="ExternalInput")
    c.dram("L0_ffn_s0", [1], F32, kind="ExternalInput")
    c.dram("L0_ffn_s1", [1], F32, kind="ExternalInput")
    c.dram("L0_ffn_s2", [1], F32, kind="ExternalInput")


def emit_L0_ffn(c: Ctx, uid="f0"):
    P = c.P
    u = uid
    first_use = (u + "WSA") not in c.t
    if first_use:
        WSA = c.sb(u + "WSA", [128, 32 * 1024], BF16)
        BA = [c.sb(u + f"BA{j}", [128, 4096], BF16) for j in range(2)]
        BB = [c.sb(u + f"BB{j}", [128, 4096], BF16) for j in range(2)]
        BC = [c.sb(u + f"BC{j}", [128, 1024], BF16) for j in range(2)]
        BD = [c.sb(u + f"BD{j}", [128, 1024], BF16) for j in range(2)]
        R1 = c.sb(u + "R1", [128, NT], F32)
        R2 = c.sb(u + "R2", [128, NT], F32)
        B2 = c.sb(u + "B2", [128, NT], F32)
        B3 = c.sb(u + "B3", [128, NT], F32)
        OTMP = c.sb(u + "OTMP", [128, 512], F32)
        RS = c.sb(u + "RS", [128, 1], F32)
        scr1 = c.dram(u + "scr1", [NT, 128, 4096], BF16)
        scr2 = c.dram(u + "scr2", [NT, 128, 4096], BF16)
        s0 = bcast_scalar_load(c, "L0_ffn_s0", u + "S0")
        s1 = bcast_scalar_load(c, "L0_ffn_s1", u + "S1")
        s2 = bcast_scalar_load(c, "L0_ffn_s2", u + "S2")
        SC = c.sb(u + "SC", [128, 8], F32)
        for j, (src, mul) in enumerate([(s0, 4.0), (s1, 4.0), (s2, 2.0), (s1, -2.0), (s2, -1.0)]):
            P.op("vector", lambda nc, j=j, src=src, mul=mul: nc.vector.tensor_scalar_mul(
                SC[:, j:j + 1], src, mul),
                reads=[u + "S0", u + "S1", u + "S2"], writes=[u + f"sc{j}"])
    else:
        WSA = c.t[u + "WSA"]
        BA = [c.t[u + f"BA{j}"] for j in range(2)]
        BB = [c.t[u + f"BB{j}"] for j in range(2)]
        BC = [c.t[u + f"BC{j}"] for j in range(2)]
        BD = [c.t[u + f"BD{j}"] for j in range(2)]
        R1 = c.t[u + "R1"]; R2 = c.t[u + "R2"]; B2 = c.t[u + "B2"]; B3 = c.t[u + "B3"]
        OTMP = c.t[u + "OTMP"]; RS = c.t[u + "RS"]
        scr1 = c.t[u + "scr1"]; scr2 = c.t[u + "scr2"]
        SC = c.t[u + "SC"]
    sc1 = SC[:, 0:1]; sc2 = SC[:, 1:2]; sc3 = SC[:, 2:3]
    m2s1 = SC[:, 3:4]; m1s2 = SC[:, 4:5]
    call = sum(1 for k in c.t if k.startswith(u + "call"))
    c.t[u + f"call{call}"] = None
    q = f"{u}_{call}_"

    XHT = c.t["XHT"]
    xv = c.t[c.xin_name].rearrange("(n p) d -> p n d", p=128)
    ov = c.t[c.xout_name].rearrange("(n p) d -> p n d", p=128)
    psf = [c.t["PSA"], c.t["PSB"], c.t["PSC"], c.t["PSD"]]
    psfk = ["PSA", "PSB", "PSC", "PSD"]
    wkeys = [u + f"W{j}" for j in range(4)]
    W8 = WSA.rearrange("p (c f) -> p c f", f=4096)
    W32 = WSA.rearrange("p (c f) -> p c f", f=1024)
    k0v = c.t["L0_ffn_k0"].rearrange("(c p) f -> p c f", p=128)
    k1v = c.t["L0_ffn_k1"].rearrange("(c p) f -> p c f", p=128)
    k2v = c.t["L0_ffn_k2"].rearrange("(c p) f -> p c f", p=128)

    for cd in range(ND):
        ba = BA[cd % 2]; bak = u + f"BA{cd % 2}"
        bb = BB[cd % 2]; bbk = u + f"BB{cd % 2}"
        P.dma(lambda nc, cd=cd, ba=ba: nc.gpsimd.dma_start(out=ba, in_=k0v[:, cd, :]),
              writes=[bak], queue="gpsimd", tag=q + "k0t")
        P.dma(lambda nc, cd=cd, bb=bb: nc.gpsimd.dma_start(out=bb, in_=k0v[:, 8 + cd, :]),
              writes=[bbk], queue="gpsimd", tag=q + "k0b")
        P.op("vector", lambda nc, cd=cd, ba=ba, bb=bb: nc.vector.tensor_tensor(
            W8[:, cd, :], ba, bb, ALU.subtract),
            reads=[bak, bbk], writes=[wkeys[cd // 2]], tag=q + "kd")

    for i in range(NT):
        ib = i % 2
        a1 = BA[ib]; a1k = u + f"BA{ib}"
        a1t = BB[ib]; a1tk = u + f"BB{ib}"
        for g in range(8):
            ps = psf[g % 4]; bank = psfk[g % 4]
            fns = []
            for cd in range(ND):
                fns.append(lambda nc, cd=cd, g=g, i=i, ps=ps: nc.tensor.matmul(
                    ps, XHT[:, cd * T + i * 128: cd * T + (i + 1) * 128],
                    W8[:, cd, g * 512:(g + 1) * 512],
                    start=(cd == 0), stop=(cd == ND - 1)))
            P.matmul_group(fns, reads=["XHT"] + wkeys, writes=[bank], tag=q + "p1")
            P.op("scalar", lambda nc, g=g, ps=ps, a1=a1: nc.scalar.activation(
                a1[:, g * 512:(g + 1) * 512], ps, AF.Relu, scale=sc1),
                reads=[bank, u + "sc0"], writes=[a1k], tag=q + "e1")
        P.op("vector", lambda nc, a1=a1, i=i: nc.vector.reduce_sum(
            R1[:, i:i + 1], a1, axis=AX.X), reads=[a1k], writes=[q + f"r1_{i}"], tag=q + "r1")
        pairs = [(a1[:, cc * 128:(cc + 1) * 128], a1t[:, cc * 128:(cc + 1) * 128])
                 for cc in range(32)]
        emit_transpose_tiles(c, a1k, a1tk, pairs)
        P.dma(lambda nc, a1t=a1t, i=i: nc.sync.dma_start(out=scr1[i], in_=a1t),
              reads=[a1tk], writes=[q + f"Dscr1_{i}"], queue="sync", tag=q + "sp1")

    P.op("vector", lambda nc: nc.vector.tensor_scalar(B2, R1, m2s1, None, ALU.mult),
         reads=[u + "sc3"] + [q + f"r1_{i}" for i in range(NT)], writes=[q + "B2"])

    for islab in range(4):
        f0 = islab * 1024
        for cg in range(4):
            P.dma(lambda nc, cg=cg, f0=f0: nc.gpsimd.dma_start(
                out=W32[:, cg * 8:(cg + 1) * 8, :], in_=k1v[:, cg * 8:(cg + 1) * 8, f0:f0 + 1024]),
                writes=[wkeys[cg]], queue="gpsimd", tag=q + "k1ld")
        for i in range(NT):
            ib = i % 2
            a1l = BA[ib]; a1lk = u + f"BA{ib}"
            a2 = BC[ib]; a2k = u + f"BC{ib}"
            a2t = BD[ib]; a2tk = u + f"BD{ib}"
            P.dma(lambda nc, a1l=a1l, i=i: nc.sync.dma_start(out=a1l, in_=scr1[i]),
                  reads=[q + f"Dscr1_{i}"], writes=[a1lk], queue="sync", tag=q + "a1ld")
            for half in range(2):
                ps = psf[(i * 2 + half) % 4]; bank = psfk[(i * 2 + half) % 4]
                fns = []
                for cd in range(32):
                    fns.append(lambda nc, cd=cd, half=half, ps=ps, a1l=a1l: nc.tensor.matmul(
                        ps, a1l[:, cd * 128:(cd + 1) * 128],
                        W32[:, cd, half * 512:(half + 1) * 512],
                        start=(cd == 0), stop=(cd == 31)))
                P.matmul_group(fns, reads=[a1lk] + wkeys, writes=[bank], tag=q + "p2")
                P.op("scalar", lambda nc, half=half, ps=ps, a2=a2, i=i: nc.scalar.activation(
                    a2[:, half * 512:(half + 1) * 512], ps, AF.Relu, scale=sc2,
                    bias=B2[:, i:i + 1]),
                    reads=[bank, u + "sc1", q + "B2"], writes=[a2k], tag=q + "e2")
            if islab == 0:
                P.op("vector", lambda nc, a2=a2, i=i: nc.vector.reduce_sum(
                    R2[:, i:i + 1], a2, axis=AX.X), reads=[a2k], writes=[q + f"r2_{i}"], tag=q + "r2")
            else:
                P.op("vector", lambda nc, a2=a2: nc.vector.reduce_sum(
                    RS, a2, axis=AX.X), reads=[a2k], writes=[q + "rs"], tag=q + "r2")
                P.op("vector", lambda nc, i=i: nc.vector.tensor_add(
                    R2[:, i:i + 1], R2[:, i:i + 1], RS),
                    reads=[q + "rs", q + f"r2_{i}"], writes=[q + f"r2_{i}"], tag=q + "r2b")
            pairs = [(a2[:, cc * 128:(cc + 1) * 128], a2t[:, cc * 128:(cc + 1) * 128])
                     for cc in range(8)]
            emit_transpose_tiles(c, a2k, a2tk, pairs)
            P.dma(lambda nc, a2t=a2t, i=i, islab=islab: nc.sync.dma_start(
                out=scr2.rearrange("i p (c t) -> i p c t", t=128)[i, :, islab * 8:(islab + 1) * 8, :],
                in_=a2t.rearrange("p (c t) -> p c t", t=128)),
                reads=[a2tk], writes=[q + f"Dscr2_{i}_{islab}"], queue="sync", tag=q + "sp2")

    P.op("vector", lambda nc: nc.vector.tensor_scalar(B3, R2, m1s2, None, ALU.mult),
         reads=[u + "sc4"] + [q + f"r2_{i}" for i in range(NT)], writes=[q + "B3"])

    for cg in range(4):
        P.dma(lambda nc, cg=cg: nc.gpsimd.dma_start(
            out=W32[:, cg * 8:(cg + 1) * 8, :], in_=k2v[:, cg * 8:(cg + 1) * 8, :]),
            writes=[wkeys[cg]], queue="gpsimd", tag=q + "k2ld")
    for i in range(NT):
        ib = i % 2
        a2l = BB[ib]; a2lk = u + f"BB{ib}"
        xt = c.t[f"XT{ib}"]; xtk = f"XT{ib}"
        P.dma(lambda nc, a2l=a2l, i=i: nc.sync.dma_start(out=a2l, in_=scr2[i]),
              reads=[q + f"Dscr2_{i}_{j}" for j in range(4)], writes=[a2lk],
              queue="sync", tag=q + "a2ld")
        P.dma(lambda nc, i=i, xt=xt: nc.sync.dma_start(out=xt, in_=xv[:, i, :]),
              reads=[f"D{c.xin_name}_{i}"], writes=[xtk], queue="sync", tag=q + "xld")
        for half in range(2):
            ps = psf[(i * 2 + half) % 4]; bank = psfk[(i * 2 + half) % 4]
            fns = []
            for cd in range(32):
                fns.append(lambda nc, cd=cd, half=half, ps=ps, a2l=a2l: nc.tensor.matmul(
                    ps, a2l[:, cd * 128:(cd + 1) * 128],
                    W32[:, cd, half * 512:(half + 1) * 512],
                    start=(cd == 0), stop=(cd == 31)))
            P.matmul_group(fns, reads=[a2lk] + wkeys, writes=[bank], tag=q + "p3")
            P.op("vector", lambda nc, half=half, ps=ps, i=i: nc.vector.tensor_scalar(
                OTMP, ps, sc3, B3[:, i:i + 1], ALU.mult, ALU.add),
                reads=[bank, u + "sc2", q + "B3"], writes=[q + "OTMP"], tag=q + "e3")
            P.op("vector", lambda nc, xt=xt, half=half: nc.vector.tensor_add(
                xt[:, half * 512:(half + 1) * 512],
                xt[:, half * 512:(half + 1) * 512], OTMP),
                reads=[q + "OTMP", xtk], writes=[xtk], tag=q + "res")
        P.dma(lambda nc, i=i, xt=xt: nc.sync.dma_start(out=ov[:, i, :], in_=xt),
              reads=[xtk], writes=[f"D{c.xout_name}_{i}"], queue="sync", tag=q + "xst")



def load_weight_bf16(c, dram_name, sb_name, d_in, n_out, queue="gpsimd", scratch=False):
    nch = d_in // 128
    Wsb = c.sb(sb_name, [128, nch * n_out], BF16, scratch=scratch)
    wv = c.t[dram_name].rearrange("(cd p) f -> p cd f", p=128)
    dst = Wsb.rearrange("p (cd f) -> p cd f", f=n_out)
    keys = []
    step = max(1, min(8, nch))
    for g0 in range(0, nch, step):
        g1 = min(nch, g0 + step)
        kk = sb_name + f"_{g0}"
        keys.append(kk)
        c.P.dma(lambda nc, g0=g0, g1=g1: nc.gpsimd.dma_start(
            out=dst[:, g0:g1, :], in_=wv[:, g0:g1, :]),
            writes=[kk], queue=queue, tag=sb_name)
    c.t[sb_name + "_keys"] = keys
    return Wsb, keys


def emit_proj(c, src_key, w_name, dst_key, n_out, n_in=D, out_major="f",
              scale=None, nt=NT, accum_tiles=None, evac_dtype=BF16,
              src_nd=None, psum_banks=("PSA", "PSB"), evac_eng="scalar",
              bias_ap=None, bias_key=None, bias_tile=None, addrow_tile=None,
              act=None):
    P = c.P
    SRC = c.t[src_key]
    W = c.t[w_name]
    wkeys = c.t[w_name + "_keys"]
    DST = c.t[dst_key]
    nch_in = n_in // 128
    if src_nd is None:
        src_nd = nch_in
    Wv = W.rearrange("p (cd f) -> p cd f", f=n_out)
    psf = [c.t[b] for b in psum_banks]
    nb = len(psum_banks)
    bi = 0
    if out_major == "f":
        nfc = n_out // 128
        for fc in range(nfc):
            for tg in range(0, T, 512):
                ps = psf[bi % nb]; bank = psum_banks[bi % nb]; bi += 1
                fns = []
                for cd in range(nch_in):
                    fns.append(lambda nc, cd=cd, fc=fc, tg=tg, ps=ps: nc.tensor.matmul(
                        ps, Wv[:, cd, fc * 128:(fc + 1) * 128],
                        SRC[:, cd * T + tg: cd * T + tg + 512],
                        start=(cd == 0), stop=(cd == nch_in - 1)))
                P.matmul_group(fns, reads=[src_key] + wkeys, writes=[bank], tag=dst_key)
                dap = DST[:, fc * T + tg: fc * T + tg + 512]
                bap = bias_tile[0][:, fc:fc + 1] if bias_tile else bias_ap
                bk = bias_tile[1] if bias_tile else bias_key
                if act is not None:
                    P.op("scalar", lambda nc, dap=dap, ps=ps, act=act: nc.scalar.activation(
                        dap, ps, act), reads=[bank], writes=[dst_key], tag=dst_key + "ev")
                else:
                    _evac(c, ps, bank, dap, dst_key, scale, evac_eng, bap, bk)
    else:
        ngr = (n_out + 511) // 512
        for i in range(nt):
            for gr in range(ngr):
                f0 = gr * 512
                f1 = min(n_out, f0 + 512)
                ps = psf[bi % nb]; bank = psum_banks[bi % nb]; bi += 1
                fns = []
                for cd in range(nch_in):
                    fns.append(lambda nc, cd=cd, i=i, f0=f0, f1=f1, ps=ps: nc.tensor.matmul(
                        ps[:, 0:f1 - f0], SRC[:, cd * T + i * 128: cd * T + (i + 1) * 128],
                        Wv[:, cd, f0:f1],
                        start=(cd == 0), stop=(cd == nch_in - 1)))
                P.matmul_group(fns, reads=[src_key] + wkeys, writes=[bank], tag=dst_key)
                dap = DST[:, i * n_out + f0: i * n_out + f1]
                arow = addrow_tile[0][:, f0:f1] if addrow_tile else None
                bk = addrow_tile[1] if addrow_tile else bias_key
                if act is not None:
                    P.op("scalar", lambda nc, dap=dap, ps=ps, f0=f0, f1=f1, act=act:
                         nc.scalar.activation(dap, ps[:, 0:f1 - f0], act),
                         reads=[bank], writes=[dst_key], tag=dst_key + "ev")
                else:
                    _evac(c, ps[:, 0:f1 - f0], bank, dap, dst_key, scale, evac_eng,
                          bias_ap, bk, addrow_ap=arow)


def _evac(c, ps_ap, bank, dap, dst_key, scale, eng, bias_ap=None, bias_key=None,
          addrow_ap=None):
    P = c.P
    rds = [bank]
    if bias_key:
        rds.append(bias_key)
    if addrow_ap is not None:
        sc = 1.0 if scale is None else scale
        P.op("vector", lambda nc: nc.vector.scalar_tensor_tensor(
            dap, ps_ap, sc, addrow_ap, ALU.mult, ALU.add),
             reads=rds, writes=[dst_key], tag=dst_key + "ev")
        return
    if bias_ap is not None:
        sc = 1.0 if scale is None else scale
        P.op("vector", lambda nc: nc.vector.tensor_scalar(dap, ps_ap, sc, bias_ap,
                                                          ALU.mult, ALU.add),
             reads=rds, writes=[dst_key], tag=dst_key + "ev")
    elif scale is None and eng == "scalar":
        P.op("scalar", lambda nc: nc.scalar.copy(dap, ps_ap),
             reads=rds, writes=[dst_key], tag=dst_key + "ev")
    elif scale is None:
        P.op(eng, lambda nc, eng=eng: getattr(nc, eng).tensor_copy(dap, ps_ap),
             reads=rds, writes=[dst_key], tag=dst_key + "ev")
    elif isinstance(scale, float):
        P.op("scalar", lambda nc: nc.scalar.mul(dap, ps_ap, scale),
             reads=rds, writes=[dst_key], tag=dst_key + "ev")
    else:
        P.op("vector", lambda nc: nc.vector.tensor_scalar_mul(dap, ps_ap, scale),
             reads=rds, writes=[dst_key], tag=dst_key + "ev")


def emit_residual_from_tmajor(c, src_key, nt=NT, tag="res"):
    P = c.P
    xv = c.t[c.xin_name].rearrange("(n p) d -> p n d", p=128)
    ov = c.t[c.xout_name].rearrange("(n p) d -> p n d", p=128)
    SRC = c.t[src_key]
    for i in range(nt):
        xt = c.t[f"XT{i % 2}"]
        xtk = f"XT{i % 2}"
        P.dma(lambda nc, i=i, xt=xt: nc.sync.dma_start(out=xt, in_=xv[:, i, :]),
              reads=[f"D{c.xin_name}_{i}"], writes=[xtk], queue="sync", tag=tag + "ld")
        P.op("vector", lambda nc, i=i, xt=xt: nc.vector.tensor_add(
            xt, xt, SRC[:, i * D:(i + 1) * D]),
            reads=[src_key, xtk], writes=[xtk], tag=tag)
        P.dma(lambda nc, i=i, xt=xt: nc.sync.dma_start(out=ov[:, i, :], in_=xt),
              reads=[xtk], writes=[f"D{c.xout_name}_{i}"], queue="sync", tag=tag + "st")


def emit_mha(c, uid, qt_key, kt_key, vtm_key, out_key, H=16, dk=64, dv=64,
             scale=1.0, causal=False, vh_cols=None):
    P = c.P
    u = uid
    QT = c.t[qt_key]; KT = c.t[kt_key]; VTM = c.t[vtm_key]; AO = c.t[out_key]
    ID = c.t["IDENT"]
    vh = H * dv if vh_cols is None else vh_cols
    if (u + "SQH") not in c.t:
        c.sb(u + "SQH", [128, T], BF16)
        c.sb(u + "MH", [128, 4], F32)
        c.sb(u + "VE", [128, NT * (dv + 1)], BF16)
        c.sb(u + "UT0", [128, 512], BF16)
        c.sb(u + "UT1", [128, 512], BF16)
        c.sb(u + "OH", [128, 512], BF16)
        c.sb(u + "OTB", [128, 128], F32)
    SQH = c.t[u + "SQH"]; MH = c.t[u + "MH"]; VE = c.t[u + "VE"]
    UT = [c.t[u + "UT0"], c.t[u + "UT1"]]
    OH = c.t[u + "OH"]
    PSA, PSB, PSC = c.t["PSA"], c.t["PSB"], c.t["PSC"]
    PSD = c.t["PSD"]
    PST = c.t["PST"]
    nkb = T // 128
    nqg = T // 512
    assert dk <= 128 and dv + 1 <= 128
    for h in range(H):
        hq = f"{u}h{h}_"
        base = (h * dk) % 128
        chq = (h * dk) // 128
        qslice = QT[base:base + dk, chq * T:(chq + 1) * T]
        kslice = KT[base:base + dk, chq * T:(chq + 1) * T]
        P.op("vector", lambda nc, qs=qslice: nc.vector.tensor_mul(SQH[0:dk, :], qs, qs),
             reads=[qt_key], writes=[hq + "sq"], tag=u + "msq")
        P.op("vector", lambda nc: nc.vector.reduce_max(MH[0:dk, 0:1], SQH[0:dk, :], axis=AX.X),
             reads=[hq + "sq"], writes=[hq + "mq"], tag=u + "mh")
        P.op("vector", lambda nc, ks=kslice: nc.vector.tensor_mul(SQH[0:dk, :], ks, ks),
             reads=[kt_key, hq + "mq"], writes=[hq + "sq"], tag=u + "msq")
        P.op("vector", lambda nc: nc.vector.reduce_max(MH[0:dk, 1:2], SQH[0:dk, :], axis=AX.X),
             reads=[hq + "sq"], writes=[hq + "mk"], tag=u + "mh")
        P.matmul_group([lambda nc: nc.tensor.matmul(
            PSD[0:1, 0:2], c.t["ONESCOLF"][0:dk, 0:1], MH[0:dk, 0:2],
            start=True, stop=True)],
            reads=[hq + "mq", hq + "mk", "ONESCOLF"], writes=["PSD"], tag=u + "mm")
        P.op("vector", lambda nc: nc.vector.tensor_copy(MH[0:1, 2:4], PSD[0:1, 0:2]),
             reads=["PSD"], writes=[hq + "m2a"], tag=u + "mh")
        P.op("vector", lambda nc: nc.vector.tensor_mul(MH[0:1, 0:1], MH[0:1, 2:3], MH[0:1, 3:4]),
             reads=[hq + "m2a"], writes=[hq + "m2"], tag=u + "mh")
        P.op("scalar", lambda nc: nc.scalar.activation(MH[0:1, 1:2], MH[0:1, 0:1], AF.Sqrt),
             reads=[hq + "m2"], writes=[hq + "m3"], tag=u + "mh")
        P.matmul_group([lambda nc: nc.tensor.matmul(
            PSD[:, 2:3], c.t["ONE1F"][0:1, :], MH[0:1, 1:2], start=True, stop=True)],
            reads=[hq + "m3", "ONE1F"], writes=["PSDb"], tag=u + "mb")
        P.op("vector", lambda nc: nc.vector.tensor_scalar_mul(MH[:, 0:1], PSD[:, 2:3], -scale),
             reads=["PSDb"], writes=[hq + "bias"], tag=u + "mh")
        P.op("vector", lambda nc: nc.vector.memset(VE, 1.0), reads=[], writes=[u + "VE"],
             tag=u + "ve")
        for kb in range(nkb):
            P.op("vector", lambda nc, kb=kb, h=h: nc.vector.tensor_copy(
                VE[:, kb * (dv + 1): kb * (dv + 1) + dv],
                VTM[:, kb * vh + h * dv: kb * vh + h * dv + dv]),
                reads=[vtm_key], writes=[u + "VE"], tag=u + "ve")
        for qg in range(nqg):
            q0 = qg * 512
            kbs = [kb for kb in range(nkb)
                   if (not causal) or kb * 128 <= q0 + 511]
            first = True
            for kb in kbs:
                utb = UT[kb % 2]
                utk = u + f"UT{kb % 2}"
                psk = "PSA" if kb % 2 == 0 else "PSB"
                ps = c.t[psk]
                P.matmul_group([lambda nc, kb=kb, q0=q0, ps=ps, qs=qslice, ks=kslice:
                                nc.tensor.matmul(ps, ks[:, kb * 128:(kb + 1) * 128],
                                                 qs[:, q0:q0 + 512], start=True, stop=True)],
                               reads=[qt_key, kt_key], writes=[psk], tag=u + "S")
                P.op("scalar", lambda nc, ps=ps, utb=utb: nc.scalar.activation(
                    utb, ps, AF.Exp, scale=scale, bias=MH[:, 0:1]),
                    reads=[psk, hq + "bias"], writes=[utk], tag=u + "exp")
                if causal and kb * 128 + 127 > q0:
                    P.op("gpsimd", lambda nc, kb=kb, q0=q0, utb=utb: nc.gpsimd.affine_select(
                        utb, utb, [[1, 512]], ALU.is_ge, 0.0,
                        base=q0 - kb * 128, channel_multiplier=-1),
                        reads=[utk], writes=[utk], tag=u + "mask")
                P.op("tensor", lambda nc, kb=kb, utb=utb, first=first, last=(kb == kbs[-1]):
                     nc.tensor.matmul(PSC[0:dv + 1, :], VE[:, kb * (dv + 1):(kb + 1) * (dv + 1)],
                                      utb, start=first, stop=last),
                     reads=[utk, u + "VE"], writes=["PSC"] if kb == kbs[-1] else [f"PSC_t"],
                     tag=u + "PV")
                first = False
            P.op("vector", lambda nc: nc.vector.tensor_copy(OH[0:dv + 1, :], PSC[0:dv + 1, :]),
                 reads=["PSC", "PSC_t"], writes=[u + "OH"], tag=u + "pvev")
            OTB = c.t[u + "OTB"]
            for j in range(4):
                P.op("tensor", lambda nc, j=j: nc.tensor.transpose(
                    PST[:, 0:dv + 1], OH[0:dv + 1, j * 128:(j + 1) * 128], ID[0:dv + 1, 0:dv + 1]),
                    reads=[u + "OH", "IDENT"], writes=["PST_0"], tag=u + "tr")
                P.op("vector", lambda nc: nc.vector.tensor_copy(
                    OTB[:, 0:dv + 1], PST[:, 0:dv + 1]),
                     reads=["PST_0"], writes=[u + "OTB"], tag=u + "trc")
                i_tile = (q0 + j * 128) // 128
                P.op("vector", lambda nc: nc.vector.reciprocal(
                    OTB[:, dv:dv + 1], OTB[:, dv:dv + 1]),
                    reads=[u + "OTB"], writes=[u + "OTBr"], tag=u + "div")
                P.op("vector", lambda nc, i_tile=i_tile, h=h: nc.vector.tensor_scalar_mul(
                    AO[:, i_tile * vh + h * dv: i_tile * vh + h * dv + dv],
                    OTB[:, 0:dv], OTB[:, dv:dv + 1]),
                    reads=[u + "OTB", u + "OTBr"], writes=[out_key], tag=u + "div")




def emit_tmajor_to_fmajor(c, src_key, dst_key, nf=ND, nt=NT, width=D):
    SRC = c.t[src_key]
    DST = c.t[dst_key]
    pairs = []
    for i in range(nt):
        for cc in range(nf):
            pairs.append((SRC[:, i * width + cc * 128: i * width + (cc + 1) * 128],
                          DST[:, cc * T + i * 128: cc * T + (i + 1) * 128]))
    emit_transpose_tiles(c, src_key, dst_key, pairs)


def emit_fmajor_to_tmajor(c, src_key, dst_key, nf=ND, nt=NT, width=D):
    SRC = c.t[src_key]
    DST = c.t[dst_key]
    pairs = []
    for i in range(nt):
        for cc in range(nf):
            pairs.append((SRC[:, cc * T + i * 128: cc * T + (i + 1) * 128],
                          DST[:, i * width + cc * 128: i * width + (cc + 1) * 128]))
    emit_transpose_tiles(c, src_key, dst_key, pairs)


def load_row_f32(c, dram_name, sb_name, n, scratch=False):
    ap = c.sb(sb_name, [1, n], F32, scratch=scratch)
    c.P.dma(lambda nc: nc.sync.dma_start(out=ap, in_=c.t[dram_name][None, :]),
            writes=[sb_name], queue="sync")
    return ap


def load_row_bf16(c, dram_name, sb_name, n, scratch=False):
    ap = c.sb(sb_name, [1, n], BF16, scratch=scratch)
    c.P.dma(lambda nc: nc.gpsimd.dma_start(out=ap, in_=c.t[dram_name][None, :]),
            writes=[sb_name], queue="gpsimd")
    return ap


def load_chunkvec_f32(c, dram_name, sb_name, n, scratch=False):
    ap = c.sb(sb_name, [128, n // 128], F32, scratch=scratch)
    src = c.t[dram_name].rearrange("(cd p) -> p cd", p=128)
    c.P.dma(lambda nc: nc.sync.dma_start(out=ap, in_=src, allow_slow_non_contiguous=True),
            writes=[sb_name], queue="sync")
    return ap


def emit_bcast_row(c, row_ap, row_key, dst_name, n, dtype=F32, scratch=False):
    P = c.P
    dst = c.sb(dst_name, [128, n], dtype, scratch=scratch)
    PSD = c.t["PSD"]
    for g0 in range(0, n, 512):
        g1 = min(n, g0 + 512)
        ones_nm = "ONE1F" if row_ap.dtype == F32 else "ONE1"
        P.matmul_group([lambda nc, g0=g0, g1=g1, ones_nm=ones_nm: nc.tensor.matmul(
            PSD[:, 0:g1 - g0], c.t[ones_nm][0:1, :], row_ap[0:1, g0:g1],
            start=True, stop=True)],
            reads=[row_key, ones_nm], writes=["PSDr"], tag=dst_name)
        P.op("vector", lambda nc, g0=g0, g1=g1: nc.vector.tensor_copy(
            dst[:, g0:g1], PSD[:, 0:g1 - g0]),
            reads=["PSDr"], writes=[dst_name], tag=dst_name)
    return dst


def stage_L2_attn_a(inputs):
    d = {}
    for nm in ("Wq", "Wk", "Wv"):
        w = np.asarray(inputs[f"L2_attn_a_{nm}"], np.float32)
        d[f"L2_attn_a_{nm}"] = np.ascontiguousarray(w.transpose(1, 0, 2).reshape(1024, 1024))
    for nm in ("bq", "bk", "bv"):
        d[f"L2_attn_a_{nm}"] = np.ascontiguousarray(
            np.asarray(inputs[f"L2_attn_a_{nm}"], np.float32).reshape(1024))
    d["L2_attn_a_Wo"] = np.ascontiguousarray(
        np.asarray(inputs["L2_attn_a_Wo"], np.float32).T)
    d["L2_attn_a_bo"] = np.ascontiguousarray(np.asarray(inputs["L2_attn_a_bo"], np.float32))
    return d


def declare_L2_attn_a(c: Ctx):
    for nm in ("Wq", "Wk", "Wv", "Wo"):
        c.dram(f"L2_attn_a_{nm}", [D, D], F32, kind="ExternalInput")
    for nm in ("bq", "bk", "bv", "bo"):
        c.dram(f"L2_attn_a_{nm}", [D], F32, kind="ExternalInput")


def emit_L2_attn_a(c: Ctx, uid="a8"):
    u = uid
    c.sb(u + "QT", [128, ND * T], BF16)
    c.sb(u + "KT", [128, ND * T], BF16)
    c.sb(u + "VTM", [128, NT * D], BF16)
    for nm in ("bq", "bk", "bv"):
        load_chunkvec_f32(c, f"L2_attn_a_{nm}", u + nm, D)
    bo_row = load_row_f32(c, "L2_attn_a_bo", u + "bo_row", D)
    BOROW = emit_bcast_row(c, bo_row, u + "bo_row", u + "BOROW", D)
    load_weight_bf16(c, "L2_attn_a_Wq", u + "W", D, D)
    emit_proj(c, "XHT", u + "W", u + "QT", D, out_major="f",
              bias_tile=(c.t[u + "bq"], u + "bq"))
    _reload_w(c, "L2_attn_a_Wk", u + "W", D)
    emit_proj(c, "XHT", u + "W", u + "KT", D, out_major="f",
              bias_tile=(c.t[u + "bk"], u + "bk"))
    _reload_w(c, "L2_attn_a_Wv", u + "W", D)
    c.sb(u + "VT", [128, ND * T], BF16)
    emit_proj(c, "XHT", u + "W", u + "VT", D, out_major="f",
              bias_tile=(c.t[u + "bv"], u + "bv"))
    emit_fmajor_to_tmajor(c, u + "VT", u + "VTM")
    emit_mha(c, u, u + "QT", u + "KT", u + "VTM", "XHT", H=16, dk=64, dv=64,
             scale=1.0 / 8.0, causal=True)
    emit_tmajor_to_fmajor(c, "XHT", u + "VT")
    _reload_w(c, "L2_attn_a_Wo", u + "W", D)
    emit_proj(c, u + "VT", u + "W", u + "VTM", D, out_major="t",
              addrow_tile=(BOROW, u + "BOROW"))
    emit_residual_from_tmajor(c, u + "VTM")


def _reload_w(c, dram_name, sb_name, n_out):
    Wsb = c.t[sb_name]
    wv = c.t[dram_name].rearrange("(cd p) f -> p cd f", p=128)
    dst = Wsb.rearrange("p (cd f) -> p cd f", f=n_out)
    keys = c.t[sb_name + "_keys"]
    c.P.dma(lambda nc, wv=wv, dst=dst: nc.gpsimd.dma_start(out=dst, in_=wv),
            writes=keys, queue="gpsimd", tag=sb_name + "rl")



def stage_L1_attn_b(inputs):
    return {k: np.ascontiguousarray(np.asarray(inputs[k], np.float32))
            for k in ("L1_attn_b_Wq", "L1_attn_b_Wk", "L1_attn_b_Wv")}


def declare_L1_attn_b(c: Ctx):
    for k in ("L1_attn_b_Wq", "L1_attn_b_Wk", "L1_attn_b_Wv"):
        c.dram(k, [D, D], F32, kind="ExternalInput")


def emit_L1_attn_b(c: Ctx, uid="a6"):
    u = uid
    first_use = (u + "QT") not in c.t
    if first_use:
        c.sb(u + "QT", [128, ND * T], BF16)
        c.sb(u + "KT", [128, ND * T], BF16)
        c.sb(u + "VTM", [128, NT * D], BF16)
    for wname, dstk, major in [("L1_attn_b_Wq", u + "QT", "f"),
                               ("L1_attn_b_Wk", u + "KT", "f"),
                               ("L1_attn_b_Wv", u + "VTM", "t")]:
        if (u + "W") not in c.t:
            load_weight_bf16(c, wname, u + "W", D, D)
        else:
            _reload_w(c, wname, u + "W", D)
        emit_proj(c, "XHT", u + "W", dstk, D, out_major=major)
    emit_mha(c, u, u + "QT", u + "KT", u + "VTM", "XHT", H=16, dk=64, dv=64,
             scale=1.0 / 8.0, causal=False)
    emit_residual_from_tmajor(c, "XHT")


def stage_L3_ffn(inputs):
    d = {
        "L3_ffn_w_freq_real": np.ascontiguousarray(np.asarray(inputs["L3_ffn_w_freq_real"], np.float32)),
        "L3_ffn_w_freq_imag": np.ascontiguousarray(np.asarray(inputs["L3_ffn_w_freq_imag"], np.float32)),
        "L3_ffn_bias": np.ascontiguousarray(np.asarray(inputs["L3_ffn_bias"], np.float32)),
    }
    ang = 2.0 * np.pi * np.outer(np.arange(D), np.arange(D)) / D
    d["c_fftc"] = np.cos(ang).astype(np.float32)
    d["c_ffts"] = np.sin(ang).astype(np.float32)
    return d


def declare_L3_ffn(c: Ctx):
    c.dram("L3_ffn_w_freq_real", [D, D], F32, kind="ExternalInput")
    c.dram("L3_ffn_w_freq_imag", [D, D], F32, kind="ExternalInput")
    c.dram("L3_ffn_bias", [D], F32, kind="ExternalInput")
    c.dram("c_fftc", [D, D], F32, kind="ExternalInput")
    c.dram("c_ffts", [D, D], F32, kind="ExternalInput")


def emit_L3_ffn(c: Ctx, uid="LF"):
    u = "LF3" + uid
    P = c.P
    first = (u + "M") not in c.t
    if first:
        M = c.sb(u + "M", [128, ND * D], BF16)
        bias_row = load_row_f32(c, "L3_ffn_bias", u + "b_row", D)
        emit_bcast_row(c, bias_row, u + "b_row", u + "BROW", D)
        c.open_scratch()
        CE, cek = load_weight_bf16(c, "c_fftc", u + "CE", D, D, scratch=True)
        SE, sek = load_weight_bf16(c, "c_ffts", u + "SE", D, D, scratch=True)
        SEN = c.sb(u + "SEN", [128, ND * D], BF16, scratch=True)
        P.op("vector", lambda nc: nc.vector.tensor_scalar_mul(SEN, SE, -1.0),
             reads=sek, writes=[u + "SENk"], tag=u)
        WR, wrk = load_weight_bf16(c, "L3_ffn_w_freq_real", u + "WR", D, D, scratch=True)
        WI, wik = load_weight_bf16(c, "L3_ffn_w_freq_imag", u + "WI", D, D, scratch=True)
        ART = c.sb(u + "ART", [128, ND * D], BF16, scratch=True)
        AIT = c.sb(u + "AIT", [128, ND * D], BF16, scratch=True)
        CEv = CE.rearrange("p (cd f) -> p cd f", f=D)
        SEv = SE.rearrange("p (cd f) -> p cd f", f=D)
        SENv = SEN.rearrange("p (cd f) -> p cd f", f=D)
        WRv = WR.rearrange("p (cd f) -> p cd f", f=D)
        WIv = WI.rearrange("p (cd f) -> p cd f", f=D)
        ARTv = ART.rearrange("p (cd f) -> p cd f", f=D)
        AITv = AIT.rearrange("p (cd f) -> p cd f", f=D)
        Mv = M.rearrange("p (cd f) -> p cd f", f=D)
        psb = ["PSA", "PSB"]
        bi = 0
        for dst, dstv, dk_, t1, r1, t2, r2 in [
            (u + "ARTk", ARTv, None, WRv, CEv, WIv, SEv),
            (u + "AITk", AITv, None, WIv, CEv, WRv, SENv),
        ]:
            for oc in range(ND):
                for ew in range(2):
                    ps = c.t[psb[bi % 2]]; bank = psb[bi % 2]; bi += 1
                    fns = []
                    for f in range(ND):
                        fns.append(lambda nc, f=f, oc=oc, ew=ew, ps=ps, t1=t1, r1=r1:
                                   nc.tensor.matmul(ps, t1[:, f, oc * 128:(oc + 1) * 128],
                                                    r1[:, f, ew * 512:(ew + 1) * 512],
                                                    start=(f == 0), stop=False))
                    for f in range(ND):
                        fns.append(lambda nc, f=f, oc=oc, ew=ew, ps=ps, t2=t2, r2=r2:
                                   nc.tensor.matmul(ps, t2[:, f, oc * 128:(oc + 1) * 128],
                                                    r2[:, f, ew * 512:(ew + 1) * 512],
                                                    start=False, stop=(f == ND - 1)))
                    P.matmul_group(fns, reads=cek + sek + wrk + wik + [u + "SENk"],
                                   writes=[bank], tag=u + "A")
                    P.op("scalar", lambda nc, dstv=dstv, oc=oc, ew=ew, ps=ps:
                         nc.scalar.copy(dstv[:, oc, ew * 512:(ew + 1) * 512], ps),
                         reads=[bank], writes=[dst], tag=u + "Aev")
        for ec in range(ND):
            for tw in range(2):
                ps = c.t[psb[bi % 2]]; bank = psb[bi % 2]; bi += 1
                fns = []
                for o in range(ND):
                    fns.append(lambda nc, o=o, ec=ec, tw=tw, ps=ps:
                               nc.tensor.matmul(ps, ARTv[:, o, ec * 128:(ec + 1) * 128],
                                                CEv[:, o, tw * 512:(tw + 1) * 512],
                                                start=(o == 0), stop=False))
                for o in range(ND):
                    fns.append(lambda nc, o=o, ec=ec, tw=tw, ps=ps:
                               nc.tensor.matmul(ps, AITv[:, o, ec * 128:(ec + 1) * 128],
                                                SENv[:, o, tw * 512:(tw + 1) * 512],
                                                start=False, stop=(o == ND - 1)))
                P.matmul_group(fns, reads=[u + "ARTk", u + "AITk"] + cek + [u + "SENk"],
                               writes=[bank], tag=u + "M")
                P.op("scalar", lambda nc, ec=ec, tw=tw, ps=ps: nc.scalar.mul(
                    Mv[:, ec, tw * 512:(tw + 1) * 512], ps, 1.0 / D),
                     reads=[bank], writes=[u + "Mk"], tag=u + "Mev")
        c.t[u + "M_keys"] = [u + "Mk"]
        c.close_scratch()
        c.sb(u + "OUTM", [128, NT * D], BF16)
    emit_proj(c, "XHT", u + "M", u + "OUTM", D, out_major="t",
              addrow_tile=(c.t[u + "BROW"], u + "BROW"))
    emit_residual_from_tmajor(c, u + "OUTM")


def stage_L1_attn_a(inputs):
    return {k: np.ascontiguousarray(np.asarray(inputs[k], np.float32))
            for k in ("L1_attn_a_Mq", "L1_attn_a_Mk", "L1_attn_a_Mv")}


def declare_L1_attn_a(c: Ctx):
    c.dram("L1_attn_a_Mq", [D, 512], F32, kind="ExternalInput")
    c.dram("L1_attn_a_Mk", [D, 512], F32, kind="ExternalInput")
    c.dram("L1_attn_a_Mv", [D, D], F32, kind="ExternalInput")


def emit_L1_attn_a(c: Ctx, uid="a4"):
    u = uid
    P = c.P
    H, dk, dv = 8, 64, 128
    c.sb(u + "QT", [128, 4 * T], BF16)
    c.sb(u + "KT", [128, 4 * T], BF16)
    c.sb(u + "VT", [128, ND * T], BF16)
    load_weight_bf16(c, "L1_attn_a_Mq", u + "W5", D, 512)
    emit_proj(c, "XHT", u + "W5", u + "QT", 512, out_major="f")
    _reload_w(c, "L1_attn_a_Mk", u + "W5", 512)
    emit_proj(c, "XHT", u + "W5", u + "KT", 512, out_major="f")
    load_weight_bf16(c, "L1_attn_a_Mv", u + "WV", D, D)
    emit_proj(c, "XHT", u + "WV", u + "VT", D, out_major="f")
    QT = c.t[u + "QT"]; KT = c.t[u + "KT"]; VT = c.t[u + "VT"]
    AO = c.t["XHT"]
    ID = c.t["IDENT"]
    QHA = c.sb(u + "QHA", [64, T], BF16)
    KHA = c.sb(u + "KHA", [64, T], BF16)
    UH = c.sb(u + "UH", [128, T], BF16)
    VE = c.sb(u + "VE", [128, 128], BF16)
    ZAC = c.sb(u + "ZAC", [128, 8], F32)
    MH = c.sb(u + "MHx", [128, 4], F32)
    OHT = c.sb(u + "OHT", [128, T], BF16)
    PSA, PSB, PSC, PSD = c.t["PSA"], c.t["PSB"], c.t["PSC"], c.t["PSD"]
    SQ = c.t["SQ"]
    PST = c.t["PST"]
    scale = 1.0 / 8.0
    for h in range(H):
        hq = f"{u}h{h}_"
        for j in range(8):
            base = (j % 2) * 64
            cf = j // 2
            P.dma(lambda nc, j=j, base=base, cf=cf, h=h: nc.sync.dma_start(
                out=QHA[:, j * 256:(j + 1) * 256],
                in_=QT[base:base + 64, cf * T + h * 256: cf * T + (h + 1) * 256]),
                reads=[u + "QT"], writes=[u + "qha"], queue="sync", tag=u + "qst")
            P.dma(lambda nc, j=j, base=base, cf=cf, h=h: nc.sync.dma_start(
                out=KHA[:, j * 256:(j + 1) * 256],
                in_=KT[base:base + 64, cf * T + h * 256: cf * T + (h + 1) * 256]),
                reads=[u + "KT"], writes=[u + "kha"], queue="sync", tag=u + "kst")
        P.op("vector", lambda nc: nc.vector.reduce_max(MH[0:64, 0:1], QHA, axis=AX.X,
                                                       apply_absolute_value=True),
             reads=[u + "qha"], writes=[hq + "mq"], tag=u + "mh")
        P.op("vector", lambda nc: nc.vector.reduce_max(MH[0:64, 1:2], KHA, axis=AX.X,
                                                       apply_absolute_value=True),
             reads=[u + "kha"], writes=[hq + "mk"], tag=u + "mh")
        P.op("vector", lambda nc: nc.vector.tensor_mul(MH[0:64, 2:3], MH[0:64, 0:1], MH[0:64, 0:1]),
             reads=[hq + "mq"], writes=[hq + "mq2"], tag=u + "mh")
        P.op("vector", lambda nc: nc.vector.tensor_mul(MH[0:64, 3:4], MH[0:64, 1:2], MH[0:64, 1:2]),
             reads=[hq + "mk"], writes=[hq + "mk2"], tag=u + "mh")
        P.matmul_group([lambda nc: nc.tensor.matmul(
            PSD[0:1, 0:2], c.t["ONESCOLF"][0:64, 0:1], MH[0:64, 2:4],
            start=True, stop=True)],
            reads=[hq + "mq2", hq + "mk2", "ONESCOLF"], writes=["PSD"], tag=u + "mm")
        P.op("vector", lambda nc: nc.vector.tensor_copy(MH[0:1, 2:4], PSD[0:1, 0:2]),
             reads=["PSD"], writes=[hq + "m2a"], tag=u + "mh")
        P.op("vector", lambda nc: nc.vector.tensor_mul(MH[0:1, 0:1], MH[0:1, 2:3], MH[0:1, 3:4]),
             reads=[hq + "m2a"], writes=[hq + "m2"], tag=u + "mh")
        P.op("scalar", lambda nc: nc.scalar.activation(MH[0:1, 1:2], MH[0:1, 0:1], AF.Sqrt),
             reads=[hq + "m2"], writes=[hq + "m3"], tag=u + "mh")
        P.matmul_group([lambda nc: nc.tensor.matmul(
            PSD[:, 2:3], c.t["ONE1F"][0:1, :], MH[0:1, 1:2], start=True, stop=True)],
            reads=[hq + "m3", "ONE1F"], writes=["PSDb"], tag=u + "mb")
        P.op("vector", lambda nc: nc.vector.tensor_scalar_mul(MH[:, 0:1], PSD[:, 2:3], -scale),
             reads=["PSDb"], writes=[hq + "bias"], tag=u + "mh")
        for kb in range(16):
            sqv = SQ.rearrange("p (a b) -> p a b", b=512)
            for qg in range(4):
                psS = sqv[:, qg % 2, :]
                bank = f"SQ{qg % 2}"
                P.matmul_group([lambda nc, kb=kb, qg=qg, psS=psS: nc.tensor.matmul(
                    psS, KHA[:, kb * 128:(kb + 1) * 128], QHA[:, qg * 512:(qg + 1) * 512],
                    start=True, stop=True)],
                    reads=[u + "qha", u + "kha"], writes=[bank], tag=u + "S")
                P.op("scalar", lambda nc, kb=kb, qg=qg, psS=psS: nc.scalar.activation(
                    UH[:, qg * 512:(qg + 1) * 512], psS, AF.Exp, scale=scale,
                    bias=MH[:, 0:1], accum_out=ZAC[:, qg:qg + 1]),
                    reads=[bank, hq + "bias"], writes=[hq + f"U{qg}", hq + f"za{qg}"],
                    tag=u + "exp")
            P.op("vector", lambda nc: nc.vector.reduce_sum(ZAC[:, 4:5], ZAC[:, 0:4], axis=AX.X),
                 reads=[hq + "za0", hq + "za1", hq + "za2", hq + "za3"],
                 writes=[hq + "z"], tag=u + "z")
            P.op("vector", lambda nc: nc.vector.reciprocal(ZAC[:, 5:6], ZAC[:, 4:5]),
                 reads=[hq + "z"], writes=[hq + "zr"], tag=u + "z")
            j = kb // 2; half = kb % 2
            P.op("tensor", lambda nc, j=j, half=half, h=h: nc.tensor.transpose(
                PST[:, 0:128], VT[:, j * T + h * 256 + half * 128: j * T + h * 256 + half * 128 + 128], ID),
                reads=[u + "VT", "IDENT"], writes=["PST_0"], tag=u + "vtr")
            P.op("vector", lambda nc: nc.vector.tensor_scalar_mul(VE, PST[:, 0:128], ZAC[:, 5:6]),
                 reads=["PST_0", hq + "zr"], writes=[hq + "ve"], tag=u + "vsc")
            for qg in range(4):
                psO = [PSA, PSB, PSC, PSD][qg]
                bank = ["PSA", "PSB", "PSC", "PSD"][qg]
                P.op("tensor", lambda nc, qg=qg, psO=psO, kb=kb: nc.tensor.matmul(
                    psO, VE, UH[:, qg * 512:(qg + 1) * 512],
                    start=(kb == 0), stop=(kb == 15)),
                    reads=[hq + "ve", hq + f"U{qg}"] + ([bank] if kb == 0 else []),
                    writes=[bank] if kb == 15 else [bank + "_t"], tag=u + "PV")
        for qg in range(4):
            psO = [PSA, PSB, PSC, PSD][qg]
            bank = ["PSA", "PSB", "PSC", "PSD"][qg]
            P.op("scalar", lambda nc, qg=qg, psO=psO: nc.scalar.copy(
                OHT[:, qg * 512:(qg + 1) * 512], psO),
                reads=[bank, bank + "_t"], writes=[hq + "oht"], tag=u + "oev")
        for j in range(8):
            for half in range(2):
                P.op("tensor", lambda nc, j=j, half=half: nc.tensor.transpose(
                    PST[:, 0:128], OHT[:, j * 256 + half * 128: j * 256 + half * 128 + 128], ID),
                    reads=[hq + "oht", "IDENT"], writes=["PST_0"], tag=u + "otr")
                it = 2 * h + half
                P.op("vector", lambda nc, it=it, j=j: nc.vector.tensor_copy(
                    AO[:, it * D + j * 128: it * D + (j + 1) * 128], PST[:, 0:128]),
                    reads=["PST_0"], writes=["XHT"], tag=u + "ocp")
    emit_residual_from_tmajor(c, "XHT")


def stage_L0_attn_b(inputs):
    d = {}
    for k in ("Wk_t", "Wv_t", "Wr_t", "Wk_c", "Wv_c", "Wr_c"):
        d[f"L0_attn_b_{k}"] = np.ascontiguousarray(np.asarray(inputs[f"L0_attn_b_{k}"], np.float32).T)
    for k in ("time_decay", "time_first", "ln1_g", "ln1_b", "ln2_g", "ln2_b",
              "mix_k", "mix_v", "mix_r", "cmix_k", "cmix_r"):
        d[f"L0_attn_b_{k}"] = np.ascontiguousarray(np.asarray(inputs[f"L0_attn_b_{k}"], np.float32))
    return d


def declare_L0_attn_b(c: Ctx):
    for k, shp in [("Wk_t", [D, D]), ("Wv_t", [D, D]), ("Wr_t", [D, D]),
                   ("Wk_c", [D, 4096]), ("Wv_c", [4096, D]), ("Wr_c", [D, D])]:
        c.dram(f"L0_attn_b_{k}", shp, F32, kind="ExternalInput")
    for k in ("time_decay", "time_first", "ln1_g", "ln1_b", "ln2_g", "ln2_b",
              "mix_k", "mix_v", "mix_r", "cmix_k", "cmix_r"):
        c.dram(f"L0_attn_b_{k}", [D], F32, kind="ExternalInput")


def _affine_ln_fmajor(c, u, src_key, dst_key, g_col, b_col, par_keys, tag):
    P = c.P
    c.open_scratch()
    SRC = c.t[src_key]
    DST = c.t[dst_key]
    PSD = c.t["PSD"]
    MU = c.sb(u + tag + "MU", [128, 2 * T], BF16, scratch=True)
    ROWS = c.sb(u + tag + "ROWS", [1, 2 * T], F32, scratch=True)
    SQT = c.sb(u + tag + "SQT", [128, ND * 512], BF16, scratch=True)
    TMPC = c.sb(u + tag + "TMPC", [128, T], F32, scratch=True)
    rD = 1.0 / D
    for tg in range(4):
        w0 = tg * 512
        fns = []
        for cd in range(ND):
            fns.append(lambda nc, cd=cd, w0=w0: nc.tensor.matmul(
                PSD[0:1, 0:512], c.t["ONESCOL"][:, 0:1],
                SRC[:, cd * T + w0: cd * T + w0 + 512],
                start=(cd == 0), stop=(cd == ND - 1)))
        P.matmul_group(fns, reads=[src_key, "ONESCOL"], writes=["PSD"], tag=tag)
        P.op("vector", lambda nc, w0=w0: nc.vector.tensor_scalar_mul(
            ROWS[0:1, w0:w0 + 512], PSD[0:1, 0:512], rD),
            reads=["PSD"], writes=[tag + "mu_r"], tag=tag)
        for cd in range(ND):
            P.op("scalar", lambda nc, cd=cd, w0=w0: nc.scalar.activation(
                SQT[:, cd * 512:(cd + 1) * 512],
                SRC[:, cd * T + w0: cd * T + w0 + 512], AF.Square),
                reads=[src_key], writes=[tag + "sqt" + str(cd)], tag=tag)
        fns = []
        for cd in range(ND):
            fns.append(lambda nc, cd=cd: nc.tensor.matmul(
                PSD[0:1, 0:512], c.t["ONESCOL"][:, 0:1],
                SQT[:, cd * 512:(cd + 1) * 512],
                start=(cd == 0), stop=(cd == ND - 1)))
        P.matmul_group(fns, reads=[tag + "sqt" + str(cd2) for cd2 in range(ND)] + ["ONESCOL"],
                       writes=["PSD"], tag=tag)
        P.op("vector", lambda nc, w0=w0: nc.vector.tensor_mul(
            ROWS[0:1, T + w0: T + w0 + 512], ROWS[0:1, w0:w0 + 512], ROWS[0:1, w0:w0 + 512]),
            reads=[tag + "mu_r"], writes=[tag + "mu2"], tag=tag)
        P.op("vector", lambda nc, w0=w0: nc.vector.scalar_tensor_tensor(
            ROWS[0:1, T + w0: T + w0 + 512], PSD[0:1, 0:512], rD,
            ROWS[0:1, T + w0: T + w0 + 512], ALU.mult, ALU.subtract),
            reads=["PSD", tag + "mu2"], writes=[tag + "var"], tag=tag)
        P.op("scalar", lambda nc, w0=w0: nc.scalar.activation(
            ROWS[0:1, T + w0: T + w0 + 512], ROWS[0:1, T + w0: T + w0 + 512],
            AF.Sqrt, bias=c.t["EPSB"][0:1, :]),
            reads=[tag + "var", "EPSB"], writes=[tag + "sqv"], tag=tag)
        P.op("vector", lambda nc, w0=w0: nc.vector.reciprocal(
            ROWS[0:1, T + w0: T + w0 + 512], ROWS[0:1, T + w0: T + w0 + 512]),
            reads=[tag + "sqv"], writes=[tag + "rstd"], tag=tag)
    for half, rk in [(0, tag + "mu_r"), (1, tag + "rstd")]:
        for tg in range(4):
            w0 = tg * 512
            P.matmul_group([lambda nc, half=half, w0=w0: nc.tensor.matmul(
                PSD[:, 0:512], c.t["ONE1F"][0:1, :],
                ROWS[0:1, half * T + w0: half * T + w0 + 512],
                start=True, stop=True)],
                reads=[rk, "ONE1F"], writes=["PSD"], tag=tag)
            P.op("vector", lambda nc, half=half, w0=w0: nc.vector.tensor_copy(
                MU[:, half * T + w0: half * T + w0 + 512], PSD[:, 0:512]),
                reads=["PSD"], writes=[tag + "murep" + str(half)], tag=tag)
    for cd in range(ND):
        P.op("vector", lambda nc, cd=cd: nc.vector.tensor_sub(
            TMPC, SRC[:, cd * T:(cd + 1) * T], MU[:, 0:T]),
            reads=[src_key, tag + "murep0"], writes=[tag + "t1"], tag=tag)
        P.op("vector", lambda nc, cd=cd: nc.vector.tensor_mul(
            TMPC, TMPC, MU[:, T:2 * T]),
            reads=[tag + "t1", tag + "murep1"], writes=[tag + "t2"], tag=tag)
        P.op("vector", lambda nc, cd=cd: nc.vector.tensor_scalar(
            DST[:, cd * T:(cd + 1) * T], TMPC,
            g_col[:, cd:cd + 1], b_col[:, cd:cd + 1], ALU.mult, ALU.add),
            reads=[tag + "t2"] + par_keys, writes=[dst_key], tag=tag)
    c.close_scratch()


def _time_shift_fmajor(c, u, src_key, dst_key, mix_col, mix_keys, tag):
    P = c.P
    SRC = c.t[src_key]
    DST = c.t[dst_key]
    OMX = c.t[u + "OMX"]
    TMPC = c.t[u + "TSTMP"]
    for cd in range(ND):
        P.op("vector", lambda nc, cd=cd: nc.vector.tensor_scalar(
            OMX[:, cd:cd + 1], mix_col[:, cd:cd + 1], -1.0, 1.0, ALU.mult, ALU.add),
            reads=mix_keys, writes=[u + "omx" + str(cd)], tag=tag)
        P.op("vector", lambda nc, cd=cd: nc.vector.tensor_scalar_mul(
            TMPC, SRC[:, cd * T:(cd + 1) * T], mix_col[:, cd:cd + 1]),
            reads=[src_key] + mix_keys, writes=[u + "ts1"], tag=tag)
        P.op("vector", lambda nc, cd=cd: nc.vector.scalar_tensor_tensor(
            DST[:, cd * T + 1:(cd + 1) * T], SRC[:, cd * T: cd * T + T - 1],
            OMX[:, cd:cd + 1], TMPC[:, 1:T], ALU.mult, ALU.add),
            reads=[src_key, u + "ts1", u + "omx" + str(cd)], writes=[dst_key], tag=tag)
        P.op("vector", lambda nc, cd=cd: nc.vector.tensor_copy(
            DST[:, cd * T: cd * T + 1], TMPC[:, 0:1]),
            reads=[u + "ts1"], writes=[dst_key], tag=tag)


def _proj_f_to_dram(c, u, src_key, w_name, scr, scrkey, act=None, tag="pd"):
    P = c.P
    SRC = c.t[src_key]
    W = c.t[w_name]
    wkeys = c.t[w_name + "_keys"]
    Wv = W.rearrange("p (cd f) -> p cd f", f=D)
    psb = ["PSA", "PSB"]
    for fc in range(ND):
        stg = c.t[u + f"STG{fc % 2}"]
        stgk = u + f"STG{fc % 2}"
        for tg in range(4):
            w0 = tg * 512
            ps = c.t[psb[tg % 2]]; bank = psb[tg % 2]
            fns = []
            for cd in range(ND):
                fns.append(lambda nc, cd=cd, fc=fc, w0=w0, ps=ps: nc.tensor.matmul(
                    ps, Wv[:, cd, fc * 128:(fc + 1) * 128],
                    SRC[:, cd * T + w0: cd * T + w0 + 512],
                    start=(cd == 0), stop=(cd == ND - 1)))
            P.matmul_group(fns, reads=[src_key] + wkeys, writes=[bank], tag=tag)
            if act is None:
                P.op("scalar", lambda nc, w0=w0, ps=ps, stg=stg: nc.scalar.copy(
                    stg[:, w0:w0 + 512], ps), reads=[bank], writes=[stgk], tag=tag)
            else:
                P.op("scalar", lambda nc, w0=w0, ps=ps, stg=stg, act=act: nc.scalar.activation(
                    stg[:, w0:w0 + 512], ps, act), reads=[bank], writes=[stgk], tag=tag)
        P.dma(lambda nc, fc=fc, stg=stg: nc.sync.dma_start(out=scr[fc], in_=stg),
              reads=[stgk], writes=[scrkey + str(fc)], queue="sync", tag=tag + "sp")


def emit_L0_attn_b(c: Ctx, uid="a2"):
    u = uid
    P = c.P
    EPS_WKV = 1e-8
    PARS = c.sb(u + "PARS", [128, 16 * ND], F32)
    par_names = ["ln1_g", "ln1_b", "ln2_g", "ln2_b", "mix_k", "mix_v", "mix_r",
                 "cmix_k", "cmix_r", "time_decay", "time_first"]
    for j, nm in enumerate(par_names):
        src = c.t[f"L0_attn_b_{nm}"].rearrange("(cd p) -> p cd", p=128)
        P.dma(lambda nc, j=j, src=src: nc.sync.dma_start(
            out=PARS[:, j * ND:(j + 1) * ND], in_=src, allow_slow_non_contiguous=True),
            writes=[u + "par"], queue="sync", tag=u + "par")
    pk = [u + "par"]
    col = lambda j: PARS[:, j * ND:(j + 1) * ND]
    c.sb(u + "OMX", [128, ND], F32)
    c.sb(u + "TSTMP", [128, T], BF16)
    scrK = c.dram(u + "scrK", [ND, 128, T], BF16)
    scrV = c.dram(u + "scrV", [ND, 128, T], BF16)
    scrR = c.dram(u + "scrR", [ND, 128, T], BF16)
    scrX = c.dram(u + "scrX", [ND, 128, T], BF16)
    scrk2 = c.dram(u + "scrk2", [NT, 128, 4096], BF16)

    c.open_scratch()
    H1T = c.sb(u + "H1T", [128, ND * T], BF16, scratch=True)
    TS = c.sb(u + "TS", [128, ND * T], BF16, scratch=True)
    c.sb(u + "STG0", [128, T], BF16, scratch=True)
    c.sb(u + "STG1", [128, T], BF16, scratch=True)
    _affine_ln_fmajor(c, u, "XHT", u + "H1T", col(0), col(1), pk, u + "ln1")
    load_weight_bf16(c, "L0_attn_b_Wk_t", u + "W", D, D, scratch=True)
    _time_shift_fmajor(c, u, u + "H1T", u + "TS", col(4), pk, u + "tsk")
    _proj_f_to_dram(c, u, u + "TS", u + "W", scrK, u + "DK", tag=u + "pk")
    _reload_w(c, "L0_attn_b_Wv_t", u + "W", D)
    _time_shift_fmajor(c, u, u + "H1T", u + "TS", col(5), pk, u + "tsv")
    _proj_f_to_dram(c, u, u + "TS", u + "W", scrV, u + "DV", tag=u + "pv")
    _reload_w(c, "L0_attn_b_Wr_t", u + "W", D)
    _time_shift_fmajor(c, u, u + "H1T", u + "TS", col(6), pk, u + "tsr")
    _proj_f_to_dram(c, u, u + "TS", u + "W", scrR, u + "DR", act=AF.Sigmoid, tag=u + "pr")
    c.close_scratch()

    c.open_scratch()
    XIN = c.sb(u + "XIN", [128, ND * T], BF16, scratch=True)
    EU = c.sb(u + "EU", [128, ND], F32, scratch=True)
    G8 = c.sb(u + "G8", [128, 4], F32, scratch=True)
    KC = c.sb(u + "KC", [128, T], BF16, scratch=True)
    VC = c.sb(u + "VC", [128, T], BF16, scratch=True)
    RC = c.sb(u + "RC", [128, T], BF16, scratch=True)
    EK = c.sb(u + "EK", [128, T], F32, scratch=True)
    E2 = c.sb(u + "E2", [128, T], F32, scratch=True)
    NS = c.sb(u + "NS", [128, T], F32, scratch=True)
    DS = c.sb(u + "DS", [128, T], F32, scratch=True)
    NUM = c.sb(u + "NUM", [128, T], F32, scratch=True)
    DEN = c.sb(u + "DEN", [128, T], F32, scratch=True)
    ONESB = c.sb(u + "ONESB", [128, T], BF16, scratch=True)
    WEXP = c.sb(u + "WEXP", [128, T], F32, scratch=True)
    IWEXP = c.sb(u + "IWEXP", [128, T], F32, scratch=True)
    P.op("vector", lambda nc: nc.vector.memset(ONESB, 1.0), writes=[u + "onesb"], tag=u)
    W1 = WEXP
    W2 = IWEXP
    for cd in range(ND):
        P.op("scalar", lambda nc, cd=cd: nc.scalar.activation(
            EU[:, cd:cd + 1], col(10)[:, cd:cd + 1], AF.Exp),
            reads=pk, writes=[u + "eu"], tag=u + "eu")
    for cd in range(ND):
        q = f"{u}wk{cd}_"
        P.dma(lambda nc, cd=cd: nc.sync.dma_start(out=KC, in_=scrK[cd]),
              reads=[u + "DK" + str(cd)], writes=[u + "kc"], queue="sync", tag=q)
        P.dma(lambda nc, cd=cd: nc.sync.dma_start(out=VC, in_=scrV[cd]),
              reads=[u + "DV" + str(cd)], writes=[u + "vc"], queue="sync", tag=q)
        P.dma(lambda nc, cd=cd: nc.sync.dma_start(out=RC, in_=scrR[cd]),
              reads=[u + "DR" + str(cd)], writes=[u + "rc"], queue="sync", tag=q)
        def V(fn, rds, wrs):
            P.op("vector", fn, reads=rds, writes=wrs, tag=u + "wkv")
        P.op("vector", lambda nc: nc.vector.reduce_max(G8[:, 0:1], KC, axis=AX.X),
             reads=[u + "kc"], writes=[q + "g"], tag=u + "wkv")
        V(lambda nc: nc.vector.tensor_scalar_mul(G8[:, 1:2], G8[:, 0:1], -1.0),
          [q + "g"], [q + "ng"])
        P.op("scalar", lambda nc: nc.scalar.activation(EK, KC, AF.Exp, bias=G8[:, 1:2]),
             reads=[u + "kc", q + "ng"], writes=[q + "ek"], tag=u + "wkv")
        V(lambda nc: nc.vector.tensor_tensor_scan(DS, ONESB, EK, 0.0, ALU.mult, ALU.add),
          [q + "ek", u + "onesb"], [q + "ds"])
        V(lambda nc: nc.vector.memset(DEN[:, 0:1], 0.0), [], [q + "dz"])
        V(lambda nc: nc.vector.tensor_copy(DEN[:, 1:T], DS[:, 0:T - 1]),
          [q + "ds"], [q + "dprev"])
        V(lambda nc: nc.vector.tensor_max(W1, DEN, EK),
          [q + "dprev", q + "dz", q + "ek"], [q + "mx"])
        V(lambda nc: nc.vector.reciprocal(W2, W1), [q + "mx"], [q + "rmx"])
        V(lambda nc: nc.vector.tensor_mul(W1, DS, W2), [q + "ds", q + "rmx"], [q + "S"])
        P.op("scalar", lambda nc: nc.scalar.activation(W1, W1, AF.Ln),
             reads=[q + "S"], writes=[q + "lnS"], tag=u + "wkv")
        V(lambda nc: nc.vector.tensor_tensor_scan(W2, ONESB, W1, 0.0, ALU.mult, ALU.add),
          [q + "lnS", u + "onesb"], [q + "L"])
        V(lambda nc: nc.vector.memset(W1[:, 0:1], 0.0), [], [q + "l0"])
        V(lambda nc: nc.vector.tensor_copy(W1[:, 1:T], W2[:, 0:T - 1]),
          [q + "L"], [q + "lprev"])
        P.op("scalar", lambda nc: nc.scalar.activation(W1, W1, AF.Exp),
             reads=[q + "lprev", q + "l0"], writes=[q + "cpp"], tag=u + "wkv")
        V(lambda nc: nc.vector.reciprocal(W2, W1), [q + "cpp"], [q + "rcp"])
        V(lambda nc: nc.vector.tensor_mul(NS, EK, VC), [q + "ek", u + "vc"], [q + "ekv"])
        V(lambda nc: nc.vector.tensor_mul(NS, NS, W2), [q + "ekv", q + "rcp"], [q + "qq"])
        V(lambda nc: nc.vector.tensor_tensor_scan(NS, ONESB, NS, 0.0, ALU.mult, ALU.add),
          [q + "qq", u + "onesb"], [q + "Q"])
        V(lambda nc: nc.vector.memset(NUM[:, 0:1], 0.0), [], [q + "q0"])
        V(lambda nc: nc.vector.tensor_copy(NUM[:, 1:T], NS[:, 0:T - 1]),
          [q + "Q"], [q + "qprev"])
        V(lambda nc: nc.vector.tensor_mul(NUM, NUM, W1),
          [q + "qprev", q + "q0", q + "cpp"], [q + "cq"])
        V(lambda nc, cd=cd: nc.vector.tensor_scalar_mul(E2, EK, EU[:, cd:cd + 1]),
          [q + "ek", u + "eu"], [q + "e2"])
        V(lambda nc: nc.vector.tensor_mul(NS, E2, VC), [q + "e2", u + "vc"], [q + "e2v"])
        V(lambda nc: nc.vector.tensor_add(NUM, NUM, NS), [q + "cq", q + "e2v"], [q + "num"])
        V(lambda nc: nc.vector.tensor_max(NS, DEN, E2),
          [q + "dprev", q + "dz", q + "e2"], [q + "mx2"])
        V(lambda nc: nc.vector.tensor_add(DEN, DEN, E2),
          [q + "dprev", q + "dz", q + "e2", q + "mx2"], [q + "d1"])
        V(lambda nc: nc.vector.scalar_tensor_tensor(DEN, NS, EPS_WKV, DEN, ALU.mult, ALU.add),
          [q + "mx2", q + "d1"], [q + "den"])
        V(lambda nc: nc.vector.reciprocal(DEN, DEN), [q + "den"], [q + "rd"])
        V(lambda nc: nc.vector.tensor_mul(NUM, NUM, DEN), [q + "num", q + "rd"], [q + "out"])
        V(lambda nc: nc.vector.tensor_mul(NUM, NUM, RC), [q + "out", u + "rc"], [q + "rw"])
        P.op("vector", lambda nc, cd=cd: nc.vector.tensor_add(
            XIN[:, cd * T:(cd + 1) * T], c.t["XHT"][:, cd * T:(cd + 1) * T], NUM),
            reads=[q + "rw", "XHT"], writes=[u + "XINk"], tag=u + "wkv")
        P.dma(lambda nc, cd=cd: nc.sync.dma_start(
            out=scrX[cd], in_=XIN[:, cd * T:(cd + 1) * T]),
            reads=[u + "XINk"], writes=[u + "DX" + str(cd)], queue="sync", tag=u + "xsp")
    c.close_scratch()

    scrRC = c.dram(u + "scrRC", [NT, 128, D], BF16)
    scrT2 = c.dram(u + "scrT2", [ND, 128, T], BF16)
    c.open_scratch()
    H2T = c.sb(u + "H2T", [128, ND * T], BF16, scratch=True)
    c.open_scratch()
    XINL = c.sb(u + "XINL", [128, ND * T], BF16, scratch=True)
    for cd in range(ND):
        P.dma(lambda nc, cd=cd: nc.sync.dma_start(
            out=XINL[:, cd * T:(cd + 1) * T], in_=scrX[cd]),
            reads=[u + "DX" + str(cd)], writes=[u + "XINL"], queue="sync", tag=u + "xrl")
    _affine_ln_fmajor(c, u, u + "XINL", u + "H2T", col(2), col(3), pk, u + "ln2")
    c.close_scratch()
    c.open_scratch()
    TS2 = c.sb(u + "TS2", [128, ND * T], BF16, scratch=True)
    RCM = c.sb(u + "RCM", [128, NT * D], BF16, scratch=True)
    load_weight_bf16(c, "L0_attn_b_Wr_c", u + "W2", D, D, scratch=True)
    _time_shift_fmajor(c, u, u + "H2T", u + "TS2", col(8), pk, u + "tsr2")
    emit_proj(c, u + "TS2", u + "W2", u + "RCM", D, out_major="t", act=AF.Sigmoid)
    for i in range(NT):
        P.dma(lambda nc, i=i: nc.sync.dma_start(
            out=scrRC[i], in_=RCM[:, i * D:(i + 1) * D]),
            reads=[u + "RCM"], writes=[u + f"Drc{i}"], queue="sync", tag=u + "rcsp")
    _time_shift_fmajor(c, u, u + "H2T", u + "TS2", col(7), pk, u + "tsk2")
    for cd in range(ND):
        P.dma(lambda nc, cd=cd: nc.sync.dma_start(
            out=scrT2[cd], in_=TS2[:, cd * T:(cd + 1) * T]),
            reads=[u + "TS2"], writes=[u + f"Dt2{cd}"], queue="sync", tag=u + "t2sp")
    c.close_scratch()
    c.close_scratch()

    c.open_scratch()
    TS2L = c.sb(u + "TS2L", [128, ND * T], BF16, scratch=True)
    for cd in range(ND):
        P.dma(lambda nc, cd=cd: nc.sync.dma_start(
            out=TS2L[:, cd * T:(cd + 1) * T], in_=scrT2[cd]),
            reads=[u + f"Dt2{cd}"], writes=[u + "TS2L"], queue="sync", tag=u + "t2rl")
    WKC = c.sb(u + "WKC", [128, ND * 2048], BF16, scratch=True)
    KCT = [c.sb(u + f"KCT{j}", [128, 4096], BF16, scratch=True) for j in range(2)]
    KCM = [c.sb(u + f"KCM{j}", [128, 4096], BF16, scratch=True) for j in range(2)]
    psf = [c.t["PSA"], c.t["PSB"], c.t["PSC"], c.t["PSD"]]
    psfk = ["PSA", "PSB", "PSC", "PSD"]
    wkcv = c.t["L0_attn_b_Wk_c"].rearrange("(cd p) f -> p cd f", p=128)
    WKCv = WKC.rearrange("p (cd f) -> p cd f", f=2048)
    for half in range(2):
        f0 = half * 2048
        for g2 in range(2):
            P.dma(lambda nc, f0=f0, g2=g2: nc.gpsimd.dma_start(
                out=WKCv[:, g2 * 4:(g2 + 1) * 4, :],
                in_=wkcv[:, g2 * 4:(g2 + 1) * 4, f0:f0 + 2048]),
                writes=[u + f"wkc{g2}"], queue="gpsimd", tag=u + "wkcld")
        for i in range(NT):
            ib = i % 2
            kcm = KCM[ib]; kcmk = u + f"KCM{ib}"
            kct = KCT[ib]; kctk = u + f"KCT{ib}"
            for gr in range(4):
                ps = psf[gr % 4]; bank = psfk[gr % 4]
                fns = []
                for cdd in range(ND):
                    fns.append(lambda nc, cdd=cdd, i=i, gr=gr, ps=ps: nc.tensor.matmul(
                        ps, TS2L[:, cdd * T + i * 128: cdd * T + (i + 1) * 128],
                        WKCv[:, cdd, gr * 512:(gr + 1) * 512],
                        start=(cdd == 0), stop=(cdd == ND - 1)))
                P.matmul_group(fns, reads=[u + "TS2L", u + "wkc0", u + "wkc1"],
                               writes=[bank], tag=u + "kc")
                P.op("scalar", lambda nc, gr=gr, ps=ps, kcm=kcm: nc.scalar.activation(
                    kcm[:, gr * 512:(gr + 1) * 512], ps, AF.Relu),
                    reads=[bank], writes=[kcmk], tag=u + "kcr")
            P.op("scalar", lambda nc, kcm=kcm: nc.scalar.activation(
                kcm[:, 0:2048], kcm[:, 0:2048], AF.Square),
                 reads=[kcmk], writes=[kcmk], tag=u + "kcs")
            pairs = [(kcm[:, cc * 128:(cc + 1) * 128], kct[:, cc * 128:(cc + 1) * 128])
                     for cc in range(16)]
            emit_transpose_tiles(c, kcmk, kctk, pairs)
            P.dma(lambda nc, kct=kct, i=i, half=half: nc.sync.dma_start(
                out=scrk2.rearrange("i p (h q) -> i p h q", h=2)[i, :, half, :],
                in_=kct[:, 0:2048]), reads=[kctk], writes=[u + f"Dkk{i}_{half}"],
                queue="sync", tag=u + "kcsp")
    c.close_scratch()

    c.open_scratch()
    WVC = c.sb(u + "WVC", [128, 32 * 1024], BF16, scratch=True)
    K2L = [c.sb(u + f"K2L{j}", [128, 4096], BF16, scratch=True) for j in range(2)]
    RCL = [c.sb(u + f"RCL{j}", [128, D], BF16, scratch=True) for j in range(2)]
    XNM = [c.sb(u + f"XNM{j}", [128, D], BF16, scratch=True) for j in range(2)]
    TMPO = c.sb(u + "TMPO", [128, 512], F32, scratch=True)
    wvcv = c.t["L0_attn_b_Wv_c"].rearrange("(cd p) f -> p cd f", p=128)
    WVCv = WVC.rearrange("p (cd f) -> p cd f", f=1024)
    scrXflat = scrX.rearrange("cd p t -> (cd p) t")
    for g2 in range(4):
        P.dma(lambda nc, g2=g2: nc.gpsimd.dma_start(
            out=WVCv[:, g2 * 8:(g2 + 1) * 8, :], in_=wvcv[:, g2 * 8:(g2 + 1) * 8, :]),
            writes=[u + f"wvc{g2}"], queue="gpsimd", tag=u + "wvcld")
    xv = c.t[c.xin_name].rearrange("(n p) d -> p n d", p=128)
    ov = c.t[c.xout_name].rearrange("(n p) d -> p n d", p=128)
    psf = [c.t["PSA"], c.t["PSB"], c.t["PSC"], c.t["PSD"]]
    psfk = ["PSA", "PSB", "PSC", "PSD"]
    for i in range(NT):
        ib = i % 2
        k2l = K2L[ib]; k2lk = u + f"K2L{ib}"
        rcl = RCL[ib]; rclk = u + f"RCL{ib}"
        xnm = XNM[ib]; xnmk = u + f"XNM{ib}"
        xt = c.t[f"XT{ib}"]; xtk = f"XT{ib}"
        P.dma(lambda nc, k2l=k2l, i=i: nc.sync.dma_start(out=k2l, in_=scrk2[i]),
              reads=[u + f"Dkk{i}_0", u + f"Dkk{i}_1"], writes=[k2lk],
              queue="sync", tag=u + "k2ld")
        P.dma(lambda nc, rcl=rcl, i=i: nc.sync.dma_start(out=rcl, in_=scrRC[i]),
              reads=[u + f"Drc{i}"], writes=[rclk], queue="sync", tag=u + "rcld")
        P.dma(lambda nc, xnm=xnm, i=i: nc.sync.dma_start_transpose(
            out=xnm, in_=scrXflat[:, i * 128:(i + 1) * 128]),
            reads=[u + "DX" + str(cdx) for cdx in range(ND)], writes=[xnmk],
            queue="sync", tag=u + "xnmld")
        P.dma(lambda nc, i=i, xt=xt: nc.sync.dma_start(out=xt, in_=xv[:, i, :]),
              reads=[f"D{c.xin_name}_{i}"], writes=[xtk], queue="sync", tag=u + "xld")
        for half in range(2):
            ps = psf[(i * 2 + half) % 4]; bank = psfk[(i * 2 + half) % 4]
            fns = []
            for cdd in range(32):
                fns.append(lambda nc, cdd=cdd, half=half, ps=ps, k2l=k2l: nc.tensor.matmul(
                    ps, k2l[:, cdd * 128:(cdd + 1) * 128],
                    WVCv[:, cdd, half * 512:(half + 1) * 512],
                    start=(cdd == 0), stop=(cdd == 31)))
            P.matmul_group(fns, reads=[k2lk] + [u + f"wvc{j}" for j in range(4)],
                           writes=[bank], tag=u + "vv")
            P.op("vector", lambda nc, half=half, ps=ps, rcl=rcl: nc.vector.tensor_mul(
                TMPO, ps, rcl[:, half * 512:(half + 1) * 512]),
                reads=[bank, rclk], writes=[u + "rcvv"], tag=u + "vvm")
            P.op("vector", lambda nc, half=half, xnm=xnm: nc.vector.tensor_add(
                TMPO, TMPO, xnm[:, half * 512:(half + 1) * 512]),
                reads=[u + "rcvv", xnmk], writes=[u + "rcv2"], tag=u + "vva")
            P.op("vector", lambda nc, xt=xt, half=half: nc.vector.tensor_add(
                xt[:, half * 512:(half + 1) * 512], xt[:, half * 512:(half + 1) * 512],
                TMPO),
                reads=[u + "rcv2", xtk], writes=[xtk], tag=u + "vvr")
        P.dma(lambda nc, i=i, xt=xt: nc.sync.dma_start(out=ov[:, i, :], in_=xt),
              reads=[xtk], writes=[f"D{c.xout_name}_{i}"], queue="sync", tag=u + "xst")
    c.close_scratch()


def stage_L2_ffn(inputs):
    import ml_dtypes
    d = {}
    d["L2_ffn_norm_weight"] = np.ascontiguousarray(np.asarray(inputs["L2_ffn_norm_weight"], np.float32))
    gw = inputs["L2_ffn_gate_weight"]
    d["L2_ffn_gwT"] = np.ascontiguousarray(np.asarray(gw).T)
    d["L2_ffn_gate_bias"] = np.ascontiguousarray(np.asarray(inputs["L2_ffn_gate_bias"]))
    m1 = np.asarray(inputs["L2_ffn_mlp1_weight"])
    b1 = np.asarray(inputs["L2_ffn_mlp1_bias"])
    perm = np.concatenate([np.arange(0, 2048, 2), np.arange(1, 2048, 2)])
    m1t = m1.transpose(0, 2, 1)[:, :, perm]
    d["L2_ffn_m1t"] = np.ascontiguousarray(m1t)
    d["L2_ffn_b1p"] = np.ascontiguousarray(b1[:, perm])
    m2 = np.asarray(inputs["L2_ffn_mlp2_weight"])
    d["L2_ffn_m2t"] = np.ascontiguousarray(m2.transpose(0, 2, 1))
    d["L2_ffn_mlp2_bias"] = np.ascontiguousarray(np.asarray(inputs["L2_ffn_mlp2_bias"]))
    return d


def declare_L2_ffn(c: Ctx):
    c.dram("L2_ffn_norm_weight", [D], F32, kind="ExternalInput")
    c.dram("L2_ffn_gwT", [D, 8], BF16, kind="ExternalInput")
    c.dram("L2_ffn_gate_bias", [8], BF16, kind="ExternalInput")
    c.dram("L2_ffn_m1t", [8, D, 2048], BF16, kind="ExternalInput")
    c.dram("L2_ffn_b1p", [8, 2048], BF16, kind="ExternalInput")
    c.dram("L2_ffn_m2t", [8, D, D], BF16, kind="ExternalInput")
    c.dram("L2_ffn_mlp2_bias", [8, D], BF16, kind="ExternalInput")


def emit_L2_ffn(c: Ctx, uid="e9"):
    u = uid
    P = c.P
    call = sum(1 for k2 in c.t if k2.startswith(u + "call"))
    c.t[u + f"call{call}"] = None
    q = f"{u}_{call}_"
    first = (u + "NRT") not in c.t
    if first:
        if (u + "XC") not in c.t:
            c.sb(u + "XC", [128, NT * D], BF16)
        c.sb(u + "NRT", [128, ND * T], BF16)
        c.sb(u + "LG", [128, NT * 8], F32)
        c.sb(u + "LGB", [128, NT * 8], BF16)
        c.sb(u + "RK", [128, NT * 8], F32)
        c.sb(u + "SEL", [128, 2 * NT * 8], F32)
        c.sb(u + "CW", [128, 8 * NT], F32)
        c.sb(u + "CMB", [128, NT * 8], F32)
        c.sb(u + "GW", [128, ND * 8], BF16)
        c.sb(u + "GBR", [128, 8], F32)
        c.sb(u + "ACC", [128, 4 * D], F32)
        c.sb(u + "WSB", [128, ND * 2048], BF16)
        c.sb(u + "A2T", [128, 4 * D], BF16)
        c.sb(u + "P1", [128, 2048], F32)
        c.sb(u + "ACT4", [128, D], BF16)
        c.sb(u + "TM1", [128, D], F32)
        c.sb(u + "TM2", [128, D], F32)
        c.sb(u + "B1R", [1, 2048], BF16)
        c.sb(u + "B2R", [1, D], BF16)
        c.sb(u + "RMS", [128, 2 * NT], F32)
        nw_row = load_row_f32(c, "L2_ffn_norm_weight", u + "nw_row", D)
        emit_bcast_row(c, nw_row, u + "nw_row", u + "NWRb", D)
        gv = c.t["L2_ffn_gwT"].rearrange("(cd p) e -> p cd e", p=128)
        P.dma(lambda nc: nc.gpsimd.dma_start(
            out=c.t[u + "GW"].rearrange("p (cd e) -> p cd e", e=8), in_=gv),
            writes=[u + "gw"], queue="gpsimd", tag=u)
        P.dma(lambda nc: nc.gpsimd.dma_start(
            out=c.t[u + "GBR"][0:1, :], in_=c.t["L2_ffn_gate_bias"][None, :]),
            writes=[u + "gbrow"], queue="gpsimd", tag=u)
        emit_bcast_row(c, c.t[u + "GBR"][0:1, :], u + "gbrow", u + "GBRb", 8)
    XC = c.t[u + "XC"]; NRT = c.t[u + "NRT"]; LG = c.t[u + "LG"]; LGB = c.t[u + "LGB"]
    RK = c.t[u + "RK"]; SEL = c.t[u + "SEL"]; CW = c.t[u + "CW"]; CMB = c.t[u + "CMB"]
    ACC = c.t[u + "ACC"]; WSB = c.t[u + "WSB"]; A2T = c.t[u + "A2T"]
    P1 = c.t[u + "P1"]; ACT4 = c.t[u + "ACT4"]; RMS = c.t[u + "RMS"]
    B1R = c.t[u + "B1R"]; B2R = c.t[u + "B2R"]
    NWRb = c.t[u + "NWRb"]; GBRb = c.t[u + "GBRb"]
    scrxc = c.dram(q + "scrxc", [NT, 128, D], BF16)
    scracc = c.dram(q + "scracc", [NT, 128, D], F32)
    psf = [c.t["PSA"], c.t["PSB"], c.t["PSC"], c.t["PSD"]]
    psfk = ["PSA", "PSB", "PSC", "PSD"]
    SQ = c.t["SQ"]
    ID = c.t["IDENT"]

    for i in range(NT):
        xc = XC[:, i * D:(i + 1) * D]
        r0 = RMS[:, 2 * i:2 * i + 1]
        r1 = RMS[:, 2 * i + 1:2 * i + 2]
        P.op("scalar", lambda nc, xc=xc, r0=r0: nc.scalar.activation(
            SQ, xc, AF.Square, accum_out=r0),
            reads=[u + "XC"], writes=["SQ", q + f"r0_{i}"], tag=q + "rms")
        P.op("vector", lambda nc, r0=r0: nc.vector.tensor_scalar_mul(r0, r0, 1.0 / D),
             reads=[q + f"r0_{i}"], writes=[q + f"rm_{i}"], tag=q + "rms")
        P.op("scalar", lambda nc, r0=r0, r1=r1: nc.scalar.activation(
            r1, r0, AF.Sqrt, bias=c.t["EPSB"]),
            reads=[q + f"rm_{i}", "EPSB"], writes=[q + f"rs_{i}"], tag=q + "rms")
        P.op("vector", lambda nc, r1=r1: nc.vector.reciprocal(r1, r1),
             reads=[q + f"rs_{i}"], writes=[q + f"rr_{i}"], tag=q + "rms")
        P.op("vector", lambda nc, xc=xc, r1=r1: nc.vector.tensor_scalar_mul(
            c.t[u + "TM1"], xc, r1),
            reads=[u + "XC", q + f"rr_{i}"], writes=[q + "tm1"], tag=q + "rms")
        P.op("vector", lambda nc: nc.vector.tensor_mul(ACT4, c.t[u + "TM1"], NWRb),
             reads=[q + "tm1", u + "NWRb"], writes=[q + "nmd"], tag=q + "rms")
        pairs = [(ACT4[:, cc * 128:(cc + 1) * 128],
                  NRT[:, cc * T + i * 128: cc * T + (i + 1) * 128]) for cc in range(ND)]
        emit_transpose_tiles(c, q + "nmd", u + "NRT", pairs)
        P.dma(lambda nc, i=i, xc=xc: nc.sync.dma_start(out=scrxc[i], in_=xc),
              reads=[u + "XC"], writes=[q + f"Dxc{i}"], queue="sync", tag=q + "xcsp")

    GWv = c.t[u + "GW"].rearrange("p (cd e) -> p cd e", e=8)
    for i in range(NT):
        ps = psf[i % 2]; bank = psfk[i % 2]
        fns = []
        for cd in range(ND):
            fns.append(lambda nc, cd=cd, i=i, ps=ps: nc.tensor.matmul(
                ps[:, 0:8], NRT[:, cd * T + i * 128: cd * T + (i + 1) * 128],
                GWv[:, cd, :], start=(cd == 0), stop=(cd == ND - 1)))
        P.matmul_group(fns, reads=[u + "NRT", u + "gw"], writes=[bank], tag=q + "g")
        P.op("vector", lambda nc, i=i, ps=ps: nc.vector.scalar_tensor_tensor(
            LG[:, i * 8:(i + 1) * 8], ps[:, 0:8], 1.0, GBRb[:, 0:8], ALU.mult, ALU.add),
            reads=[bank, u + "GBRb"], writes=[q + "lg"], tag=q + "g")
    P.op("vector", lambda nc: nc.vector.tensor_copy(LGB, LG), reads=[q + "lg"],
         writes=[q + "lgb"], tag=q + "g")
    LGv = LGB.rearrange("p (i e) -> p i e", e=8)
    RKv = RK.rearrange("p (i e) -> p i e", e=8)
    P.op("vector", lambda nc: nc.vector.memset(RK, 0.0), writes=[q + "rk"], tag=q + "t2")
    TMc = CW[:, 0:NT]
    for e in range(8):
        for f in range(8):
            if f == e:
                continue
            op = ALU.is_ge if f < e else ALU.is_gt
            P.op("vector", lambda nc, e=e, f=f, op=op: nc.vector.tensor_tensor(
                TMc, LGv[:, :, f], LGv[:, :, e], op),
                reads=[q + "lgb"], writes=[q + "tmc"], tag=q + "t2")
            P.op("vector", lambda nc, e=e: nc.vector.tensor_add(
                RKv[:, :, e], RKv[:, :, e], TMc),
                reads=[q + "tmc", q + "rk"], writes=[q + "rk"], tag=q + "t2")
    SEL0 = SEL[:, 0:NT * 8]; SEL1 = SEL[:, NT * 8:]
    P.op("vector", lambda nc: nc.vector.tensor_single_scalar(SEL0, RK, 0.0, ALU.is_equal),
         reads=[q + "rk"], writes=[q + "s0"], tag=q + "t2")
    P.op("vector", lambda nc: nc.vector.tensor_single_scalar(SEL1, RK, 1.0, ALU.is_equal),
         reads=[q + "rk"], writes=[q + "s1"], tag=q + "t2")
    P.op("vector", lambda nc: nc.vector.tensor_mul(CMB, LGB, SEL0),
         reads=[q + "lgb", q + "s0"], writes=[q + "m0"], tag=q + "t2")
    P.op("vector", lambda nc: nc.vector.reduce_sum(CW[:, 0:NT], CMB.rearrange(
        "p (i e) -> p i e", e=8), axis=AX.X), reads=[q + "m0"], writes=[q + "v1"], tag=q + "t2")
    P.op("vector", lambda nc: nc.vector.tensor_mul(CMB, LGB, SEL1),
         reads=[q + "lgb", q + "s1"], writes=[q + "m1"], tag=q + "t2")
    P.op("vector", lambda nc: nc.vector.reduce_sum(CW[:, NT:2 * NT], CMB.rearrange(
        "p (i e) -> p i e", e=8), axis=AX.X), reads=[q + "m1"], writes=[q + "v2"], tag=q + "t2")
    P.op("vector", lambda nc: nc.vector.tensor_sub(CW[:, 2 * NT:3 * NT], CW[:, NT:2 * NT],
                                                   CW[:, 0:NT]),
         reads=[q + "v1", q + "v2"], writes=[q + "dd"], tag=q + "t2")
    P.op("scalar", lambda nc: nc.scalar.activation(CW[:, 3 * NT:4 * NT],
                                                   CW[:, 2 * NT:3 * NT], AF.Exp),
         reads=[q + "dd"], writes=[q + "ew"], tag=q + "t2")
    P.op("vector", lambda nc: nc.vector.tensor_scalar_add(CW[:, 4 * NT:5 * NT],
                                                          CW[:, 3 * NT:4 * NT], 1.0),
         reads=[q + "ew"], writes=[q + "e1"], tag=q + "t2")
    P.op("vector", lambda nc: nc.vector.reciprocal(CW[:, 4 * NT:5 * NT],
                                                   CW[:, 4 * NT:5 * NT]),
         reads=[q + "e1"], writes=[q + "w1c"], tag=q + "t2")
    P.op("vector", lambda nc: nc.vector.tensor_mul(CW[:, 5 * NT:6 * NT],
                                                   CW[:, 3 * NT:4 * NT],
                                                   CW[:, 4 * NT:5 * NT]),
         reads=[q + "ew", q + "w1c"], writes=[q + "w2c"], tag=q + "t2")
    CMBv = CMB.rearrange("p (i e) -> p i e", e=8)
    SEL0v = SEL0.rearrange("p (i e) -> p i e", e=8)
    SEL1v = SEL1.rearrange("p (i e) -> p i e", e=8)
    for e in range(8):
        P.op("vector", lambda nc, e=e: nc.vector.tensor_mul(
            CMBv[:, :, e], SEL0v[:, :, e], CW[:, 4 * NT:5 * NT]),
            reads=[q + "s0", q + "w1c"], writes=[q + "cmbf"], tag=q + "t2")
        P.op("vector", lambda nc, e=e: nc.vector.tensor_mul(
            TMc, SEL1v[:, :, e], CW[:, 5 * NT:6 * NT]),
            reads=[q + "s1", q + "w2c"], writes=[q + "tmc"], tag=q + "t2")
        P.op("vector", lambda nc, e=e: nc.vector.tensor_add(
            CMBv[:, :, e], CMBv[:, :, e], TMc),
            reads=[q + "tmc", q + "cmbf"], writes=[q + "cmbf"], tag=q + "t2")

    WSBv = WSB.rearrange("p (cd f) -> p cd f", f=2048)
    m1v = c.t["L2_ffn_m1t"].rearrange("e (cd p) f -> e p cd f", p=128)
    m2v = c.t["L2_ffn_m2t"].rearrange("e (cd p) f -> e p cd f", p=128)
    for half in range(4):
        i0_ = half * 4
        P.op("vector", lambda nc: nc.vector.memset(ACC, 0.0), writes=[q + "acc"],
             tag=q + "acz")
        for e in range(8):
            for g2 in range(2):
                P.dma(lambda nc, e=e, g2=g2: nc.gpsimd.dma_start(
                    out=WSBv[:, g2 * 4:(g2 + 1) * 4, :], in_=m1v[e, :, g2 * 4:(g2 + 1) * 4, :]),
                    writes=[q + f"Q{2 * g2}", q + f"Q{2 * g2 + 1}"], queue="gpsimd",
                    tag=q + "m1ld")
            P.dma(lambda nc, e=e: nc.gpsimd.dma_start(
                out=B1R, in_=c.t["L2_ffn_b1p"][e][None, :]),
                writes=[q + "b1r"], queue="gpsimd", tag=q + "b1ld")
            for ii in range(4):
                i = i0_ + ii
                for gr in range(4):
                    ps = psf[gr % 4]; bank = psfk[gr % 4]
                    fns = []
                    for cd in range(ND):
                        fns.append(lambda nc, cd=cd, i=i, gr=gr, ps=ps: nc.tensor.matmul(
                            ps, NRT[:, cd * T + i * 128: cd * T + (i + 1) * 128],
                            WSBv[:, cd, gr * 512:(gr + 1) * 512],
                            start=(cd == 0), stop=False))
                    fns.append(lambda nc, i=i, gr=gr, ps=ps: nc.tensor.matmul(
                        ps, c.t["ONE1"], B1R[0:1, gr * 512:(gr + 1) * 512],
                        start=False, stop=True))
                    P.matmul_group(fns, reads=[u + "NRT", q + "b1r"] + [q + f"Q{j}" for j in range(4)],
                                   writes=[bank], tag=q + "p1")
                    P.op("vector", lambda nc, gr=gr, ps=ps: nc.vector.tensor_copy(
                        P1[:, gr * 512:(gr + 1) * 512], ps),
                        reads=[bank], writes=[q + "p1t"], tag=q + "p1e")
                P.op("vector", lambda nc: nc.vector.tensor_scalar_min(
                    c.t[u + "TM1"], P1[:, 0:D], 7.0),
                    reads=[q + "p1t"], writes=[q + "gmin"], tag=q + "glu")
                P.op("scalar", lambda nc: nc.scalar.activation(
                    c.t[u + "TM2"], c.t[u + "TM1"], AF.Sigmoid, scale=1.702),
                    reads=[q + "gmin"], writes=[q + "sig"], tag=q + "glu")
                P.op("vector", lambda nc: nc.vector.tensor_mul(
                    c.t[u + "TM1"], c.t[u + "TM1"], c.t[u + "TM2"]),
                    reads=[q + "gmin", q + "sig"], writes=[q + "t1"], tag=q + "glu")
                P.op("vector", lambda nc: nc.vector.tensor_scalar(
                    c.t[u + "TM2"], P1[:, D:2 * D], 7.0, -7.0, ALU.min, ALU.max),
                    reads=[q + "p1t"], writes=[q + "lin"], tag=q + "glu")
                P.op("vector", lambda nc: nc.vector.scalar_tensor_tensor(
                    ACT4, c.t[u + "TM2"], 1.0, c.t[u + "TM1"], ALU.add, ALU.mult),
                    reads=[q + "lin", q + "t1"], writes=[q + "act"], tag=q + "glu")
                pairs = [(ACT4[:, cc * 128:(cc + 1) * 128],
                          A2T[:, ii * D + cc * 128: ii * D + (cc + 1) * 128])
                         for cc in range(ND)]
                emit_transpose_tiles(c, q + "act", q + "a2t", pairs)
            W2v = WSB.rearrange("p (cd f) -> p cd f", f=1024)
            for g2 in range(2):
                P.dma(lambda nc, e=e, g2=g2: nc.gpsimd.dma_start(
                    out=W2v[:, g2 * 4:(g2 + 1) * 4, :], in_=m2v[e, :, g2 * 4:(g2 + 1) * 4, :]),
                    writes=[q + f"Q{g2}"], queue="gpsimd", tag=q + "m2ld")
            P.dma(lambda nc, e=e: nc.gpsimd.dma_start(
                out=B2R, in_=c.t["L2_ffn_mlp2_bias"][e][None, :]),
                writes=[q + "b2r"], queue="gpsimd", tag=q + "b2ld")
            for ii in range(4):
                i = i0_ + ii
                for gr in range(2):
                    ps = psf[2 + gr]; bank = psfk[2 + gr]
                    fns = []
                    for cd in range(ND):
                        fns.append(lambda nc, cd=cd, ii=ii, gr=gr, ps=ps: nc.tensor.matmul(
                            ps, A2T[:, ii * D + cd * 128: ii * D + (cd + 1) * 128],
                            W2v[:, cd, gr * 512:(gr + 1) * 512],
                            start=(cd == 0), stop=False))
                    fns.append(lambda nc, gr=gr, ps=ps: nc.tensor.matmul(
                        ps, c.t["ONE1"], B2R[0:1, gr * 512:(gr + 1) * 512],
                        start=False, stop=True))
                    P.matmul_group(fns, reads=[q + "a2t", q + "Q0", q + "Q1", q + "b2r"],
                                   writes=[bank], tag=q + "p2")
                    P.op("vector", lambda nc, ii=ii, gr=gr, ps=ps, i=i, e=e:
                         nc.vector.scalar_tensor_tensor(
                             ACC[:, ii * D + gr * 512: ii * D + gr * 512 + 512], ps,
                             CMB[:, i * 8 + e: i * 8 + e + 1],
                             ACC[:, ii * D + gr * 512: ii * D + gr * 512 + 512],
                             ALU.mult, ALU.add),
                         reads=[bank, q + "cmbf", q + "acc"],
                         writes=[q + "acc"], tag=q + "p2a")
        for ii in range(4):
            i = i0_ + ii
            P.dma(lambda nc, ii=ii, i=i: nc.sync.dma_start(
                out=scracc[i], in_=ACC[:, ii * D:(ii + 1) * D]),
                reads=[q + "acc"], writes=[q + f"Dacc{i}"], queue="sync",
                tag=q + "accsp")

    xv = c.t[c.xin_name].rearrange("(n p) d -> p n d", p=128)
    ov = c.t[c.xout_name].rearrange("(n p) d -> p n d", p=128)
    for i in range(NT):
        ib = i % 2
        xt = c.t[f"XT{ib}"]; xtk = f"XT{ib}"
        P.dma(lambda nc, i=i: nc.sync.dma_start(out=ACT4, in_=scrxc[i]),
              reads=[q + f"Dxc{i}"], writes=[q + "xcl"], queue="sync", tag=q + "xcld")
        P.dma(lambda nc, i=i: nc.sync.dma_start(out=c.t[u + "TM1"], in_=scracc[i]),
              reads=[q + f"Dacc{i}"], writes=[q + "accl"], queue="sync", tag=q + "accld")
        P.dma(lambda nc, i=i, xt=xt: nc.sync.dma_start(out=xt, in_=xv[:, i, :]),
              reads=[f"D{c.xin_name}_{i}"], writes=[xtk], queue="sync", tag=q + "xld")
        CB16 = c.t[u + "TM2"].bitcast(BF16)
        P.op("vector", lambda nc: nc.vector.tensor_copy(CB16[:, 0:D], c.t[u + "TM1"]),
             reads=[q + "accl"], writes=[q + "cb"], tag=q + "fin")
        P.op("vector", lambda nc: nc.vector.tensor_add(CB16[:, D:2 * D], ACT4, CB16[:, 0:D]),
             reads=[q + "cb", q + "xcl"], writes=[q + "sm"], tag=q + "fin")
        P.op("vector", lambda nc, xt=xt: nc.vector.tensor_add(
            xt, xt, CB16[:, D:2 * D]),
            reads=[q + "sm", xtk], writes=[xtk], tag=q + "fin")
        P.dma(lambda nc, i=i, xt=xt: nc.sync.dma_start(out=ov[:, i, :], in_=xt),
              reads=[xtk], writes=[f"D{c.xout_name}_{i}"], queue="sync", tag=q + "xst")


def stage_L3_attn_b(inputs):
    d = {}
    qkv_w = np.asarray(inputs["L3_attn_b_qkv_w"], np.float32)
    qkv_b = np.asarray(inputs["L3_attn_b_qkv_b"], np.float32)
    d["L3b_Wq"] = np.ascontiguousarray(qkv_w[:, 0:1024])
    d["L3b_Wk"] = np.ascontiguousarray(qkv_w[:, 1024:2048])
    d["L3b_Wv"] = np.ascontiguousarray(qkv_w[:, 2048:3072])
    d["L3b_bq"] = np.ascontiguousarray(qkv_b[0:1024])
    d["L3b_bk"] = np.ascontiguousarray(qkv_b[1024:2048])
    d["L3b_bv"] = np.ascontiguousarray(qkv_b[2048:3072])
    for nm in ("w1_write", "w2_write", "w1_read", "w2_read"):
        w = np.asarray(inputs[f"L3_attn_b_{nm}"], np.float32)
        d[f"L3b_{nm}"] = np.ascontiguousarray(np.pad(w, ((0, 0), (0, 64))))
    d["L3b_memv_w"] = np.ascontiguousarray(np.asarray(inputs["L3_attn_b_memv_w"], np.float32))
    d["L3b_memv_b"] = np.ascontiguousarray(np.asarray(inputs["L3_attn_b_memv_b"], np.float32))
    d["L3b_memg_w"] = np.ascontiguousarray(np.pad(
        np.asarray(inputs["L3_attn_b_memg_w"], np.float32), ((0, 0), (0, 112))))
    d["L3b_memg_b"] = np.ascontiguousarray(np.pad(
        np.asarray(inputs["L3_attn_b_memg_b"], np.float32), (0, 112)))
    d["L3b_out_w"] = np.ascontiguousarray(np.asarray(inputs["L3_attn_b_out_w"], np.float32))
    d["L3b_out_b"] = np.ascontiguousarray(np.asarray(inputs["L3_attn_b_out_b"], np.float32))
    d["L3b_J"] = np.ascontiguousarray(np.asarray(inputs["L3_attn_b_J"], np.float32).reshape(36))
    d["L3b_mem_scale"] = np.ascontiguousarray(np.asarray(inputs["L3_attn_b_mem_scale"], np.float32))
    d["L3b_decays"] = np.ascontiguousarray(np.asarray(inputs["L3_attn_b_decays"], np.float32))
    selh = np.zeros((640, 16), np.float32)
    for cc in range(576):
        selh[cc, cc // 36] = 1.0
    d["c_l3b_selh"] = selh
    dsel = np.zeros((16, 640), np.float32)
    for cc in range(576):
        dsel[cc // 36, cc] = 1.0
    d["c_l3b_dsel"] = dsel
    return d


def declare_L3_attn_b(c: Ctx):
    for nm, shp in [("Wq", [D, D]), ("Wk", [D, D]), ("Wv", [D, D]),
                    ("bq", [D]), ("bk", [D]), ("bv", [D]),
                    ("w1_write", [D, 128]), ("w2_write", [D, 128]),
                    ("w1_read", [D, 128]), ("w2_read", [D, 128]),
                    ("memv_w", [D, D]), ("memv_b", [D]),
                    ("memg_w", [D, 128]), ("memg_b", [128]),
                    ("out_w", [D, D]), ("out_b", [D]),
                    ("J", [36]), ("decays", [16]), ("mem_scale", [16])]:
        c.dram(f"L3b_{nm}", shp, F32, kind="ExternalInput")
    c.dram("c_l3b_selh", [640, 16], F32, kind="ExternalInput")
    c.dram("c_l3b_dsel", [16, 640], F32, kind="ExternalInput")


L3B_CUT = 4

def emit_L3_attn_b(c: Ctx, uid="a14"):
    u = uid
    P = c.P
    H, PD = 16, 4
    PAIRS = [(0, 1), (0, 2), (0, 3), (1, 2), (1, 3), (2, 3)]
    scrAO = c.dram(u + "scrAO", [NT, 128, D], BF16)
    psf = [c.t["PSA"], c.t["PSB"], c.t["PSC"], c.t["PSD"]]
    psfk = ["PSA", "PSB", "PSC", "PSD"]
    ID = c.t["IDENT"]

    c.open_scratch()
    c.sb(u + "QT", [128, ND * T], BF16, scratch=True)
    c.sb(u + "KT", [128, ND * T], BF16, scratch=True)
    c.sb(u + "VT", [128, ND * T], BF16, scratch=True)
    c.sb(u + "VTM", [128, NT * D], BF16, scratch=True)
    for nm in ("bq", "bk", "bv"):
        load_chunkvec_f32(c, f"L3b_{nm}", u + nm, D, scratch=True)
    load_weight_bf16(c, "L3b_Wq", u + "W", D, D, scratch=True)
    emit_proj(c, "XHT", u + "W", u + "QT", D, out_major="f",
              bias_tile=(c.t[u + "bq"], u + "bq"))
    _reload_w(c, "L3b_Wk", u + "W", D)
    emit_proj(c, "XHT", u + "W", u + "KT", D, out_major="f",
              bias_tile=(c.t[u + "bk"], u + "bk"))
    _reload_w(c, "L3b_Wv", u + "W", D)
    emit_proj(c, "XHT", u + "W", u + "VT", D, out_major="f",
              bias_tile=(c.t[u + "bv"], u + "bv"))
    emit_fmajor_to_tmajor(c, u + "VT", u + "VTM")
    emit_mha(c, u, u + "QT", u + "KT", u + "VTM", u + "VT", H=16, dk=64, dv=64,
             scale=1.0 / 8.0, causal=True)
    for i in range(NT):
        P.dma(lambda nc, i=i: nc.sync.dma_start(
            out=scrAO[i], in_=c.t[u + "VT"][:, i * D:(i + 1) * D]),
            reads=[u + "VT"], writes=[u + f"Dao{i}"], queue="sync", tag=u + "aosp")
    c.close_scratch()

    if L3B_CUT < 1:
        OUT0 = c.sb(u + "OUT0", [128, NT * D], BF16)
        for i in range(NT):
            P.dma(lambda nc, i=i: nc.sync.dma_start(
                out=OUT0[:, i * D:(i + 1) * D], in_=scrAO[i]),
                reads=[u + f"Dao{i}"], writes=[u + "OUT0"], queue="sync", tag=u)
        emit_residual_from_tmajor(c, u + "OUT0")
        return
    GMR = c.sb(u + "GMR", [128, T], BF16)
    c.open_scratch()
    R36T = c.sb(u + "R36T", [128, 5 * T], BF16)
    W36T = c.sb(u + "W36T", [128, 5 * T], BF16)
    c.open_scratch()
    SX = c.sb(u + "SX", [128, ND * T], BF16, scratch=True)
    for cd in range(ND):
        P.op("vector", lambda nc, cd=cd: nc.vector.tensor_copy(
            SX[:, cd * T + 1:(cd + 1) * T], c.t["XHT"][:, cd * T: cd * T + T - 1]),
            reads=["XHT"], writes=[u + "sx"], tag=u + "sx")
        P.op("vector", lambda nc, cd=cd: nc.vector.memset(SX[:, cd * T: cd * T + 1], 0.0),
             writes=[u + "sx"], tag=u + "sx")
    WSm = c.sb(u + "WSm", [128, ND * 128], BF16, scratch=True)
    c.t[u + "WSm_keys"] = [u + "wsm"]
    PRJ = c.sb(u + "PRJ", [128, 4 * NT * 128], BF16, scratch=True)
    for j, (wnm, srck) in enumerate([("w1_write", u + "sx"), ("w2_write", "XHT"),
                                     ("w1_read", "XHT"), ("w2_read", "XHT")]):
        wv = c.t[f"L3b_{wnm}"].rearrange("(cd p) f -> p cd f", p=128)
        dstw = WSm.rearrange("p (cd f) -> p cd f", f=128)
        P.dma(lambda nc, wv=wv, dstw=dstw: nc.gpsimd.dma_start(out=dstw, in_=wv),
              writes=[u + "wsm"], queue="gpsimd", tag=u + "wld")
        src_key = u + "sx" if j == 0 else "XHT"
        SRC = SX if j == 0 else c.t["XHT"]
        for i in range(NT):
            ps = psf[i % 2]; bank = psfk[i % 2]
            fns = []
            for cd in range(ND):
                fns.append(lambda nc, cd=cd, i=i, ps=ps, SRC=SRC, dstw=dstw: nc.tensor.matmul(
                    ps[:, 0:128], SRC[:, cd * T + i * 128: cd * T + (i + 1) * 128],
                    dstw[:, cd, :], start=(cd == 0), stop=(cd == ND - 1)))
            P.matmul_group(fns, reads=[src_key, u + "wsm"], writes=[bank], tag=u + "pp")
            P.op("scalar", lambda nc, i=i, j=j, ps=ps: nc.scalar.copy(
                PRJ[:, j * NT * 128 + i * 128:(j * NT * 128 + (i + 1) * 128)], ps[:, 0:128]),
                reads=[bank], writes=[u + f"prj{j}"], tag=u + "ppe")
    c.sb(u + "TM96", [128, 96], F32, scratch=True)
    WL = c.sb(u + "WL", [128, NT * 96], BF16, scratch=True)
    RL = c.sb(u + "RL", [128, NT * 96], BF16, scratch=True)
    TMW = c.sb(u + "TMW", [128, 96], F32, scratch=True)
    TMN = c.sb(u + "TMN", [128, 16], F32, scratch=True)
    for dst, dstk, j1, j2, rks in [(WL, u + "wl", 0, 1, [u + "prj0", u + "prj1"]),
                                   (RL, u + "rl", 2, 3, [u + "prj2", u + "prj3"])]:
        for i in range(NT):
            a = PRJ[:, j1 * NT * 128 + i * 128: j1 * NT * 128 + (i + 1) * 128]
            b = PRJ[:, j2 * NT * 128 + i * 128: j2 * NT * 128 + (i + 1) * 128]
            av = a.rearrange("p (h pd) -> p h pd", pd=8)
            bv = b.rearrange("p (h pd) -> p h pd", pd=8)
            tv = TMW.rearrange("p (h pr) -> p h pr", pr=6)
            for pi, (p1_, p2_) in enumerate(PAIRS):
                P.op("vector", lambda nc, pi=pi, p1_=p1_, p2_=p2_, av=av, bv=bv, tv=tv:
                     nc.vector.tensor_mul(tv[:, :, pi], av[:, :, p1_], bv[:, :, p2_]),
                     reads=rks, writes=[u + "tmw"], tag=u + "ext")
                P.op("vector", lambda nc, pi=pi, p1_=p1_, p2_=p2_, av=av, bv=bv:
                     nc.vector.tensor_mul(TMN[:, 0:16].rearrange("p h -> p h"),
                                          av[:, :, p2_], bv[:, :, p1_]),
                     reads=rks, writes=[u + "tmn"], tag=u + "ext")
                P.op("vector", lambda nc, pi=pi, tv=tv: nc.vector.tensor_sub(
                    tv[:, :, pi], tv[:, :, pi], TMN[:, 0:16]),
                    reads=[u + "tmw", u + "tmn"], writes=[u + "tmw"], tag=u + "ext")
            P.op("vector", lambda nc, tv=tv: nc.vector.tensor_mul(
                c.t[u + "TM96"], TMW, TMW), reads=[u + "tmw"], writes=[u + "t96"], tag=u + "ext")
            P.op("vector", lambda nc: nc.vector.reduce_sum(
                TMN[:, 0:16], c.t[u + "TM96"].rearrange("p (h pr) -> p h pr", pr=6), axis=AX.X),
                reads=[u + "t96"], writes=[u + "nrm"], tag=u + "ext")
            P.op("scalar", lambda nc: nc.scalar.activation(TMN[:, 0:16], TMN[:, 0:16], AF.Sqrt),
                 reads=[u + "nrm"], writes=[u + "nrm"], tag=u + "ext")
            P.op("vector", lambda nc: nc.vector.tensor_scalar_max(
                TMN[:, 0:16], TMN[:, 0:16], 1e-12),
                reads=[u + "nrm"], writes=[u + "nrm"], tag=u + "ext")
            P.op("vector", lambda nc: nc.vector.reciprocal(TMN[:, 0:16], TMN[:, 0:16]),
                 reads=[u + "nrm"], writes=[u + "nrm"], tag=u + "ext")
            dvv = dst[:, i * 96:(i + 1) * 96].rearrange("p (h pr) -> p h pr", pr=6)
            for pi in range(6):
                P.op("vector", lambda nc, pi=pi, dvv=dvv, tv=tv: nc.vector.tensor_mul(
                    dvv[:, :, pi], tv[:, :, pi], TMN[:, 0:16]),
                    reads=[u + "tmw", u + "nrm"], writes=[dstk], tag=u + "ext")
    jrow = load_row_f32(c, "L3b_J", u + "jrow", 36, scratch=True)
    emit_bcast_row(c, jrow, u + "jrow", u + "JB", 36, scratch=True)
    JW = c.sb(u + "JW", [128, NT * 96], BF16, scratch=True)
    JBv = c.t[u + "JB"]
    for i in range(NT):
        wv_ = WL[:, i * 96:(i + 1) * 96].rearrange("p (h pr) -> p h pr", pr=6)
        jv_ = JW[:, i * 96:(i + 1) * 96].rearrange("p (h pr) -> p h pr", pr=6)
        for qi in range(6):
            first = True
            for pi in range(6):
                if first:
                    P.op("vector", lambda nc, qi=qi, pi=pi, wv_=wv_, jv_=jv_:
                         nc.vector.tensor_scalar_mul(jv_[:, :, qi], wv_[:, :, pi],
                                                     JBv[:, 6 * pi + qi: 6 * pi + qi + 1]),
                         reads=[u + "wl", u + "JB"], writes=[u + "jw"], tag=u + "jw")
                    first = False
                else:
                    P.op("vector", lambda nc, qi=qi, pi=pi, wv_=wv_, jv_=jv_:
                         nc.vector.tensor_scalar(c.t[u + "TM96"][:, 0:16], wv_[:, :, pi],
                                                 JBv[:, 6 * pi + qi: 6 * pi + qi + 1],
                                                 None, ALU.mult),
                         reads=[u + "wl", u + "JB"], writes=[u + "jwtmp"], tag=u + "jw")
                    P.op("vector", lambda nc, qi=qi, jv_=jv_: nc.vector.tensor_add(
                        jv_[:, :, qi], jv_[:, :, qi], c.t[u + "TM96"][:, 0:16]),
                        reads=[u + "jwtmp", u + "jw"], writes=[u + "jw"], tag=u + "jw")
    R36 = c.sb(u + "R36", [128, NT * 576], BF16, scratch=True)
    W36 = c.sb(u + "W36", [128, NT * 576], BF16, scratch=True)
    for i in range(NT):
        rv = RL[:, i * 96:(i + 1) * 96].rearrange("p (h pr) -> p h pr", pr=6)
        jv_ = JW[:, i * 96:(i + 1) * 96].rearrange("p (h pr) -> p h pr", pr=6)
        r36v = R36[:, i * 576:(i + 1) * 576].rearrange("p (h pq) -> p h pq", pq=36)
        w36v = W36[:, i * 576:(i + 1) * 576].rearrange("p (h pq) -> p h pq", pq=36)
        for pp in range(6):
            for qq in range(6):
                P.op("vector", lambda nc, pp=pp, qq=qq, rv=rv, r36v=r36v:
                     nc.vector.tensor_mul(r36v[:, :, pp * 6 + qq], rv[:, :, pp], rv[:, :, qq]),
                     reads=[u + "rl"], writes=[u + "r36"], tag=u + "f36")
                P.op("vector", lambda nc, pp=pp, qq=qq, jv_=jv_, w36v=w36v:
                     nc.vector.tensor_mul(w36v[:, :, pp * 6 + qq], jv_[:, :, pp], jv_[:, :, qq]),
                     reads=[u + "jw"], writes=[u + "w36"], tag=u + "f36")
    P.op("vector", lambda nc: nc.vector.memset(R36T, 0.0), writes=[u + "r36t"], tag=u)
    P.op("vector", lambda nc: nc.vector.memset(W36T, 0.0), writes=[u + "w36t"], tag=u)
    for src, srck, dstt, dstk in [(R36, u + "r36", R36T, u + "r36t"),
                                  (W36, u + "w36", W36T, u + "w36t")]:
        pairs = []
        for i in range(NT):
            for cc in range(5):
                w0 = cc * 128
                w1_ = min(576, w0 + 128)
                if w0 >= 576:
                    break
                pairs.append((src[:, i * 576 + w0: i * 576 + w1_],
                              dstt[:, cc * T + i * 128: cc * T + (i + 1) * 128]))
        emit_transpose_tiles_rect(c, srck, dstk, pairs)
    c.close_scratch()
    if L3B_CUT < 2:
        c.close_scratch()
        OUT0 = c.sb(u + "OUT0", [128, NT * D], BF16)
        for i in range(NT):
            P.dma(lambda nc, i=i: nc.sync.dma_start(
                out=OUT0[:, i * D:(i + 1) * D], in_=scrAO[i]),
                reads=[u + f"Dao{i}"], writes=[u + "OUT0"], queue="sync", tag=u)
        emit_residual_from_tmajor(c, u + "OUT0")
        return

    c.open_scratch()
    G = c.sb(u + "G", [128, 5 * T], BF16, scratch=True)
    DR = c.sb(u + "DR", [128, 8], F32, scratch=True)
    DCN = c.sb(u + "DCN", [128, T], F32, scratch=True)
    SW = c.sb(u + "SW", [128, T], F32, scratch=True)
    ONEB = c.sb(u + "ONEB", [128, T], BF16, scratch=True)
    P.op("vector", lambda nc: nc.vector.memset(ONEB, 1.0), writes=[u + "oneb"], tag=u)
    DSEL = c.sb(u + "DSEL", [16, 640], F32, scratch=True)
    DCOL = c.sb(u + "DCOL", [16, 1], F32, scratch=True)
    P.dma(lambda nc: nc.sync.dma_start(out=DSEL, in_=c.t["c_l3b_dsel"]),
          writes=[u + "dsel"], queue="sync", tag=u)
    P.dma(lambda nc: nc.sync.dma_start(out=DCOL, in_=c.t["L3b_decays"][:, None]),
          writes=[u + "dcol"], queue="sync", tag=u)
    for k in range(5):
        P.matmul_group([lambda nc, k=k: nc.tensor.matmul(
            c.t["PSD"][:, 0:1], DSEL[:, k * 128:(k + 1) * 128], DCOL,
            start=True, stop=True)],
            reads=[u + "dsel", u + "dcol"], writes=["PSD"], tag=u + "dr")
        P.op("vector", lambda nc, k=k: nc.vector.tensor_copy(DR[:, k:k + 1], c.t["PSD"][:, 0:1]),
             reads=["PSD"], writes=[u + "dr"], tag=u + "dr")
    for k in range(5):
        P.op("vector", lambda nc, k=k: nc.vector.tensor_scalar_mul(
            DCN, ONEB, DR[:, k:k + 1]), reads=[u + "oneb", u + "dr"], writes=[u + "dcn"],
            tag=u + "g")
        P.op("vector", lambda nc, k=k: nc.vector.tensor_scalar_mul(
            SW[:, 1:T], W36T[:, k * T: k * T + T - 1], DR[:, k:k + 1]),
            reads=[u + "w36t", u + "dr"], writes=[u + "sw"], tag=u + "g")
        P.op("vector", lambda nc: nc.vector.memset(SW[:, 0:1], 0.0),
             writes=[u + "sw"], tag=u + "g")
        P.op("vector", lambda nc, k=k: nc.vector.tensor_tensor_scan(
            G[:, k * T:(k + 1) * T], DCN, SW, 0.0, ALU.mult, ALU.add),
            reads=[u + "dcn", u + "sw"], writes=[u + "gk"], tag=u + "g")
        P.op("vector", lambda nc, k=k: nc.vector.tensor_mul(
            R36T[:, k * T:(k + 1) * T], R36T[:, k * T:(k + 1) * T], G[:, k * T:(k + 1) * T]),
            reads=[u + "gk", u + "r36t"], writes=[u + "r36t"], tag=u + "g")
    if L3B_CUT < 2.7:
        c.close_scratch(); c.close_scratch()
        OUT0 = c.sb(u + "OUT0", [128, NT * D], BF16)
        for i in range(NT):
            P.dma(lambda nc, i=i: nc.sync.dma_start(
                out=OUT0[:, i * D:(i + 1) * D], in_=scrAO[i]),
                reads=[u + f"Dao{i}"], writes=[u + "OUT0"], queue="sync", tag=u)
        emit_residual_from_tmajor(c, u + "OUT0")
        return
    SELH = c.sb(u + "SELH", [128, 5 * 16], BF16, scratch=True)
    sv = c.t["c_l3b_selh"].rearrange("(k p) h -> p k h", p=128)
    P.dma(lambda nc: nc.gpsimd.dma_start(
        out=SELH.rearrange("p (k h) -> p k h", h=16), in_=sv),
        writes=[u + "selh"], queue="gpsimd", tag=u)
    MS = c.sb(u + "MS", [16, T], BF16, scratch=True)
    MSCL = c.sb(u + "MSCL", [16, 1], F32, scratch=True)
    P.dma(lambda nc: nc.sync.dma_start(out=MSCL, in_=c.t["L3b_mem_scale"][:, None]),
          writes=[u + "mscl"], queue="sync", tag=u)
    for tg in range(4):
        w0 = tg * 512
        ps = psf[tg % 2]; bank = psfk[tg % 2]
        fns = []
        for k in range(5):
            fns.append(lambda nc, k=k, w0=w0, ps=ps: nc.tensor.matmul(
                ps[0:16, :], SELH[:, k * 16:(k + 1) * 16],
                R36T[:, k * T + w0: k * T + w0 + 512],
                start=(k == 0), stop=(k == 4)))
        P.matmul_group(fns, reads=[u + "selh", u + "r36t"], writes=[bank], tag=u + "ms")
        P.op("scalar", lambda nc, w0=w0, ps=ps: nc.scalar.activation(
            MS[:, w0:w0 + 512], ps[0:16, :], AF.Sigmoid, scale=MSCL),
            reads=[bank, u + "mscl"], writes=[u + "ms"], tag=u + "ms")
    if L3B_CUT < 2.9:
        c.close_scratch(); c.close_scratch()
        OUT0 = c.sb(u + "OUT0", [128, NT * D], BF16)
        for i in range(NT):
            P.dma(lambda nc, i=i: nc.sync.dma_start(
                out=OUT0[:, i * D:(i + 1) * D], in_=scrAO[i]),
                reads=[u + f"Dao{i}"], writes=[u + "OUT0"], queue="sync", tag=u)
        emit_residual_from_tmajor(c, u + "OUT0")
        return
    GT = c.sb(u + "GT", [128, T], BF16, scratch=True)
    load_weight_bf16(c, "L3b_memg_w", u + "WG", D, 128, scratch=True)
    load_chunkvec_f32(c, "L3b_memg_b", u + "bg", 128, scratch=True)
    for tg in range(4):
        w0 = tg * 512
        ps = psf[2 + tg % 2]; bank = psfk[2 + tg % 2]
        Wv = c.t[u + "WG"].rearrange("p (cd f) -> p cd f", f=128)
        fns = []
        for cd in range(ND):
            fns.append(lambda nc, cd=cd, w0=w0, ps=ps, Wv=Wv: nc.tensor.matmul(
                ps, Wv[:, cd, :], c.t["XHT"][:, cd * T + w0: cd * T + w0 + 512],
                start=(cd == 0), stop=(cd == ND - 1)))
        P.matmul_group(fns, reads=["XHT"] + c.t[u + "WG_keys"], writes=[bank], tag=u + "gt")
        P.op("vector", lambda nc, w0=w0, ps=ps: nc.vector.tensor_scalar(
            c.t[u + "SW"][:, 0:512], ps, 1.0, c.t[u + "bg"], ALU.mult, ALU.add),
            reads=[bank, u + "bg"], writes=[u + "gt0"], tag=u + "gt")
        P.op("scalar", lambda nc, w0=w0: nc.scalar.activation(
            GT[:, w0:w0 + 512], c.t[u + "SW"][:, 0:512], AF.Sigmoid),
            reads=[u + "gt0"], writes=[u + "gtk"], tag=u + "gt")
    if L3B_CUT < 2.97:
        c.close_scratch(); c.close_scratch()
        OUT0 = c.sb(u + "OUT0", [128, NT * D], BF16)
        for i in range(NT):
            P.dma(lambda nc, i=i: nc.sync.dma_start(
                out=OUT0[:, i * D:(i + 1) * D], in_=scrAO[i]),
                reads=[u + f"Dao{i}"], writes=[u + "OUT0"], queue="sync", tag=u)
        emit_residual_from_tmajor(c, u + "OUT0")
        return
    GAT = c.sb(u + "GAT", [128, T], BF16, scratch=True)
    GR1 = c.sb(u + "GR1", [1, T], F32, scratch=True)
    P.op("vector", lambda nc: nc.vector.memset(GAT, 0.0), writes=[u + "gat"], tag=u + "gm")
    P.op("vector", lambda nc: nc.vector.tensor_mul(GAT[0:16, :], MS, GT[0:16, :]),
         reads=[u + "ms", u + "gtk", u + "gat"], writes=[u + "gat"], tag=u + "gm")
    for tg in range(4):
        w0 = tg * 512
        P.matmul_group([lambda nc, w0=w0: nc.tensor.matmul(
            c.t["PSD"][0:1, 0:512], c.t["ONESCOL"][:, 0:1], GAT[:, w0:w0 + 512],
            start=True, stop=True)],
            reads=[u + "gat", "ONESCOL"], writes=["PSD"], tag=u + "gm")
        P.op("vector", lambda nc, w0=w0: nc.vector.tensor_scalar_mul(
            GR1[0:1, w0:w0 + 512], c.t["PSD"][0:1, 0:512], 1.0 / 16.0),
            reads=["PSD"], writes=[u + "gmr0"], tag=u + "gm")
        P.matmul_group([lambda nc, w0=w0: nc.tensor.matmul(
            c.t["PSD"][:, 0:512], c.t["ONE1F"][0:1, :], GR1[0:1, w0:w0 + 512],
            start=True, stop=True)],
            reads=[u + "gmr0", "ONE1F"], writes=["PSD"], tag=u + "gm")
        P.op("vector", lambda nc, w0=w0: nc.vector.tensor_copy(
            GMR[:, w0:w0 + 512], c.t["PSD"][:, 0:512]),
            reads=["PSD"], writes=[u + "gmr"], tag=u + "gm")
    c.close_scratch()
    c.close_scratch()
    if L3B_CUT < 3:
        OUT0 = c.sb(u + "OUT0", [128, NT * D], BF16)
        for i in range(NT):
            P.dma(lambda nc, i=i: nc.sync.dma_start(
                out=OUT0[:, i * D:(i + 1) * D], in_=scrAO[i]),
                reads=[u + f"Dao{i}"], writes=[u + "OUT0"], queue="sync", tag=u)
        emit_residual_from_tmajor(c, u + "OUT0")
        return
    c.open_scratch()
    MV = c.sb(u + "MV", [128, ND * T], BF16, scratch=True)
    load_weight_bf16(c, "L3b_memv_w", u + "WM", D, D, scratch=True)
    load_chunkvec_f32(c, "L3b_memv_b", u + "bm", D, scratch=True)
    emit_proj(c, "XHT", u + "WM", u + "MV", D, out_major="f",
              bias_tile=(c.t[u + "bm"], u + "bm"))
    SOT = c.sb(u + "SOT", [128, ND * T], BF16, scratch=True)
    aoflat = scrAO.rearrange("i p d -> (i p) d")
    for cd in range(ND):
        P.dma(lambda nc, cd=cd: nc.sync.dma_start_transpose(
            out=SOT[:, cd * T:(cd + 1) * T], in_=aoflat[:, cd * 128:(cd + 1) * 128]),
            reads=[u + f"Dao{i2}" for i2 in range(NT)], writes=[u + "SOT"],
            queue="sync", tag=u + "sotld")
    for cd in range(ND):
        P.op("vector", lambda nc, cd=cd: nc.vector.tensor_mul(
            MV[:, cd * T:(cd + 1) * T], MV[:, cd * T:(cd + 1) * T], GMR),
            reads=[u + "gmr", u + "MV"], writes=[u + "MV"], tag=u + "mf")
        P.op("vector", lambda nc, cd=cd: nc.vector.tensor_add(
            SOT[:, cd * T:(cd + 1) * T], SOT[:, cd * T:(cd + 1) * T],
            MV[:, cd * T:(cd + 1) * T]),
            reads=[u + "MV", u + "SOT"], writes=[u + "SOT"], tag=u + "mf")
    load_weight_bf16(c, "L3b_out_w", u + "WO", D, D, scratch=True)
    OB = load_row_f32(c, "L3b_out_b", u + "ob_row", D, scratch=True)
    emit_bcast_row(c, OB, u + "ob_row", u + "OBR", D, scratch=True)
    OUTM = c.sb(u + "OUTM", [128, NT * D], BF16, scratch=True)
    emit_proj(c, u + "SOT", u + "WO", u + "OUTM", D, out_major="t",
              addrow_tile=(c.t[u + "OBR"], u + "OBR"))
    emit_residual_from_tmajor(c, u + "OUTM")
    c.close_scratch()


def emit_transpose_tiles_rect(c, src_key, dst_key, pairs, psk=("PST", "PST2")):
    P = c.P
    ID = c.t["IDENT"]
    for g in range(0, len(pairs), 2):
        batch = pairs[g:g + 2]
        which = (g // 2) % 2
        pst = c.t[psk[which]]
        pk = psk[0] + f"_{which}"
        for j, (sap, _dap) in enumerate(batch):
            w = sap.shape[-1]
            P.op("tensor", lambda nc, sap=sap, j=j, pst=pst, w=w: nc.tensor.transpose(
                pst[0:w, j * 128:j * 128 + 128], sap, ID),
                 reads=[src_key, "IDENT"], writes=[pk], tag="trr")
        for j, (sap, dap) in enumerate(batch):
            w = sap.shape[-1]
            P.op("vector", lambda nc, dap=dap, j=j, pst=pst, w=w: nc.vector.tensor_copy(
                dap[0:w, :], pst[0:w, j * 128:j * 128 + 128]),
                 reads=[pk], writes=[dst_key], tag="trrc")


def stage_L1_ffn(inputs):
    d = {}
    for k in ("routing_token", "light_gamma", "light_b1", "light_b2",
              "heavy_gamma", "heavy_b1", "heavy_b2"):
        d[f"L1_ffn_{k}"] = np.ascontiguousarray(np.asarray(inputs[f"L1_ffn_{k}"], np.float32))
    for k in ("light_w1", "light_w2", "heavy_w1", "heavy_w2"):
        d[f"L1_ffn_{k}"] = np.ascontiguousarray(np.asarray(inputs[f"L1_ffn_{k}"], np.float32))
    return d


def declare_L1_ffn(c: Ctx):
    for k, shp in [("routing_token", [D]), ("light_gamma", [D]), ("light_b1", [512]),
                   ("light_b2", [D]), ("heavy_gamma", [D]), ("heavy_b1", [4096]),
                   ("heavy_b2", [D]), ("light_w1", [D, 512]), ("light_w2", [512, D]),
                   ("heavy_w1", [D, 4096]), ("heavy_w2", [4096, D])]:
        c.dram(f"L1_ffn_{k}", shp, F32, kind="ExternalInput")


def emit_L1_ffn(c: Ctx, uid="f5"):
    u = uid
    P = c.P
    call = sum(1 for k2 in c.t if k2.startswith(u + "call"))
    c.t[u + f"call{call}"] = None
    q = f"{u}_{call}_"
    psf = [c.t["PSA"], c.t["PSB"], c.t["PSC"], c.t["PSD"]]
    psfk = ["PSA", "PSB", "PSC", "PSD"]
    ID = c.t["IDENTF"]
    SQ = c.t["SQ"]
    if (u + "RN") not in c.t:
        c.sb(u + "RN", [128, NT])
        c.sb(u + "SELM", [128, NT])
        c.sb(u + "LOUT", [128, NT * D], BF16)
        c.sb(u + "XCF", [128, NT * D], BF16)
    RN = c.t[u + "RN"]; SELM = c.t[u + "SELM"]; LOUT = c.t[u + "LOUT"]
    XCF = c.t[u + "XCF"]
    scrh = c.dram(q + "scrh", [NT, 128, 4096], BF16)

    emit_fmajor_to_tmajor(c, "XHT", u + "XCF")
    for i in range(NT):
        xc = XCF[:, i * D:(i + 1) * D]
        P.op("scalar", lambda nc, xc=xc, i=i: nc.scalar.activation(
            SQ, xc, AF.Square, accum_out=RN[:, i:i + 1]),
            reads=[u + "XCF"], writes=["SQ", q + f"rn{i}"], tag=q + "n")
        P.op("scalar", lambda nc, i=i: nc.scalar.activation(
            RN[:, i:i + 1], RN[:, i:i + 1], AF.Sqrt),
            reads=[q + f"rn{i}"], writes=[q + f"rn{i}"], tag=q + "n")
        P.op("vector", lambda nc, i=i: nc.vector.tensor_scalar_max(
            RN[:, i:i + 1], RN[:, i:i + 1], 1e-12),
            reads=[q + f"rn{i}"], writes=[q + f"rn{i}"], tag=q + "n")
        P.op("vector", lambda nc, i=i: nc.vector.reciprocal(
            RN[:, i:i + 1], RN[:, i:i + 1]),
            reads=[q + f"rn{i}"], writes=[q + f"rn{i}"], tag=q + "n")
        P.op("vector", lambda nc, i=i: nc.vector.tensor_scalar_mul(
            RN[:, i:i + 1], RN[:, i:i + 1], float(np.sqrt(D))),
            reads=[q + f"rn{i}"], writes=[q + f"rn{i}"], tag=q + "n")
    rnk = [q + f"rn{i}" for i in range(NT)]

    c.open_scratch()
    RTF = load_chunkvec_f32(c, "L1_ffn_routing_token", q + "rtf", D, scratch=True)
    RT = c.sb(q + "rt", [128, ND], BF16, scratch=True)
    c.t[q + "rt"] = RT
    P.op("vector", lambda nc: nc.vector.tensor_copy(RT, RTF),
         reads=[q + "rtf"], writes=[q + "rt"], tag=q)
    STOK = c.sb(q + "STOK", [128, NT], F32, scratch=True)
    for i in range(NT):
        ps = psf[i % 2]; bank = psfk[i % 2]
        fns = []
        for cd in range(ND):
            fns.append(lambda nc, cd=cd, i=i, ps=ps: nc.tensor.matmul(
                ps[:, 0:1], c.t["XHT"][:, cd * T + i * 128: cd * T + (i + 1) * 128],
                c.t[q + "rt"][:, cd:cd + 1], start=(cd == 0), stop=(cd == ND - 1)))
        P.matmul_group(fns, reads=["XHT", q + "rt"], writes=[bank], tag=q + "s")
        P.op("vector", lambda nc, i=i, ps=ps: nc.vector.tensor_copy(
            STOK[:, i:i + 1], ps[:, 0:1]), reads=[bank], writes=[q + "stok"], tag=q + "s")
    CD = c.sb(q + "CD", [128, NT], F32, scratch=True)
    AB = c.sb(q + "AB", [128, 8], F32, scratch=True)
    SC1 = c.sb(q + "SC1", [1, 128], F32, scratch=True)
    B0 = c.sb(q + "B0", [128, NT], F32, scratch=True)
    logk = float(np.log(min(1024 * 9.0 / 8.0, T)))
    P.op("vector", lambda nc: nc.vector.tensor_scalar_mul(B0, STOK, -1.0),
         reads=[q + "stok"], writes=[q + "b"], tag=q + "cd")
    PSD = c.t["PSD"]
    for it in range(50):
        P.op("vector", lambda nc: nc.vector.tensor_add(CD, STOK, B0),
             reads=[q + "stok", q + "b"], writes=[q + "sb"], tag=q + "cd")
        P.op("vector", lambda nc: nc.vector.reduce_max(AB[:, 0:1], CD, axis=AX.X),
             reads=[q + "sb"], writes=[q + "m0"], tag=q + "cd")
        P.op("tensor", lambda nc: nc.tensor.transpose(PSD[0:1, 0:128], AB[:, 0:1], ID),
             reads=[q + "m0", "IDENTF"], writes=["PSD"], tag=q + "cd")
        P.op("vector", lambda nc: nc.vector.reduce_max(SC1[0:1, 0:1], PSD[0:1, 0:128], axis=AX.X),
             reads=["PSD"], writes=[q + "m1"], tag=q + "cd")
        P.matmul_group([lambda nc: nc.tensor.matmul(
            PSD[:, 4:5], c.t["ONE1F"][0:1, :], SC1[0:1, 0:1], start=True, stop=True)],
            reads=[q + "m1", "ONE1F"], writes=["PSDm"], tag=q + "cd")
        P.op("vector", lambda nc: nc.vector.tensor_copy(AB[:, 1:2], PSD[:, 4:5]),
             reads=["PSDm"], writes=[q + "mb"], tag=q + "cd")
        P.op("vector", lambda nc: nc.vector.tensor_scalar_mul(AB[:, 2:3], AB[:, 1:2], -1.0),
             reads=[q + "mb"], writes=[q + "mn"], tag=q + "cd")
        P.op("scalar", lambda nc: nc.scalar.activation(
            CD, CD, AF.Exp, bias=AB[:, 2:3], accum_out=AB[:, 3:4]),
            reads=[q + "sb", q + "mn"], writes=[q + "ex", q + "exa"], tag=q + "cd")
        P.op("tensor", lambda nc: nc.tensor.transpose(PSD[0:1, 0:128], AB[:, 3:4], ID),
             reads=[q + "exa", "IDENTF"], writes=["PSD"], tag=q + "cd")
        P.op("vector", lambda nc: nc.vector.reduce_sum(SC1[0:1, 1:2], PSD[0:1, 0:128], axis=AX.X),
             reads=["PSD"], writes=[q + "ses"], tag=q + "cd")
        P.op("scalar", lambda nc: nc.scalar.activation(SC1[0:1, 1:2], SC1[0:1, 1:2], AF.Ln),
             reads=[q + "ses"], writes=[q + "lns"], tag=q + "cd")
        P.op("vector", lambda nc: nc.vector.tensor_add(SC1[0:1, 2:3], SC1[0:1, 1:2], SC1[0:1, 0:1]),
             reads=[q + "lns", q + "m1"], writes=[q + "lse"], tag=q + "cd")
        P.op("vector", lambda nc: nc.vector.tensor_scalar(
            SC1[0:1, 3:4], SC1[0:1, 2:3], -1.0, logk, ALU.mult, ALU.add),
            reads=[q + "lse"], writes=[q + "a"], tag=q + "cd")
        P.matmul_group([lambda nc: nc.tensor.matmul(
            PSD[:, 4:5], c.t["ONE1F"][0:1, :], SC1[0:1, 3:4], start=True, stop=True)],
            reads=[q + "a", "ONE1F"], writes=["PSDm"], tag=q + "cd")
        P.op("vector", lambda nc: nc.vector.tensor_copy(AB[:, 4:5], PSD[:, 4:5]),
             reads=["PSDm"], writes=[q + "ab"], tag=q + "cd")
        P.op("vector", lambda nc: nc.vector.tensor_scalar_add(CD, STOK, AB[:, 4:5]),
             reads=[q + "stok", q + "ab"], writes=[q + "sa"], tag=q + "cd")
        P.op("vector", lambda nc: nc.vector.tensor_relu(B0, CD),
             reads=[q + "sa"], writes=[q + "b"], tag=q + "cd")
        P.op("vector", lambda nc: nc.vector.tensor_scalar_mul(B0, B0, -1.0),
             reads=[q + "b"], writes=[q + "b"], tag=q + "cd")
    P.op("vector", lambda nc: nc.vector.tensor_scalar_add(CD, STOK, AB[:, 4:5]),
         reads=[q + "stok", q + "ab"], writes=[q + "t"], tag=q + "sel")
    P.op("vector", lambda nc: nc.vector.tensor_scalar_min(CD, CD, 0.0),
         reads=[q + "t"], writes=[q + "t"], tag=q + "sel")
    S2 = c.sb(q + "S2", [16, 128], F32, scratch=True)
    Z2 = c.sb(q + "Z2", [16, 128], F32, scratch=True)
    CZ = c.sb(q + "CZ", [16, 128], F32, scratch=True)
    SB2 = c.sb(q + "SB2", [16, 128], F32, scratch=True)
    ONE2 = c.sb(q + "ONE2", [16, 128], BF16, scratch=True)
    P.op("vector", lambda nc: nc.vector.memset(ONE2, 1.0), writes=[q + "one2"], tag=q)
    P.op("tensor", lambda nc: nc.tensor.transpose(PSD[0:16, 0:128], CD, ID),
         reads=[q + "t", "IDENTF"], writes=["PSD"], tag=q + "sel")
    P.op("vector", lambda nc: nc.vector.tensor_copy(S2, PSD[0:16, 0:128]),
         reads=["PSD"], writes=[q + "s2"], tag=q + "sel")
    P.op("vector", lambda nc: nc.vector.tensor_single_scalar(Z2, S2, 0.0, ALU.is_ge),
         reads=[q + "s2"], writes=[q + "z2"], tag=q + "sel")
    P.op("vector", lambda nc: nc.vector.tensor_tensor_scan(CZ, ONE2, Z2, 0.0, ALU.mult, ALU.add),
         reads=[q + "z2", q + "one2"], writes=[q + "cz"], tag=q + "sel")
    P.op("tensor", lambda nc: nc.tensor.transpose(PSD[0:1, 0:16], CZ[:, 127:128], ID[0:16, 0:16]),
         reads=[q + "cz", "IDENTF"], writes=["PSD"], tag=q + "sel")
    P.op("vector", lambda nc: nc.vector.tensor_copy(SC1[0:1, 8:24], PSD[0:1, 0:16]),
         reads=["PSD"], writes=[q + "rowt"], tag=q + "sel")
    P.op("vector", lambda nc: nc.vector.memset(SC1[0:1, 32:33], 0.0),
         writes=[q + "pz"], tag=q + "sel")
    P.op("vector", lambda nc: nc.vector.tensor_tensor_scan(
        SC1[0:1, 33:48], ONE2[0:1, 0:15], SC1[0:1, 8:23], 0.0, ALU.mult, ALU.add),
        reads=[q + "rowt", q + "one2"], writes=[q + "pz2"], tag=q + "sel")
    P.op("tensor", lambda nc: nc.tensor.transpose(PSD[0:16, 0:1], SC1[0:1, 32:48], ID[0:1, 0:1]),
         reads=[q + "pz", q + "pz2", "IDENTF"], writes=["PSD"], tag=q + "sel")
    P.op("vector", lambda nc: nc.vector.tensor_copy(SB2[:, 0:1], PSD[0:16, 0:1]),
         reads=["PSD"], writes=[q + "pref"], tag=q + "sel")
    P.op("vector", lambda nc: nc.vector.tensor_scalar_add(CZ, CZ, SB2[:, 0:1]),
         reads=[q + "cz", q + "pref"], writes=[q + "cz"], tag=q + "sel")
    P.op("vector", lambda nc: nc.vector.reduce_sum(SB2[:, 6:7], Z2, axis=AX.X),
         reads=[q + "z2"], writes=[q + "zr"], tag=q + "sel")
    P.op("tensor", lambda nc: nc.tensor.transpose(PSD[0:1, 0:16], SB2[:, 6:7], ID[0:16, 0:16]),
         reads=[q + "zr", "IDENTF"], writes=["PSD"], tag=q + "sel")
    P.op("vector", lambda nc: nc.vector.reduce_sum(SC1[0:1, 50:51], PSD[0:1, 0:16], axis=AX.X),
         reads=["PSD"], writes=[q + "mcnt"], tag=q + "sel")
    P.op("vector", lambda nc: nc.vector.tensor_single_scalar(SB2, CZ, 1024.5, ALU.is_lt),
         reads=[q + "cz"], writes=[q + "le"], tag=q + "sel")
    P.op("vector", lambda nc: nc.vector.tensor_mul(SB2, SB2, Z2),
         reads=[q + "le", q + "z2"], writes=[q + "selA"], tag=q + "sel")
    TH = SC1[0:1, 51:52]; LO = SC1[0:1, 52:53]; HI = SC1[0:1, 53:54]
    NEG = c.sb(q + "NEG", [16, 128], F32, scratch=True)
    CNT = c.sb(q + "CNT", [16, 128], F32, scratch=True)
    P.op("vector", lambda nc: nc.vector.tensor_single_scalar(NEG, S2, 0.0, ALU.is_lt),
         reads=[q + "s2"], writes=[q + "neg"], tag=q + "bi")
    P.op("vector", lambda nc: nc.vector.memset(LO, -200.0), writes=[q + "lo"], tag=q + "bi")
    P.op("vector", lambda nc: nc.vector.memset(HI, 0.0), writes=[q + "hi"], tag=q + "bi")
    P.op("vector", lambda nc: nc.vector.tensor_scalar(
        SC1[0:1, 54:55], SC1[0:1, 50:51], -1.0, 1024.0, ALU.mult, ALU.add),
        reads=[q + "mcnt"], writes=[q + "tgt"], tag=q + "bi")
    for bit in range(30):
        P.op("vector", lambda nc: nc.vector.tensor_add(TH, LO, HI),
             reads=[q + "lo", q + "hi"], writes=[q + "th"], tag=q + "bi")
        P.op("vector", lambda nc: nc.vector.tensor_scalar_mul(TH, TH, 0.5),
             reads=[q + "th"], writes=[q + "th"], tag=q + "bi")
        P.matmul_group([lambda nc: nc.tensor.matmul(
            PSD[0:16, 5:6], c.t["ONE1F"][0:1, 0:16], TH, start=True, stop=True)],
            reads=[q + "th", "ONE1F"], writes=["PSDt"], tag=q + "bi")
        P.op("vector", lambda nc: nc.vector.tensor_copy(SB2[:, 2:3], PSD[0:16, 5:6]),
             reads=["PSDt"], writes=[q + "thb"], tag=q + "bi")
        P.op("vector", lambda nc: nc.vector.tensor_scalar(
            CNT, S2, SB2[:, 2:3], None, ALU.is_ge),
            reads=[q + "s2", q + "thb"], writes=[q + "cnt"], tag=q + "bi")
        P.op("vector", lambda nc: nc.vector.tensor_mul(CNT, CNT, NEG),
             reads=[q + "cnt", q + "neg"], writes=[q + "cnt"], tag=q + "bi")
        P.op("vector", lambda nc: nc.vector.reduce_sum(SB2[:, 3:4], CNT, axis=AX.X),
             reads=[q + "cnt"], writes=[q + "cnr"], tag=q + "bi")
        P.op("tensor", lambda nc: nc.tensor.transpose(PSD[0:1, 0:16], SB2[:, 3:4], ID[0:16, 0:16]),
             reads=[q + "cnr", "IDENTF"], writes=["PSD"], tag=q + "bi")
        P.op("vector", lambda nc: nc.vector.reduce_sum(SC1[0:1, 55:56], PSD[0:1, 0:16], axis=AX.X),
             reads=["PSD"], writes=[q + "cns"], tag=q + "bi")
        P.op("vector", lambda nc: nc.vector.tensor_tensor(
            SC1[0:1, 56:57], SC1[0:1, 55:56], SC1[0:1, 54:55], ALU.is_gt),
            reads=[q + "cns", q + "tgt"], writes=[q + "gtm"], tag=q + "bi")
        P.op("vector", lambda nc: nc.vector.tensor_sub(SC1[0:1, 59:60], TH, LO),
             reads=[q + "th", q + "lo"], writes=[q + "dlo"], tag=q + "bi")
        P.op("vector", lambda nc: nc.vector.scalar_tensor_tensor(
            LO, SC1[0:1, 59:60], SC1[0:1, 56:57], LO, ALU.mult, ALU.add),
            reads=[q + "dlo", q + "gtm", q + "lo"], writes=[q + "lo"], tag=q + "bi")
        P.op("vector", lambda nc: nc.vector.tensor_scalar(
            SC1[0:1, 57:58], SC1[0:1, 56:57], -1.0, 1.0, ALU.mult, ALU.add),
            reads=[q + "gtm"], writes=[q + "ngm"], tag=q + "bi")
        P.op("vector", lambda nc: nc.vector.tensor_sub(SC1[0:1, 60:61], TH, HI),
             reads=[q + "th", q + "hi"], writes=[q + "dhi"], tag=q + "bi")
        P.op("vector", lambda nc: nc.vector.scalar_tensor_tensor(
            HI, SC1[0:1, 60:61], SC1[0:1, 57:58], HI, ALU.mult, ALU.add),
            reads=[q + "dhi", q + "ngm", q + "hi"], writes=[q + "hi"], tag=q + "bi")
    P.matmul_group([lambda nc: nc.tensor.matmul(
        PSD[0:16, 5:6], c.t["ONE1F"][0:1, 0:16], HI, start=True, stop=True)],
        reads=[q + "hi", "ONE1F"], writes=["PSDt"], tag=q + "bi")
    P.op("vector", lambda nc: nc.vector.tensor_copy(SB2[:, 2:3], PSD[0:16, 5:6]),
         reads=["PSDt"], writes=[q + "thb"], tag=q + "bi")
    P.op("vector", lambda nc: nc.vector.tensor_scalar(
        CNT, S2, SB2[:, 2:3], None, ALU.is_ge),
        reads=[q + "s2", q + "thb"], writes=[q + "cnt2"], tag=q + "bi")
    P.op("vector", lambda nc: nc.vector.tensor_mul(CNT, CNT, NEG),
         reads=[q + "cnt2", q + "neg"], writes=[q + "cnt2"], tag=q + "bi")
    P.op("vector", lambda nc: nc.vector.tensor_add(CNT, CNT, Z2),
         reads=[q + "cnt2", q + "z2"], writes=[q + "selB"], tag=q + "bi")
    P.op("vector", lambda nc: nc.vector.tensor_single_scalar(
        SC1[0:1, 58:59], SC1[0:1, 50:51], 1023.5, ALU.is_gt),
        reads=[q + "mcnt"], writes=[q + "mf"], tag=q + "sel")
    P.matmul_group([lambda nc: nc.tensor.matmul(
        PSD[0:16, 5:6], c.t["ONE1F"][0:1, 0:16], SC1[0:1, 58:59], start=True, stop=True)],
        reads=[q + "mf", "ONE1F"], writes=["PSDt"], tag=q + "sel")
    P.op("vector", lambda nc: nc.vector.tensor_copy(SB2[:, 4:5], PSD[0:16, 5:6]),
         reads=["PSDt"], writes=[q + "mfb"], tag=q + "sel")
    P.op("vector", lambda nc: nc.vector.tensor_scalar_mul(SB2, SB2, SB2[:, 4:5]),
         reads=[q + "selA", q + "mfb"], writes=[q + "selA2"], tag=q + "sel")
    P.op("vector", lambda nc: nc.vector.tensor_scalar(
        SB2[:, 5:6], SB2[:, 4:5], -1.0, 1.0, ALU.mult, ALU.add),
        reads=[q + "mfb"], writes=[q + "nmfb"], tag=q + "sel")
    P.op("vector", lambda nc: nc.vector.tensor_scalar_mul(CNT, CNT, SB2[:, 5:6]),
         reads=[q + "selB", q + "nmfb"], writes=[q + "selB2"], tag=q + "sel")
    P.op("vector", lambda nc: nc.vector.tensor_add(CNT, CNT, SB2),
         reads=[q + "selA2", q + "selB2"], writes=[q + "self"], tag=q + "sel")
    P.op("tensor", lambda nc: nc.tensor.transpose(PSD[0:128, 0:16], CNT, ID[0:16, 0:16]),
         reads=[q + "self", "IDENTF"], writes=["PSD"], tag=q + "sel")
    P.op("vector", lambda nc: nc.vector.tensor_copy(SELM, PSD[0:128, 0:16]),
         reads=["PSD"], writes=[q + "selm"], tag=q + "sel")
    c.close_scratch()

    def _branch(branch, gamma_nm, w1nm, b1nm, w2nm, b2nm, dh):
        c.open_scratch()
        b = q + branch
        NRM = c.sb(b + "NRM", [128, ND * T], BF16, scratch=True)
        GR = load_row_f32(c, f"L1_ffn_{gamma_nm}", b + "g_row", D, scratch=True)
        emit_bcast_row(c, GR, b + "g_row", b + "GRb", D, scratch=True)
        TMN = c.sb(b + "TMN", [128, D], F32, scratch=True)
        NTM = c.sb(b + "NTM", [128, D], BF16, scratch=True)
        for i in range(NT):
            P.op("vector", lambda nc, i=i: nc.vector.tensor_scalar_mul(
                TMN, XCF[:, i * D:(i + 1) * D], RN[:, i:i + 1]),
                reads=[u + "XCF"] + rnk, writes=[b + "t1"], tag=b + "nr")
            P.op("vector", lambda nc: nc.vector.tensor_mul(NTM, TMN, c.t[b + "GRb"]),
                 reads=[b + "t1", b + "GRb"], writes=[b + "ntm"], tag=b + "nr")
            pairs = [(NTM[:, cc * 128:(cc + 1) * 128],
                      NRM[:, cc * T + i * 128: cc * T + (i + 1) * 128]) for cc in range(ND)]
            emit_transpose_tiles(c, b + "ntm", b + "NRM", pairs)
        whalf = dh if branch == "L" else 2048
        W1 = c.sb(b + "W1", [128, ND * whalf], BF16, scratch=True)
        w1v = c.t[f"L1_ffn_{w1nm}"].rearrange("(cd p) f -> p cd f", p=128)
        W1v = W1.rearrange("p (cd f) -> p cd f", f=whalf)
        B1 = load_row_bf16(c, f"L1_ffn_{b1nm}", b + "b1_row", dh, scratch=True)
        ACTT = c.sb(b + "ACTT", [128, whalf], BF16, scratch=True)
        AT2 = c.sb(b + "AT2", [128, whalf], BF16, scratch=True)
        w1keys = [b + "w1k" + str(g2) for g2 in range(4)]
        nhalf = dh // whalf
        ngr = whalf // 512
        for hh in range(nhalf):
            f0 = hh * whalf
            for g2 in range(4):
                P.dma(lambda nc, g2=g2, f0=f0: nc.gpsimd.dma_start(
                    out=W1v[:, g2 * 2:(g2 + 1) * 2, :],
                    in_=w1v[:, g2 * 2:(g2 + 1) * 2, f0:f0 + whalf]),
                    writes=[b + "w1k" + str(g2)], queue="gpsimd", tag=b + "w1ld")
            for i in range(NT):
                for gr in range(ngr):
                    ps = psf[gr % 4]; bank = psfk[gr % 4]
                    fns = []
                    for cd in range(ND):
                        fns.append(lambda nc, cd=cd, i=i, gr=gr, ps=ps: nc.tensor.matmul(
                            ps, NRM[:, cd * T + i * 128: cd * T + (i + 1) * 128],
                            W1v[:, cd, gr * 512:(gr + 1) * 512],
                            start=(cd == 0), stop=False))
                    fns.append(lambda nc, gr=gr, ps=ps, f0=f0: nc.tensor.matmul(
                        ps, c.t["ONE1"], B1[0:1, f0 + gr * 512: f0 + (gr + 1) * 512],
                        start=False, stop=True))
                    P.matmul_group(fns, reads=[b + "NRM", b + "b1_row"] + w1keys,
                                   writes=[bank], tag=b + "p1")
                    P.op("scalar", lambda nc, gr=gr, ps=ps: nc.scalar.activation(
                        ACTT[:, gr * 512:(gr + 1) * 512], ps, AF.Gelu),
                        reads=[bank], writes=[b + "actt"], tag=b + "p1g")
                pairs = [(ACTT[:, cc * 128:(cc + 1) * 128], AT2[:, cc * 128:(cc + 1) * 128])
                         for cc in range(whalf // 128)]
                emit_transpose_tiles(c, b + "actt", b + "at2", pairs)
                if branch == "H":
                    P.dma(lambda nc, i=i, hh=hh: nc.sync.dma_start(
                        out=scrh.rearrange("i p (h w) -> i p h w", h=2)[i, :, hh, :],
                        in_=AT2),
                        reads=[b + "at2"], writes=[b + f"Dh{i}_{hh}"], queue="sync",
                        tag=b + "sp")
                    continue
                if i == 0:
                    W2 = c.sb(b + "W2", [128, 4 * D], BF16, scratch=True)
                    w2v = c.t[f"L1_ffn_{w2nm}"].rearrange("(cd p) f -> p cd f", p=128)
                    W2v = W2.rearrange("p (cd f) -> p cd f", f=D)
                    P.dma(lambda nc: nc.gpsimd.dma_start(out=W2v, in_=w2v),
                          writes=[b + "w2k"], queue="gpsimd", tag=b + "w2ld")
                    B2 = load_row_f32(c, f"L1_ffn_{b2nm}", b + "b2_row", D, scratch=True)
                    emit_bcast_row(c, B2, b + "b2_row", b + "B2b", D, scratch=True)
                for gr2 in range(2):
                    ps = psf[2 + gr2]; bank = psfk[2 + gr2]
                    fns = []
                    for cd in range(4):
                        fns.append(lambda nc, cd=cd, i=i, gr2=gr2, ps=ps: nc.tensor.matmul(
                            ps, AT2[:, cd * 128:(cd + 1) * 128],
                            W2v[:, cd, gr2 * 512:(gr2 + 1) * 512],
                            start=(cd == 0), stop=(cd == 3)))
                    P.matmul_group(fns, reads=[b + "at2", b + "w2k"], writes=[bank],
                                   tag=b + "p2")
                    P.op("vector", lambda nc, i=i, gr2=gr2, ps=ps: nc.vector.scalar_tensor_tensor(
                        LOUT[:, i * D + gr2 * 512: i * D + gr2 * 512 + 512], ps, 1.0,
                        c.t[b + "B2b"][:, gr2 * 512:(gr2 + 1) * 512], ALU.mult, ALU.add),
                        reads=[bank, b + "B2b"], writes=[u + "LOUT"], tag=b + "p2e")
        c.close_scratch()

    _branch("L", "light_gamma", "light_w1", "light_b1", "light_w2", "light_b2", 512)
    _branch("H", "heavy_gamma", "heavy_w1", "heavy_b1", "heavy_w2", "heavy_b2", 4096)

    c.open_scratch()
    WH2 = c.sb(q + "WH2", [128, 32 * D], BF16, scratch=True)
    wh2v = c.t["L1_ffn_heavy_w2"].rearrange("(cd p) f -> p cd f", p=128)
    WH2v = WH2.rearrange("p (cd f) -> p cd f", f=D)
    for g2 in range(4):
        P.dma(lambda nc, g2=g2: nc.gpsimd.dma_start(
            out=WH2v[:, g2 * 8:(g2 + 1) * 8, :], in_=wh2v[:, g2 * 8:(g2 + 1) * 8, :]),
            writes=[q + "wh2" + str(g2)], queue="gpsimd", tag=q + "wh2ld")
    BH2 = load_row_f32(c, "L1_ffn_heavy_b2", q + "bh2_row", D, scratch=True)
    emit_bcast_row(c, BH2, q + "bh2_row", q + "BH2b", D, scratch=True)
    HL = [c.sb(q + f"HL{j}", [128, 4096], BF16, scratch=True) for j in range(2)]
    TMO = c.sb(q + "TMO", [128, 512], F32, scratch=True)
    wh2keys = [q + "wh2" + str(g2) for g2 in range(4)]
    xv = c.t[c.xin_name].rearrange("(n p) d -> p n d", p=128)
    ov = c.t[c.xout_name].rearrange("(n p) d -> p n d", p=128)
    for i in range(NT):
        ib = i % 2
        hl = HL[ib]; hlk = q + f"HL{ib}"
        xt = c.t[f"XT{ib}"]; xtk = f"XT{ib}"
        P.dma(lambda nc, hl=hl, i=i: nc.sync.dma_start(out=hl, in_=scrh[i]),
              reads=[q + f"HDh{i}_0", q + f"HDh{i}_1"], writes=[hlk],
              queue="sync", tag=q + "hld")
        P.dma(lambda nc, i=i, xt=xt: nc.sync.dma_start(out=xt, in_=xv[:, i, :]),
              reads=[f"D{c.xin_name}_{i}"], writes=[xtk], queue="sync", tag=q + "xld")
        for gr2 in range(2):
            ps = psf[(i * 2 + gr2) % 4]; bank = psfk[(i * 2 + gr2) % 4]
            fns = []
            for cd in range(32):
                fns.append(lambda nc, cd=cd, gr2=gr2, ps=ps, hl=hl: nc.tensor.matmul(
                    ps, hl[:, cd * 128:(cd + 1) * 128],
                    WH2v[:, cd, gr2 * 512:(gr2 + 1) * 512],
                    start=(cd == 0), stop=(cd == 31)))
            P.matmul_group(fns, reads=[hlk] + wh2keys, writes=[bank], tag=q + "hp2")
            P.op("vector", lambda nc, gr2=gr2, ps=ps: nc.vector.scalar_tensor_tensor(
                TMO, ps, 1.0, c.t[q + "BH2b"][:, gr2 * 512:(gr2 + 1) * 512],
                ALU.mult, ALU.add),
                reads=[bank, q + "BH2b"], writes=[q + "ho"], tag=q + "hoe")
            P.op("vector", lambda nc, i=i: nc.vector.tensor_scalar_mul(
                TMO, TMO, SELM[:, i:i + 1]),
                reads=[q + "ho", q + "selm"], writes=[q + "ho2"], tag=q + "hoe")
            P.op("vector", lambda nc, i=i, gr2=gr2: nc.vector.tensor_add(
                TMO, TMO, LOUT[:, i * D + gr2 * 512: i * D + gr2 * 512 + 512]),
                reads=[q + "ho2", u + "LOUT"], writes=[q + "ho3"], tag=q + "hoe")
            P.op("vector", lambda nc, xt=xt, gr2=gr2: nc.vector.tensor_add(
                xt[:, gr2 * 512:(gr2 + 1) * 512], xt[:, gr2 * 512:(gr2 + 1) * 512], TMO),
                reads=[q + "ho3", xtk], writes=[xtk], tag=q + "hoe")
        P.dma(lambda nc, i=i, xt=xt: nc.sync.dma_start(out=ov[:, i, :], in_=xt),
              reads=[xtk], writes=[f"D{c.xout_name}_{i}"], queue="sync", tag=q + "xst")
    c.close_scratch()


def stage_L2_attn_b(inputs):
    d = {}
    w = np.asarray(inputs["L2_attn_b_w_qkv"], np.float32)
    d["L2b_Wq"] = np.ascontiguousarray(w[0:1024].T)
    d["L2b_Wk"] = np.ascontiguousarray(w[1024:2048].T)
    d["L2b_Wv"] = np.ascontiguousarray(w[2048:3072].T)
    d["L2b_Wo"] = np.ascontiguousarray(np.asarray(inputs["L2_attn_b_w_out"], np.float32).T)
    sg = np.asarray(inputs["L2_attn_b_sorf_signs"], np.float32)
    d["L2b_signs"] = np.ascontiguousarray(sg.transpose(3, 0, 1, 2).reshape(64, 192))
    sc = np.asarray(inputs["L2_attn_b_sorf_scales"], np.float32)
    d["L2b_scales"] = np.ascontiguousarray(sc.transpose(2, 0, 1).reshape(64, 64))
    h = np.array([[1.0]])
    while h.shape[0] < 64:
        h = np.block([[h, h], [h, -h]])
    d["c_h64"] = np.ascontiguousarray(h.astype(np.float32))
    DH = 64
    freqs = 10000.0 ** (-np.arange(0, DH, 2, dtype=np.float64) / DH)
    ang = np.zeros((T, 32))
    ang[1:] = np.arange(T - 1)[:, None] * freqs[None, :]
    cosr = np.tile(np.cos(ang), (1, 16)).astype(np.float32)
    sinr = np.tile(np.sin(ang), (1, 16)).astype(np.float32)
    sinr[0] = 0.0; cosr[0] = 1.0
    d["c_ropec"] = np.ascontiguousarray(cosr)
    d["c_ropes"] = np.ascontiguousarray(sinr)
    return d


def declare_L2_attn_b(c: Ctx):
    for nm in ("Wq", "Wk", "Wv", "Wo"):
        c.dram(f"L2b_{nm}", [D, D], F32, kind="ExternalInput")
    c.dram("L2b_signs", [64, 192], F32, kind="ExternalInput")
    c.dram("L2b_scales", [64, 64], F32, kind="ExternalInput")
    c.dram("c_h64", [64, 64], F32, kind="ExternalInput")
    c.dram("c_ropec", [T, 512], F32, kind="ExternalInput")
    c.dram("c_ropes", [T, 512], F32, kind="ExternalInput")


def emit_L2_attn_b(c: Ctx, uid="a10"):
    u = uid
    P = c.P
    psf = [c.t["PSA"], c.t["PSB"], c.t["PSC"], c.t["PSD"]]
    psfk = ["PSA", "PSB", "PSC", "PSD"]
    ID = c.t["IDENT"]
    scrq = c.dram(u + "scrq", [NT, 128, D], BF16)
    scrk = c.dram(u + "scrk", [NT, 128, D], BF16)
    scrv = c.dram(u + "scrv", [NT, 128, D], BF16)
    XN2Q = c.sb(u + "XN2Q", [128, 16 * NT], F32)
    XN2K = c.sb(u + "XN2K", [128, 16 * NT], F32)
    c.open_scratch()
    QTM = c.sb(u + "QTM", [128, NT * D], BF16, scratch=True)
    KTM = c.sb(u + "KTM", [128, NT * D], BF16, scratch=True)
    load_weight_bf16(c, "L2b_Wq", u + "W", D, D, scratch=True)
    emit_proj(c, "XHT", u + "W", u + "QTM", D, out_major="t")
    _reload_w(c, "L2b_Wk", u + "W", D)
    emit_proj(c, "XHT", u + "W", u + "KTM", D, out_major="t")
    _reload_w(c, "L2b_Wv", u + "W", D)
    VS = c.sb(u + "VS", [128, D], BF16, scratch=True)
    Wv_ = c.t[u + "W"].rearrange("p (cd f) -> p cd f", f=D)
    for i in range(NT):
        for gr in range(2):
            ps = psf[gr]; bank = psfk[gr]
            fns = []
            for cd in range(ND):
                fns.append(lambda nc, cd=cd, i=i, gr=gr, ps=ps: nc.tensor.matmul(
                    ps, c.t["XHT"][:, cd * T + i * 128: cd * T + (i + 1) * 128],
                    Wv_[:, cd, gr * 512:(gr + 1) * 512],
                    start=(cd == 0), stop=(cd == ND - 1)))
            P.matmul_group(fns, reads=["XHT"] + c.t[u + "W_keys"], writes=[bank], tag=u + "v")
            P.op("scalar", lambda nc, gr=gr, ps=ps: nc.scalar.copy(
                VS[:, gr * 512:(gr + 1) * 512], ps),
                reads=[bank], writes=[u + "vs"], tag=u + "v")
        P.dma(lambda nc, i=i: nc.sync.dma_start(out=scrv[i], in_=VS),
              reads=[u + "vs"], writes=[u + f"Dv{i}"], queue="sync", tag=u + "vsp")
    CR = c.sb(u + "CR", [128, NT * 512], BF16, scratch=True)
    SR = c.sb(u + "SR", [128, NT * 512], BF16, scratch=True)
    RE = c.sb(u + "RE", [128, 512], F32, scratch=True)
    IM = c.sb(u + "IM", [128, 512], F32, scratch=True)
    T2 = c.sb(u + "T2", [128, 512], F32, scratch=True)
    for nmc, dst in [("c_ropec", CR), ("c_ropes", SR)]:
        src = c.t[nmc].rearrange("(n p) d -> p n d", p=128)
        P.dma(lambda nc, src=src, dst=dst: nc.gpsimd.dma_start(
            out=dst.rearrange("p (n d) -> p n d", d=512), in_=src),
            writes=[u + nmc], queue="gpsimd", tag=u)
    for tm, tmk in [(QTM, u + "QTM"), (KTM, u + "KTM")]:
        for i in range(NT):
            tv = tm[:, i * D:(i + 1) * D].rearrange("p (a two) -> p a two", two=2)
            cv = CR[:, i * 512:(i + 1) * 512]
            sv = SR[:, i * 512:(i + 1) * 512]
            qe = tv[:, :, 0]
            qo = tv[:, :, 1]
            P.op("vector", lambda nc, qe=qe, cv=cv: nc.vector.tensor_mul(RE, qe, cv),
                 reads=[tmk, u + "c_ropec"], writes=[u + "re"], tag=u + "rp")
            P.op("vector", lambda nc, qo=qo, sv=sv: nc.vector.tensor_mul(T2, qo, sv),
                 reads=[tmk, u + "c_ropes"], writes=[u + "t2"], tag=u + "rp")
            P.op("vector", lambda nc: nc.vector.tensor_sub(RE, RE, T2),
                 reads=[u + "re", u + "t2"], writes=[u + "re"], tag=u + "rp")
            P.op("vector", lambda nc, qe=qe, sv=sv: nc.vector.tensor_mul(IM, qe, sv),
                 reads=[tmk, u + "c_ropes"], writes=[u + "im"], tag=u + "rp")
            P.op("vector", lambda nc, qo=qo, cv=cv: nc.vector.tensor_mul(T2, qo, cv),
                 reads=[tmk, u + "c_ropec"], writes=[u + "t2"], tag=u + "rp")
            P.op("vector", lambda nc: nc.vector.tensor_add(IM, IM, T2),
                 reads=[u + "im", u + "t2"], writes=[u + "im"], tag=u + "rp")
            P.op("vector", lambda nc, qe=qe: nc.vector.tensor_copy(qe, RE),
                 reads=[u + "re"], writes=[tmk], tag=u + "rp")
            P.op("vector", lambda nc, qo=qo: nc.vector.tensor_copy(qo, IM),
                 reads=[u + "im"], writes=[tmk], tag=u + "rp")
    for i in range(NT):
        P.dma(lambda nc, i=i: nc.sync.dma_start(out=scrq[i], in_=QTM[:, i * D:(i + 1) * D]),
              reads=[u + "QTM"], writes=[u + f"Dq{i}"], queue="sync", tag=u + "qsp")
        P.dma(lambda nc, i=i: nc.sync.dma_start(out=scrk[i], in_=KTM[:, i * D:(i + 1) * D]),
              reads=[u + "KTM"], writes=[u + f"Dk{i}"], queue="sync", tag=u + "ksp")
    SQH2a = c.sb(u + "SQH2a", [128, 64], BF16, scratch=True)
    for tm, tmk, xn, xnk in [(QTM, u + "QTM", XN2Q, u + "xn2q"),
                             (KTM, u + "KTM", XN2K, u + "xn2k")]:
        for h in range(16):
            for i in range(NT):
                P.op("scalar", lambda nc, i=i, h=h, tm=tm, xn=xn: nc.scalar.activation(
                    SQH2a, tm[:, i * D + h * 64: i * D + (h + 1) * 64], AF.Square,
                    accum_out=xn[:, h * NT + i: h * NT + i + 1]),
                    reads=[tmk], writes=[u + "sqh2a", xnk], tag=u + "xn")
        P.op("vector", lambda nc, xn=xn: nc.vector.tensor_scalar(
            xn, xn, -0.5, -2.772588722239781, ALU.mult, ALU.add),
             reads=[xnk], writes=[xnk], tag=u + "xn")
    c.close_scratch()

    MG = c.sb(u + "MG", [128, 32 * 64], BF16)
    c.open_scratch()
    H64 = c.sb(u + "H64", [64, 64], BF16, scratch=True)
    SGN = c.sb(u + "SGN", [64, 192], F32, scratch=True)
    SCL = c.sb(u + "SCL", [64, 64], F32, scratch=True)
    B1 = c.sb(u + "B1", [64, 64], BF16, scratch=True)
    B2 = c.sb(u + "B2", [64, 64], BF16, scratch=True)
    B3 = c.sb(u + "B3", [64, 64], BF16, scratch=True)
    B1T = c.sb(u + "B1T", [64, 64], BF16, scratch=True)
    C1 = c.sb(u + "C1", [64, 64], BF16, scratch=True)
    MTG = c.sb(u + "MTG", [64, 64], BF16, scratch=True)
    P.dma(lambda nc: nc.gpsimd.dma_start(out=H64, in_=c.t["c_h64"]),
          writes=[u + "h64"], queue="gpsimd", tag=u)
    P.dma(lambda nc: nc.sync.dma_start(out=SGN, in_=c.t["L2b_signs"]),
          writes=[u + "sgn"], queue="sync", tag=u)
    P.dma(lambda nc: nc.sync.dma_start(out=SCL, in_=c.t["L2b_scales"]),
          writes=[u + "scl"], queue="sync", tag=u)
    PST = c.t["PST"]
    for h in range(16):
        for r in range(4):
            hr = h * 4 + r
            col0 = (h * 4 + r) * 3
            for bj, B in enumerate([B1, B2, B3]):
                P.op("vector", lambda nc, B=B, col=col0 + bj: nc.vector.tensor_scalar_mul(
                    B, H64, SGN[:, col:col + 1]),
                    reads=[u + "h64", u + "sgn"], writes=[u + f"B{bj+1}k"], tag=u + "mb")
            P.op("tensor", lambda nc: nc.tensor.transpose(PST[0:64, 0:64], B1, ID[0:64, 0:64]),
                 reads=[u + "B1k", "IDENT"], writes=["PST_0"], tag=u + "mb")
            P.op("vector", lambda nc: nc.vector.tensor_copy(B1T, PST[0:64, 0:64]),
                 reads=["PST_0"], writes=[u + "b1t"], tag=u + "mb")
            PSAm = c.t["PSA"]
            P.op("tensor", lambda nc, PSAm=PSAm: nc.tensor.matmul(PSAm[0:64, 0:64], B2, B1T,
                                                                  start=True, stop=True),
                 reads=[u + "B2k", u + "b1t"], writes=["PSA"], tag=u + "mb")
            P.op("vector", lambda nc, PSAm=PSAm: nc.vector.tensor_copy(C1, PSAm[0:64, 0:64]),
                 reads=["PSA"], writes=[u + "c1"], tag=u + "mb")
            P.op("tensor", lambda nc, PSAm=PSAm: nc.tensor.matmul(PSAm[0:64, 0:64], B3, C1,
                                                                  start=True, stop=True),
                 reads=[u + "B3k", u + "c1"], writes=["PSA"], tag=u + "mb")
            P.op("vector", lambda nc, h=h, r=r, PSAm=PSAm: nc.vector.tensor_scalar_mul(
                MTG, PSAm[0:64, 0:64], SCL[:, h * 4 + r: h * 4 + r + 1]),
                reads=["PSA", u + "scl"], writes=[u + "mtg"], tag=u + "mb")
            P.op("vector", lambda nc: nc.vector.tensor_scalar_mul(MTG, MTG, float(2.0 ** -12)),
                 reads=[u + "mtg"], writes=[u + "mtg"], tag=u + "mb")
            P.op("tensor", lambda nc: nc.tensor.transpose(PST[0:64, 64:128], MTG, ID[0:64, 0:64]),
                 reads=[u + "mtg", "IDENT"], writes=["PST_1b"], tag=u + "mb")
            base = (h % 2) * 64
            off = (h // 2) * 256 + r * 64
            P.op("vector", lambda nc, base=base, off=off: nc.vector.tensor_copy(
                MG[base:base + 64, off:off + 64], PST[0:64, 64:128]),
                reads=["PST_1b"], writes=[u + "mg"], tag=u + "mb")
    c.close_scratch()

    AO = c.t["XHT"]
    c.open_scratch()
    QRT = c.sb(u + "QRT", [128, ND * T], BF16, scratch=True)
    KRT = c.sb(u + "KRT", [128, ND * T], BF16, scratch=True)
    VL = c.sb(u + "VL", [128, NT * D], BF16, scratch=True)
    scrqf = scrq.rearrange("i p d -> (i p) d")
    scrkf = scrk.rearrange("i p d -> (i p) d")
    for cd in range(ND):
        P.dma(lambda nc, cd=cd: nc.sync.dma_start_transpose(
            out=QRT[:, cd * T:(cd + 1) * T], in_=scrqf[:, cd * 128:(cd + 1) * 128]),
            reads=[u + f"Dq{i2}" for i2 in range(NT)], writes=[u + "QRT"],
            queue="sync", tag=u + "qtr")
        P.dma(lambda nc, cd=cd: nc.sync.dma_start_transpose(
            out=KRT[:, cd * T:(cd + 1) * T], in_=scrkf[:, cd * 128:(cd + 1) * 128]),
            reads=[u + f"Dk{i2}" for i2 in range(NT)], writes=[u + "KRT"],
            queue="sync", tag=u + "ktr")
    for i in range(NT):
        P.dma(lambda nc, i=i: nc.sync.dma_start(out=VL[:, i * D:(i + 1) * D], in_=scrv[i]),
              reads=[u + f"Dv{i}"], writes=[u + "VL"], queue="sync", tag=u)
    QPH = c.sb(u + "QPH", [128, NT * 256], BF16, scratch=True)
    KPH = c.sb(u + "KPH", [128, NT * 256], BF16, scratch=True)
    QPT = c.sb(u + "QPT", [128, NT * 256], BF16, scratch=True)
    VE = c.sb(u + "VE", [128, 65], BF16, scratch=True)
    SPR = c.sb(u + "SPR", [128, 2 * 65], BF16, scratch=True)
    OTB = c.sb(u + "OTB", [128, 65], F32, scratch=True)
    MGv = MG
    P.op("vector", lambda nc: nc.vector.memset(VE[:, 64:65], 1.0), writes=[u + "veones"], tag=u)
    for h in range(16):
        hq = f"{u}h{h}_"
        hb = (h % 2) * 64
        hc = (h // 2)
        for phsrc, phsrck, phdst, phdstk, xn, xnk in [
                (QRT, u + "QRT", QPH, u + "QPH", XN2Q, u + "xn2q"),
                (KRT, u + "KRT", KPH, u + "KPH", XN2K, u + "xn2k")]:
            for i in range(NT):
                ps = psf[i % 2]; bank = psfk[i % 2]
                fns = []
                for r in range(4):
                    off = hc * 256 + r * 64
                    fns.append(lambda nc, i=i, hb=hb, hc=hc, off=off, r=r, ps=ps,
                               phsrc=phsrc: nc.tensor.matmul(
                        ps[:, r * 64:(r + 1) * 64],
                        phsrc[hb:hb + 64, hc * T + i * 128: hc * T + i * 128 + 128],
                        MGv[hb:hb + 64, off:off + 64],
                        start=True, stop=True))
                P.matmul_group(fns, reads=[phsrck, u + "mg"], writes=[bank],
                               tag=u + "ph")
                P.op("scalar", lambda nc, i=i, h=h, ps=ps, phdst=phdst, xn=xn:
                     nc.scalar.activation(
                         phdst[:, i * 256:(i + 1) * 256],
                         ps[:, 0:256], AF.Exp, bias=xn[:, h * NT + i: h * NT + i + 1]),
                     reads=[bank, xnk], writes=[phdstk], tag=u + "phe")
        pairs = []
        for i in range(NT):
            for mc in range(2):
                pairs.append((QPH[:, i * 256 + mc * 128: i * 256 + (mc + 1) * 128],
                              QPT[:, i * 256 + mc * 128: i * 256 + (mc + 1) * 128]))
        emit_transpose_tiles(c, u + "QPH", u + "QPT", pairs)
        for mc in range(2):
            ps = psf[mc]; bank = psfk[mc]
            for i in range(NT):
                P.op("vector", lambda nc, i=i, h=h: nc.vector.tensor_copy(
                    VE[:, 0:64], VL[:, i * D + h * 64: i * D + (h + 1) * 64]),
                    reads=[u + "VL", u + "veones"], writes=[u + "ve"], tag=u + "s")
                P.op("tensor", lambda nc, i=i, mc=mc, ps=ps: nc.tensor.matmul(
                    ps[:, 0:65], KPH[:, i * 256 + mc * 128: i * 256 + (mc + 1) * 128],
                    VE, start=(i == 0), stop=(i == NT - 1)),
                    reads=[u + "KPH", u + "ve"] + ([bank] if i == 0 else []),
                    writes=[bank] if i == NT - 1 else [bank + "_t"], tag=u + "s")
            P.op("vector", lambda nc, mc=mc, ps=ps: nc.vector.tensor_copy(
                SPR[:, mc * 65:(mc + 1) * 65], ps[:, 0:65]),
                reads=[bank, bank + "_t"], writes=[u + "spr"], tag=u + "s")
        for i in range(NT):
            ps = psf[2 + i % 2]; bank = psfk[2 + i % 2]
            fns = []
            for mc in range(2):
                fns.append(lambda nc, i=i, mc=mc, ps=ps: nc.tensor.matmul(
                    ps[:, 0:65], QPT[:, i * 256 + mc * 128: i * 256 + (mc + 1) * 128],
                    SPR[:, mc * 65:(mc + 1) * 65], start=(mc == 0), stop=(mc == 1)))
            P.matmul_group(fns, reads=[u + "QPT", u + "spr"], writes=[bank], tag=u + "nd")
            P.op("vector", lambda nc, i=i, ps=ps: nc.vector.tensor_copy(
                OTB, ps[:, 0:65]), reads=[bank], writes=[u + "otb"], tag=u + "nd")
            P.op("vector", lambda nc: nc.vector.tensor_scalar_max(
                OTB[:, 64:65], OTB[:, 64:65], 1e-6),
                reads=[u + "otb"], writes=[u + "otbm"], tag=u + "nd")
            P.op("vector", lambda nc: nc.vector.reciprocal(OTB[:, 64:65], OTB[:, 64:65]),
                 reads=[u + "otbm"], writes=[u + "otbr"], tag=u + "nd")
            P.op("vector", lambda nc, i=i, h=h: nc.vector.tensor_scalar_mul(
                AO[:, i * D + h * 64: i * D + (h + 1) * 64], OTB[:, 0:64], OTB[:, 64:65]),
                reads=[u + "otbr", u + "otb"], writes=["XHT"], tag=u + "nd")
    c.close_scratch()
    c.open_scratch()
    AOT = c.sb(u + "AOT", [128, ND * T], BF16, scratch=True)
    emit_tmajor_to_fmajor(c, "XHT", u + "AOT")
    load_weight_bf16(c, "L2b_Wo", u + "WO", D, D, scratch=True)
    OUTM = c.sb(u + "OUTM", [128, NT * D], BF16, scratch=True)
    emit_proj(c, u + "AOT", u + "WO", u + "OUTM", D, out_major="t")
    emit_residual_from_tmajor(c, u + "OUTM")
    c.close_scratch()


def stage_L3_attn_a(inputs):
    d = {}
    for nm, key in (("Wq", "L3a_Wq"), ("Wk", "L3a_Wk"), ("Wv", "L3a_Wv")):
        d[key] = np.ascontiguousarray(np.asarray(inputs[f"L3_attn_a_{nm}"], np.float32))
    d["L3a_Wb"] = np.ascontiguousarray(np.asarray(inputs["L3_attn_a_Wb"], np.float32))
    for nm, key in (("conv_q", "L3a_convq"), ("conv_k", "L3a_convk"), ("conv_v", "L3a_convv")):
        d[key] = np.ascontiguousarray(np.asarray(inputs[f"L3_attn_a_{nm}"], np.float32))
    d["L3a_firs"] = np.ascontiguousarray(
        np.asarray(inputs["L3_attn_a_fir_short_w"], np.float32).reshape(1024, 3))
    d["L3a_firl"] = np.ascontiguousarray(
        np.asarray(inputs["L3_attn_a_fir_long_w"], np.float32).reshape(1024, 63))
    d["L3a_forget"] = np.ascontiguousarray(np.asarray(inputs["L3_attn_a_forget_param"], np.float32))
    d["L3a_logtemp"] = np.ascontiguousarray(np.asarray(inputs["L3_attn_a_log_temp"], np.float32))
    d["L3a_floor"] = np.ascontiguousarray(
        np.asarray(inputs["L3_attn_a_floor_param"], np.float32).reshape(16))
    w1 = np.asarray(inputs["L3_attn_a_W1"], np.float32)
    d["L3a_W1h"] = np.ascontiguousarray(w1[0:1024])
    d["L3a_W1s"] = np.ascontiguousarray(w1[1024:1088])
    d["L3a_b1"] = np.ascontiguousarray(np.asarray(inputs["L3_attn_a_b1"], np.float32))
    d["L3a_W2"] = np.ascontiguousarray(np.asarray(inputs["L3_attn_a_W2"], np.float32))
    d["L3a_b2"] = np.ascontiguousarray(np.asarray(inputs["L3_attn_a_b2"], np.float32))
    d["L3a_normw"] = np.ascontiguousarray(np.asarray(inputs["L3_attn_a_norm_w"], np.float32))
    d["L3a_Wo"] = np.ascontiguousarray(np.asarray(inputs["L3_attn_a_Wo"], np.float32))
    oh = np.zeros((32, 32, 128), np.float32)
    oh[np.arange(32), np.arange(32), :] = 1.0
    d["c_oh32"] = np.ascontiguousarray(oh.reshape(32, 32 * 128))
    return d


def declare_L3_attn_a(c: Ctx):
    for nm in ("Wq", "Wk", "Wv", "Wo"):
        c.dram(f"L3a_{nm}", [D, D], F32, kind="ExternalInput")
    c.dram("L3a_Wb", [D, 4], F32, kind="ExternalInput")
    for nm in ("convq", "convk", "convv"):
        c.dram(f"L3a_{nm}", [D, 4], F32, kind="ExternalInput")
    c.dram("L3a_firs", [D, 3], F32, kind="ExternalInput")
    c.dram("L3a_firl", [D, 63], F32, kind="ExternalInput")
    c.dram("L3a_forget", [4], F32, kind="ExternalInput")
    c.dram("L3a_logtemp", [4], F32, kind="ExternalInput")
    c.dram("L3a_floor", [16], F32, kind="ExternalInput")
    c.dram("L3a_W1h", [D, 2048], F32, kind="ExternalInput")
    c.dram("L3a_W1s", [64, 2048], F32, kind="ExternalInput")
    c.dram("L3a_b1", [2048], F32, kind="ExternalInput")
    c.dram("L3a_W2", [2048, 16], F32, kind="ExternalInput")
    c.dram("L3a_b2", [16], F32, kind="ExternalInput")
    c.dram("L3a_normw", [256], F32, kind="ExternalInput")
    if "c_oh32" not in c.t:
        c.dram("c_oh32", [32, 32 * 128], F32, kind="ExternalInput")


def _l3a_dwconv(c, u, X, Xkey, FW, FWkey, K, PADT, ACC, dst=None, dstkey=None,
                act=None, ncc=ND, tag="cv", ACC2=None):
    P = c.P
    if dst is None:
        dst = X
        dstkey = Xkey
    split = False
    kd = K
    for cc in range(ncc):
        P.op("vector", lambda nc, K=K: nc.vector.memset(PADT[:, 0:K - 1], 0.0),
             writes=[u + "padt"], tag=u + tag)
        P.op("gpsimd", lambda nc, cc=cc, K=K: nc.gpsimd.tensor_copy(
            PADT[:, K - 1:T + K - 1], X[:, cc * T:(cc + 1) * T]),
            reads=[Xkey], writes=[u + "padt"], tag=u + tag)
        P.op("vector", lambda nc, cc=cc, K=K: nc.vector.tensor_scalar_mul(
            ACC, PADT[:, 0:T], FW[:, cc * K:cc * K + 1]),
            reads=[u + "padt", FWkey], writes=[u + "acc"], tag=u + tag)
        for j in range(1, kd):
            P.op("vector", lambda nc, cc=cc, j=j, K=K: nc.vector.scalar_tensor_tensor(
                ACC, PADT[:, j:T + j], FW[:, cc * K + j:cc * K + j + 1], ACC,
                ALU.mult, ALU.add),
                reads=[u + "padt", FWkey, u + "acc"], writes=[u + "acc"], tag=u + tag)
        if split:
            P.op("gpsimd", lambda nc, cc=cc, K=K, kd=kd: nc.gpsimd.tensor_scalar_mul(
                ACC2, PADT[:, kd:T + kd], FW[:, cc * K + kd:cc * K + kd + 1]),
                reads=[u + "padt", FWkey], writes=[u + "acc2"], tag=u + tag)
            for j in range(kd + 1, K):
                P.op("gpsimd", lambda nc, cc=cc, j=j, K=K:
                     nc.gpsimd.scalar_tensor_tensor(
                         ACC2, PADT[:, j:T + j], FW[:, cc * K + j:cc * K + j + 1],
                         ACC2, ALU.mult, ALU.add),
                     reads=[u + "padt", FWkey, u + "acc2"], writes=[u + "acc2"],
                     tag=u + tag)
            P.op("vector", lambda nc: nc.vector.tensor_add(ACC, ACC, ACC2),
                 reads=[u + "acc", u + "acc2"], writes=[u + "acc"], tag=u + tag)
        if act is not None:
            P.op("scalar", lambda nc, cc=cc, act=act: nc.scalar.activation(
                dst[:, cc * T:(cc + 1) * T], ACC, act),
                reads=[u + "acc"], writes=[dstkey], tag=u + tag)
        else:
            P.op("scalar", lambda nc, cc=cc: nc.scalar.copy(
                dst[:, cc * T:(cc + 1) * T], ACC),
                reads=[u + "acc"], writes=[dstkey], tag=u + tag)


def emit_L3_attn_a(c: Ctx, uid="s12"):
    u = uid
    P = c.P
    H = 4
    CH = 32
    NCH = T // CH
    psf = [c.t["PSA"], c.t["PSB"], c.t["PSC"], c.t["PSD"]]
    psfk = ["PSA", "PSB", "PSC", "PSD"]
    PSA, PSB, PSC, PSD = psf
    PST = c.t["PST"]
    ID = c.t["IDENT"]
    IDF = c.t["IDENTF"]
    scrOD = c.dram(u + "scrOD", [NCH, CH, D], BF16)
    scrVBT = c.dram(u + "scrVBT", [NT, 128, D], BF16)
    scrKBT = c.dram(u + "scrKBT", [NT, 128, D], BF16)
    scrKNT = c.dram(u + "scrKNT", [NT, 128, D], BF16)
    c.open_scratch()
    VF = c.sb(u + "VF", [128, ND * T], BF16)
    BET = c.sb(u + "BET", [128, NT * 4], F32)
    LAMB = c.sb(u + "LAMB", [128, 4], F32)
    PALL = c.sb(u + "PALL", [128, NT * 16], F32)

    c.open_scratch()
    QNF = c.sb(u + "QNF", [128, ND * T], BF16)
    KNF = c.sb(u + "KNF", [128, ND * T], BF16)
    c.open_scratch()
    load_weight_bf16(c, "L3a_Wq", u + "W", D, D, scratch=True)
    emit_proj(c, "XHT", u + "W", u + "QNF", D, out_major="f")
    _reload_w(c, "L3a_Wk", u + "W", D)
    emit_proj(c, "XHT", u + "W", u + "KNF", D, out_major="f")
    _reload_w(c, "L3a_Wv", u + "W", D)
    emit_proj(c, "XHT", u + "W", u + "VF", D, out_major="f")
    PADT = c.sb(u + "PADT", [128, T + 64], BF16, scratch=True)
    ACC = c.sb(u + "ACC", [128, T], F32, scratch=True)
    CW = c.sb(u + "CW", [128, ND * 4], F32, scratch=True)
    for wnm, X, Xk in (("L3a_convq", QNF, u + "QNF"), ("L3a_convk", KNF, u + "KNF"),
                       ("L3a_convv", VF, u + "VF")):
        cwv = c.t[wnm].rearrange("(cc p) j -> cc p j", p=128)
        for cc in range(ND):
            P.dma(lambda nc, cc=cc, cwv=cwv: nc.sync.dma_start(
                out=CW[:, cc * 4:(cc + 1) * 4], in_=cwv[cc]),
                writes=[u + "cw"], queue="sync", tag=u + "cv")
        _l3a_dwconv(c, u, X, Xk, CW, u + "cw", 4, PADT, ACC, act=AF.Silu)
    WB = c.sb(u + "WB", [128, ND * 4], BF16, scratch=True)
    wbv = c.t["L3a_Wb"].rearrange("(cd p) f -> p cd f", p=128)
    P.dma(lambda nc: nc.gpsimd.dma_start(
        out=WB.rearrange("p (cd f) -> p cd f", f=4), in_=wbv),
        writes=[u + "wb"], queue="gpsimd", tag=u + "bt")
    for i in range(NT):
        fns = []
        for cd in range(ND):
            fns.append(lambda nc, cd=cd, i=i: nc.tensor.matmul(
                PSA[:, 0:4], c.t["XHT"][:, cd * T + i * 128: cd * T + (i + 1) * 128],
                WB[:, cd * 4:(cd + 1) * 4], start=(cd == 0), stop=(cd == ND - 1)))
        P.matmul_group(fns, reads=["XHT", u + "wb"], writes=["PSA"], tag=u + "bt")
        P.op("scalar", lambda nc, i=i: nc.scalar.activation(
            BET[:, i * 4:(i + 1) * 4], PSA[:, 0:4], AF.Sigmoid),
            reads=["PSA"], writes=[u + "bet"], tag=u + "bt")
    FR = load_row_f32(c, "L3a_forget", u + "FR", 4, scratch=True)
    P.op("scalar", lambda nc: nc.scalar.activation(FR, FR, AF.Sigmoid),
         reads=[u + "FR"], writes=[u + "FR"], tag=u)
    P.op("vector", lambda nc: nc.vector.tensor_scalar(
        FR, FR, 0.45, 0.55, ALU.mult, ALU.add),
        reads=[u + "FR"], writes=[u + "FR"], tag=u)
    emit_bcast_row(c, FR, u + "FR", u + "LAMBt", 4, dtype=F32, scratch=True)
    P.op("vector", lambda nc: nc.vector.tensor_copy(LAMB, c.t[u + "LAMBt"]),
         reads=[u + "LAMBt"], writes=[u + "lamb"], tag=u)
    c.close_scratch()
    BETBALL = c.sb(u + "BETBALL", [128, H * T], BF16)

    c.open_scratch()
    SQW = c.sb(u + "SQW", [128, 512], F32, scratch=True)
    SQW2 = c.sb(u + "SQW2", [128, 512], F32, scratch=True)
    E6 = c.sb(u + "E6", [1, 1], F32, scratch=True)
    P.op("vector", lambda nc: nc.vector.memset(E6, 1e-6), writes=[u + "e6"], tag=u)
    RSR = c.sb(u + "RSR", [1, T], F32, scratch=True)
    RSB = c.sb(u + "RSB", [128, T], BF16, scratch=True)
    for X, Xk in ((QNF, u + "QNF"), (KNF, u + "KNF")):
        for h in range(H):
            for tb in range(4):
                for q2 in range(2):
                    cd2 = 2 * h + q2
                    sq = SQW if q2 == 0 else SQW2
                    sqk = u + ("sqw" if q2 == 0 else "sqw2")
                    P.op("scalar", lambda nc, cd2=cd2, tb=tb, sq=sq, X=X:
                         nc.scalar.activation(
                             sq, X[:, cd2 * T + tb * 512: cd2 * T + (tb + 1) * 512],
                             AF.Square),
                         reads=[Xk], writes=[sqk], tag=u + "nr")
                P.matmul_group([
                    lambda nc: nc.tensor.matmul(PSC[0:1, 0:512], c.t["ONESCOLF"],
                                                SQW, start=True, stop=False),
                    lambda nc: nc.tensor.matmul(PSC[0:1, 0:512], c.t["ONESCOLF"],
                                                SQW2, start=False, stop=True)],
                    reads=[u + "sqw", u + "sqw2", "ONESCOLF"],
                    writes=["PSCn"], tag=u + "nr")
                P.op("scalar", lambda nc, tb=tb: nc.scalar.activation(
                    RSR[0:1, tb * 512:(tb + 1) * 512], PSC[0:1, 0:512], AF.Sqrt,
                    bias=E6), reads=["PSCn", u + "e6"], writes=[u + "rsr"], tag=u + "nr")
            P.op("vector", lambda nc: nc.vector.reciprocal(RSR, RSR),
                 reads=[u + "rsr"], writes=[u + "rsr"], tag=u + "nr")
            for g0 in range(0, T, 512):
                P.matmul_group([lambda nc, g0=g0: nc.tensor.matmul(
                    PSD[:, 0:512], c.t["ONE1F"][0:1, :], RSR[0:1, g0:g0 + 512],
                    start=True, stop=True)],
                    reads=[u + "rsr", "ONE1F"], writes=["PSDn"], tag=u + "nr")
                P.op("vector", lambda nc, g0=g0: nc.vector.tensor_copy(
                    RSB[:, g0:g0 + 512], PSD[:, 0:512]),
                    reads=["PSDn"], writes=[u + "rsb"], tag=u + "nr")
            for q2 in range(2):
                cd2 = 2 * h + q2
                P.op("vector", lambda nc, cd2=cd2, X=X: nc.vector.tensor_mul(
                    X[:, cd2 * T:(cd2 + 1) * T], X[:, cd2 * T:(cd2 + 1) * T], RSB),
                    reads=[Xk, u + "rsb"], writes=[Xk], tag=u + "nr")
    SSB = c.sb(u + "SSB", [32, 128], BF16, scratch=True)
    P.op("vector", lambda nc: nc.vector.memset(SSB, 0.0), writes=[u + "ssbz"], tag=u)
    OH = c.sb(u + "OH", [32, 32 * 128], BF16, scratch=True)
    P.dma(lambda nc: nc.gpsimd.dma_start(out=OH, in_=c.t["c_oh32"]),
          writes=[u + "oh"], queue="gpsimd", tag=u)
    for i in range(NT):
        P.op("tensor", lambda nc, i=i: nc.tensor.transpose(
            PSC[0:4, 0:128], BET[:, i * 4:(i + 1) * 4], IDF),
            reads=[u + "bet", "IDENTF"], writes=["PSCn"], tag=u + "bb")
        P.op("vector", lambda nc: nc.vector.tensor_copy(SSB[0:4, :], PSC[0:4, 0:128]),
             reads=["PSCn", u + "ssbz"], writes=[u + "ssb"], tag=u + "bb")
        for h in range(H):
            P.matmul_group([lambda nc, h=h: nc.tensor.matmul(
                PSD[:, 0:128], OH[:, h * 128:(h + 1) * 128], SSB,
                start=True, stop=True)],
                reads=[u + "ssb", u + "oh"], writes=["PSDn"], tag=u + "bb")
            P.op("vector", lambda nc, i=i, h=h: nc.vector.tensor_copy(
                BETBALL[:, h * T + i * 128: h * T + (i + 1) * 128], PSD[:, 0:128]),
                reads=["PSDn"], writes=[u + "betb"], tag=u + "bb")
    TRS = c.sb(u + "TRS", [128, D], BF16, scratch=True)
    TRS2 = c.sb(u + "TRS2", [128, D], BF16, scratch=True)
    TRS3 = c.sb(u + "TRS3", [128, D], BF16, scratch=True)
    VBC = c.sb(u + "VBC", [128, 128], BF16, scratch=True)
    KBC = c.sb(u + "KBC", [128, 128], BF16, scratch=True)
    for i in range(NT):
        for cd in range(ND):
            h = cd // 2
            bsl = BETBALL[:, h * T + i * 128: h * T + (i + 1) * 128]
            P.op("gpsimd", lambda nc, cd=cd, i=i, bsl=bsl: nc.gpsimd.tensor_mul(
                VBC, VF[:, cd * T + i * 128: cd * T + (i + 1) * 128], bsl),
                reads=[u + "VF", u + "betb"], writes=[u + "vbc"], tag=u + "sp")
            P.op("vector", lambda nc, cd=cd, i=i, bsl=bsl: nc.vector.tensor_mul(
                KBC, KNF[:, cd * T + i * 128: cd * T + (i + 1) * 128], bsl),
                reads=[u + "KNF", u + "betb"], writes=[u + "kbc"], tag=u + "sp")
            P.op("tensor", lambda nc: nc.tensor.transpose(PST[:, 0:128], VBC, ID),
                 reads=[u + "vbc", "IDENT"], writes=["PST_0"], tag=u + "sp")
            P.op("vector", lambda nc, cd=cd: nc.vector.tensor_copy(
                TRS[:, cd * 128:(cd + 1) * 128], PST[:, 0:128]),
                reads=["PST_0"], writes=[u + "trs"], tag=u + "sp")
            P.op("tensor", lambda nc: nc.tensor.transpose(PST[:, 128:256], KBC, ID),
                 reads=[u + "kbc", "IDENT"], writes=["PST_0"], tag=u + "sp")
            P.op("vector", lambda nc, cd=cd: nc.vector.tensor_copy(
                TRS2[:, cd * 128:(cd + 1) * 128], PST[:, 128:256]),
                reads=["PST_0"], writes=[u + "trs2"], tag=u + "sp")
            P.op("tensor", lambda nc, cd=cd, i=i: nc.tensor.transpose(
                PST[:, 256:384], KNF[:, cd * T + i * 128: cd * T + (i + 1) * 128], ID),
                reads=[u + "KNF", "IDENT"], writes=["PST_0"], tag=u + "sp")
            P.op("vector", lambda nc, cd=cd: nc.vector.tensor_copy(
                TRS3[:, cd * 128:(cd + 1) * 128], PST[:, 256:384]),
                reads=["PST_0"], writes=[u + "trs3"], tag=u + "sp")
        P.dma(lambda nc, i=i: nc.sync.dma_start(out=scrVBT[i], in_=TRS),
              reads=[u + "trs"], writes=[u + f"Dvbt{i}"], queue="sync", tag=u + "sp")
        P.dma(lambda nc, i=i: nc.sync.dma_start(out=scrKBT[i], in_=TRS2),
              reads=[u + "trs2"], writes=[u + f"Dkbt{i}"], queue="sync", tag=u + "sp")
        P.dma(lambda nc, i=i: nc.sync.dma_start(out=scrKNT[i], in_=TRS3),
              reads=[u + "trs3"], writes=[u + f"Dknt{i}"], queue="sync", tag=u + "sp")
    c.close_scratch()

    c.open_scratch()
    SF = c.sb(u + "SF", [128, H * 2 * 256], F32)
    SB16 = c.sb(u + "SB16", [128, H * 2 * 256], BF16)
    SQp = c.t["SQ"]
    P.op("vector", lambda nc: nc.vector.memset(SF, 0.0),
         writes=[u + f"sf{h2}" for h2 in range(4)], tag=u)
    P.op("vector", lambda nc: nc.vector.memset(SB16, 0.0),
         writes=[u + f"sb16_{h2}" for h2 in range(4)], tag=u)
    XCp_ = [[c.sb(u + f"XC{p}_{j}", [64, 512], F32) for p in range(2)]
            for j in range(2)]
    UCp_ = [[c.sb(u + f"UC{p}_{j}", [64, 256], BF16) for p in range(2)]
            for j in range(2)]
    KCp_ = [[c.sb(u + f"KC{p}_{j}", [64, 256], BF16) for p in range(2)]
            for j in range(2)]
    M0p_ = [[c.sb(u + f"M0{p}_{j}", [64, 64], F32) for p in range(2)]
            for j in range(2)]
    MRW2_ = [[c.sb(u + f"MRW{p}_{j}", [64, 64], F32) for p in range(2)]
             for j in range(2)]
    MJT2_ = [[c.sb(u + f"MJT{p}_{j}", [64, 64], F32) for p in range(2)]
             for j in range(2)]
    MJ2_ = [[c.sb(u + f"MJ2{p}_{j}", [64, 64], F32) for p in range(2)]
            for j in range(2)]
    ATTr2_ = [[c.sb(u + f"ATTr{p}_{j}", [64, 64], F32) for p in range(2)]
              for j in range(2)]
    ATTf_ = [[c.sb(u + f"ATTf{p}_{j}", [64, 64], BF16) for p in range(2)]
             for j in range(2)]
    OSp_ = [[c.sb(u + f"OS{p}_{j}", [64, 256], BF16) for p in range(2)]
            for j in range(2)]
    WT_ = [[c.sb(u + f"WT{p}_{j}", [128, 128], BF16) for p in range(2)]
           for j in range(2)]
    KBW_ = [c.sb(u + f"KBWj{j}", [128, 256], BF16) for j in range(2)]
    for j in range(2):
        for p in range(2):
            P.op("vector", lambda nc, p=p, j=j: nc.vector.memset(ATTr2_[j][p], 0.0),
                 writes=[u + f"attrz{p}_{j}"], tag=u)
    KBW = c.sb(u + "KBW", [128, 256], BF16)
    VBTL = c.sb(u + "VBTL", [128, D], F32)
    KBTL = c.sb(u + "KBTL", [128, D], F32)
    KNTL = c.sb(u + "KNTL", [128, D], BF16)
    import os as _os
    _nch = int(_os.environ.get("L3A_NCH", str(NCH)))
    def _l3a_chunk(i, tile, b0, ip, XCp, UCp, KCp, M0p, MRW2, MJT2, MJ2,
                   ATTr2, ATTf, OSp, WT, KBW):
        for h in range(H):
            for q2 in range(2):
                cd2 = 2 * h + q2
                off = tile * 128 + b0
                P.op("vector", lambda nc, cd2=cd2, h=h, q2=q2, off=off:
                     nc.vector.tensor_mul(
                         KBW[:, h * 64 + q2 * 32: h * 64 + q2 * 32 + 32],
                         KNF[:, cd2 * T + off: cd2 * T + off + 32],
                         BETBALL[:, h * T + off: h * T + off + 32]),
                     reads=[u + "KNF", u + "betb"], writes=[u + f"kbw{ip}"], tag=u + "cA")
        if b0 == 0:
            P.dma(lambda nc, tile=tile: nc.gpsimd.dma_start(
                out=VBTL, in_=scrVBT[tile]),
                reads=[u + f"Dvbt{tile}"], writes=[u + "vbtl"],
                queue="gpsimd", tag=u + "cl")
            P.dma(lambda nc, tile=tile: nc.gpsimd.dma_start(
                out=KBTL, in_=scrKBT[tile]),
                reads=[u + f"Dkbt{tile}"], writes=[u + "kbtl"],
                queue="gpsimd", tag=u + "cl")
            P.dma(lambda nc, tile=tile: nc.sync.dma_start(
                out=KNTL, in_=scrKNT[tile]),
                reads=[u + f"Dknt{tile}"], writes=[u + "kntl"],
                queue="sync", tag=u + "cl")
        for p in range(2):
            for hp in range(2):
                h = 2 * p + hp
                P.op("vector", lambda nc, p=p, hp=hp, h=h, b0=b0:
                     nc.vector.tensor_copy(
                         XCp[p][32 * hp:32 * hp + 32, 0:256],
                         VBTL[b0:b0 + 32, h * 256:(h + 1) * 256]),
                     reads=[u + "vbtl"], writes=[u + f"xc{p}_{ip}"], tag=u + "cl")
                P.op("scalar", lambda nc, p=p, hp=hp, h=h, b0=b0:
                     nc.scalar.copy(
                         XCp[p][32 * hp:32 * hp + 32, 256:512],
                         KBTL[b0:b0 + 32, h * 256:(h + 1) * 256]),
                     reads=[u + "kbtl"], writes=[u + f"xc{p}_{ip}"], tag=u + "cl")
                P.op("vector", lambda nc, p=p, hp=hp, h=h, b0=b0:
                     nc.vector.tensor_copy(
                         KCp[p][32 * hp:32 * hp + 32, :],
                         KNTL[b0:b0 + 32, h * 256:(h + 1) * 256]),
                     reads=[u + "kntl"], writes=[u + f"kc{p}_{ip}"], tag=u + "cl")
        for p in range(2):
            pq = u + f"p{p}i{ip}_"
            for hp in range(2):
                h = 2 * p + hp
                fns = []
                for q2 in range(2):
                    cd2 = 2 * h + q2
                    fns.append(lambda nc, cd2=cd2, hp=hp, h=h, q2=q2, tile=tile,
                               b0=b0, p=p: nc.tensor.matmul(
                        PSC[32 * hp:32 * hp + 32,
                            256 * p + 32 * hp:256 * p + 32 * hp + 32],
                        KNF[:, cd2 * T + tile * 128 + b0: cd2 * T + tile * 128 + b0 + 32],
                        KBW[:, h * 64 + q2 * 32: h * 64 + q2 * 32 + 32],
                        start=(q2 == 0), stop=(q2 == 1)))
                P.matmul_group(fns, reads=[u + "KNF", u + f"kbw{ip}"],
                               writes=[u + f"psC{p}"], tag=u + "cA")
            P.op("vector", lambda nc, p=p: nc.vector.memset(MRW2[p], 0.0),
                 writes=[pq + "mrw"], tag=u + "cA")
            for hp in range(2):
                P.op("scalar", lambda nc, hp=hp, p=p: nc.scalar.activation(
                    MRW2[p][32 * hp:32 * hp + 32, 32 * hp:32 * hp + 32],
                    PSC[32 * hp:32 * hp + 32,
                        256 * p + 32 * hp:256 * p + 32 * hp + 32],
                    AF.Copy, scale=-1.0),
                    reads=[u + f"psC{p}"], writes=[pq + "mrw"], tag=u + "cA")
            P.op("gpsimd", lambda nc, p=p: nc.gpsimd.affine_select(
                M0p[p], MRW2[p], [[1, 64]], ALU.is_ge, 0.0,
                base=-1, channel_multiplier=-1),
                reads=[pq + "mrw"], writes=[pq + "m0"], tag=u + "cA")
            mcur, mck = M0p[p], pq + "m0"
            pb = 64 * p
            for j in range(5):
                P.matmul_group([lambda nc, p=p, pb=pb, mcur=mcur: nc.tensor.matmul(
                    PSB[pb:pb + 64, 0:512], mcur, XCp[p], start=True, stop=True)],
                    reads=[mck, u + f"xc{p}_{ip}"], writes=[u + f"psB{p}"], tag=u + "cT")
                P.op("vector", lambda nc, p=p, pb=pb: nc.vector.tensor_add(
                    XCp[p], XCp[p], PSB[pb:pb + 64, 0:512]),
                    reads=[u + f"psB{p}", u + f"xc{p}_{ip}"], writes=[u + f"xc{p}_{ip}"],
                    tag=u + "cT")
                if j == 4:
                    break
                P.op("tensor", lambda nc, mcur=mcur, p=p: nc.tensor.transpose(
                    PSC[0:64, 256 * p + 64:256 * p + 128], mcur, IDF[0:64, 0:64]),
                    reads=[mck, "IDENTF"], writes=[u + f"psC{p}"], tag=u + "cT")
                P.op("vector", lambda nc, p=p: nc.vector.tensor_copy(
                    MJT2[p], PSC[0:64, 256 * p + 64:256 * p + 128]),
                    reads=[u + f"psC{p}"], writes=[pq + "mjt"], tag=u + "cT")
                P.matmul_group([lambda nc, mcur=mcur, p=p: nc.tensor.matmul(
                    PSC[0:64, 256 * p:256 * p + 64], MJT2[p], mcur,
                    start=True, stop=True)],
                    reads=[pq + "mjt", mck], writes=[u + f"psC{p}"], tag=u + "cT")
                P.op("vector", lambda nc, p=p: nc.vector.tensor_copy(
                    MJ2[p], PSC[0:64, 256 * p:256 * p + 64]),
                    reads=[u + f"psC{p}"], writes=[pq + "mj2"], tag=u + "cT")
                mcur, mck = MJ2[p], pq + "mj2"
            for q2 in range(2):
                P.op("tensor", lambda nc, p=p, q2=q2: nc.tensor.transpose(
                    PSC[:, 256 * p + 128:256 * p + 192],
                    XCp[p][:, 256 + 128 * q2: 256 + 128 * (q2 + 1)],
                    IDF[0:64, 0:64]),
                    reads=[u + f"xc{p}_{ip}", "IDENTF"], writes=[u + f"psC{p}"],
                    tag=u + "cW")
                P.op("vector", lambda nc, p=p, q2=q2: nc.vector.tensor_copy(
                    WT[p][:, 64 * q2:64 * (q2 + 1)], PSC[:, 256 * p + 128:256 * p + 192]),
                    reads=[u + f"psC{p}"], writes=[pq + "wt"], tag=u + "cW")
            for hp in range(2):
                h = 2 * p + hp
                fns = []
                for q2 in range(2):
                    fns.append(lambda nc, p=p, hp=hp, h=h, q2=q2: nc.tensor.matmul(
                        PSD[32 * hp:32 * hp + 32, 256 * p:256 * p + 256],
                        WT[p][:, 64 * q2 + 32 * hp: 64 * q2 + 32 * hp + 32],
                        SB16[:, (h * 2 + q2) * 256:(h * 2 + q2 + 1) * 256],
                        start=(q2 == 0), stop=(q2 == 1)))
                P.matmul_group(fns, reads=[pq + "wt", u + f"sb16_{h}"],
                               writes=[u + f"p{p}psW{hp}"], tag=u + "cS")
            P.op("vector", lambda nc, p=p: nc.vector.tensor_sub(
                UCp[p], XCp[p][:, 0:256], PSD[0:64, 256 * p:256 * p + 256]),
                reads=[u + f"p{p}psW0", u + f"p{p}psW1", u + f"xc{p}_{ip}"],
                writes=[u + f"uc{p}_{ip}"], tag=u + "cS")
            for hp in range(2):
                h = 2 * p + hp
                fns = []
                for q2 in range(2):
                    cd2 = 2 * h + q2
                    fns.append(lambda nc, cd2=cd2, hp=hp, p=p, tile=tile, b0=b0:
                               nc.tensor.matmul(
                        PSC[32 * hp:32 * hp + 32,
                            256 * p + 32 * hp:256 * p + 32 * hp + 32],
                        KNF[:, cd2 * T + tile * 128 + b0: cd2 * T + tile * 128 + b0 + 32],
                        QNF[:, cd2 * T + tile * 128 + b0: cd2 * T + tile * 128 + b0 + 32],
                        start=(q2 == 0), stop=(q2 == 1)))
                P.matmul_group(fns, reads=[u + "KNF", u + "QNF"],
                               writes=[u + f"psC{p}"], tag=u + "cAt")
                P.op("scalar", lambda nc, hp=hp, p=p: nc.scalar.copy(
                    ATTr2[p][32 * hp:32 * hp + 32, 32 * hp:32 * hp + 32],
                    PSC[32 * hp:32 * hp + 32,
                        256 * p + 32 * hp:256 * p + 32 * hp + 32]),
                    reads=[u + f"psC{p}", u + f"attrz{p}_{ip}"], writes=[pq + "attr"],
                    tag=u + "cAt")
            P.op("gpsimd", lambda nc, p=p: nc.gpsimd.affine_select(
                ATTf[p], ATTr2[p], [[1, 64]], ALU.is_ge, 0.0,
                base=0, channel_multiplier=-1),
                reads=[pq + "attr"], writes=[pq + "attf"], tag=u + "cAt")
            for hp in range(2):
                h = 2 * p + hp
                fns = []
                for q2 in range(2):
                    cd2 = 2 * h + q2
                    fns.append(lambda nc, cd2=cd2, hp=hp, h=h, q2=q2, p=p,
                               tile=tile, b0=b0: nc.tensor.matmul(
                        PSA[32 * hp:32 * hp + 32, 256 * p:256 * p + 256],
                        QNF[:, cd2 * T + tile * 128 + b0: cd2 * T + tile * 128 + b0 + 32],
                        SB16[:, (h * 2 + q2) * 256:(h * 2 + q2 + 1) * 256],
                        start=(q2 == 0), stop=(q2 == 1)))
                P.matmul_group(fns, reads=[u + "QNF", u + f"sb16_{h}"],
                               writes=[u + f"p{p}psQ{hp}"], tag=u + "cO")
            P.matmul_group([lambda nc, p=p: nc.tensor.matmul(
                PSD[64:128, 256 * p:256 * p + 256], ATTf[p], UCp[p],
                start=True, stop=True)],
                reads=[pq + "attf", u + f"uc{p}_{ip}"], writes=[u + f"p{p}psO"], tag=u + "cO")
            P.op("scalar", lambda nc, p=p: nc.scalar.copy(
                OSp[p], PSA[0:64, 256 * p:256 * p + 256]),
                reads=[u + f"p{p}psQ0", u + f"p{p}psQ1"], writes=[pq + "os"], tag=u + "cO")
            P.op("vector", lambda nc, p=p: nc.vector.tensor_add(
                OSp[p], OSp[p], PSD[64:128, 256 * p:256 * p + 256]),
                reads=[pq + "os", u + f"p{p}psO"], writes=[pq + "os"], tag=u + "cO")
            for hp in range(2):
                h = 2 * p + hp
                P.dma(lambda nc, i=i, p=p, hp=hp, h=h: nc.sync.dma_start(
                    out=scrOD[i][:, h * 256:(h + 1) * 256],
                    in_=OSp[p][32 * hp:32 * hp + 32, :]),
                    reads=[pq + "os"], writes=[u + f"Dod{i}_{p}"], queue="sync",
                    tag=u + "cO")
            for hp in range(2):
                h = 2 * p + hp
                fns = []
                for q2 in range(2):
                    fns.append(lambda nc, p=p, hp=hp, q2=q2: nc.tensor.matmul(
                        SQp[:, 512 * p + 256 * q2:512 * p + 256 * (q2 + 1)],
                        KCp[p][32 * hp:32 * hp + 32, 128 * q2:128 * (q2 + 1)],
                        UCp[p][32 * hp:32 * hp + 32, :],
                        start=True, stop=True))
                P.matmul_group(fns, reads=[u + f"kc{p}_{ip}", u + f"uc{p}_{ip}"],
                               writes=[u + f"psU{p}"], tag=u + "cU")
                for q2 in range(2):
                    sl = SF[:, (h * 2 + q2) * 256:(h * 2 + q2 + 1) * 256]
                    P.op("vector", lambda nc, sl=sl, h=h, q2=q2, p=p:
                         nc.vector.scalar_tensor_tensor(
                             sl, sl, LAMB[:, h:h + 1],
                             SQp[:, 512 * p + 256 * q2:512 * p + 256 * (q2 + 1)],
                             ALU.mult, ALU.add),
                         reads=[u + f"psU{p}", u + "lamb", u + f"sf{h}"],
                         writes=[u + f"sf{h}"], tag=u + "cU")
                    P.op("gpsimd", lambda nc, sl=sl, h=h, q2=q2:
                         nc.gpsimd.tensor_copy(
                             SB16[:, (h * 2 + q2) * 256:(h * 2 + q2 + 1) * 256], sl),
                         reads=[u + f"sf{h}"], writes=[u + f"sb16_{h}"], tag=u + "cU")
    for i in range(_nch):
        _l3a_chunk(i, i // 4, (i % 4) * 32, i % 2, XCp_[i % 2], UCp_[i % 2],
                   KCp_[i % 2], M0p_[i % 2], MRW2_[i % 2], MJT2_[i % 2],
                   MJ2_[i % 2], ATTr2_[i % 2], ATTf_[i % 2], OSp_[i % 2],
                   WT_[i % 2], KBW_[i % 2])
    c.close_scratch()
    c.close_scratch()

    emit_L3_attn_a_fuse(c, uid)
    c.close_scratch()


def emit_L3_attn_a_fuse(c: Ctx, uid):
    u = uid
    P = c.P
    H = 4
    VF = c.t[u + "VF"]
    scrOD = c.t[u + "scrOD"]
    PALL = c.t[u + "PALL"]
    PSA, PSB, PSC, PSD = c.t["PSA"], c.t["PSB"], c.t["PSC"], c.t["PSD"]
    PST = c.t["PST"]
    ID = c.t["IDENT"]
    scrODf = scrOD.rearrange("n c d -> (n c) d")
    c.open_scratch()
    SHF = c.sb(u + "SHF", [128, ND * T], BF16)
    LOF = c.sb(u + "LOF", [128, ND * T], BF16)
    c.open_scratch()
    PADT = c.sb(u + "PAD2", [128, T + 64], BF16, scratch=True)
    ACC = c.sb(u + "ACC2", [128, T], F32, scratch=True)
    FW3 = c.sb(u + "FW3", [128, ND * 3], F32, scratch=True)
    FWL = c.sb(u + "FWL", [128, ND * 63], F32, scratch=True)
    P.dma(lambda nc: nc.sync.dma_start(
        out=FW3.rearrange("p (cc k) -> p cc k", k=3),
        in_=c.t["L3a_firs"].rearrange("(cc p) k -> p cc k", p=128)),
        writes=[u + "fw3"], queue="sync", tag=u + "fir")
    P.dma(lambda nc: nc.sync.dma_start(
        out=FWL.rearrange("p (cc k) -> p cc k", k=63),
        in_=c.t["L3a_firl"].rearrange("(cc p) k -> p cc k", p=128)),
        writes=[u + "fwl"], queue="sync", tag=u + "fir")
    _l3a_dwconv(c, u, VF, u + "VF", FW3, u + "fw3", 3, PADT, ACC,
                dst=SHF, dstkey=u + "SHF", tag="f3")
    DIAGS = [c.sb(u + f"DIAGS{jj}", [128, 63 * 128], BF16, scratch=True)
             for jj in range(2)]
    PADT2 = [c.sb(u + f"PADT2_{jj}", [128, T + 64], BF16, scratch=True)
             for jj in range(2)]
    PSA_ = c.t["PSA"]
    PSB_ = c.t["PSB"]
    for cc in range(ND):
        cp = cc % 2
        P.op("vector", lambda nc, cp=cp: nc.vector.memset(PADT2[cp][:, 0:62], 0.0),
             writes=[u + f"padt2{cp}"], tag=u + "f63")
        P.op("gpsimd", lambda nc, cc=cc, cp=cp: nc.gpsimd.tensor_copy(
            PADT2[cp][:, 62:T + 62], VF[:, cc * T:(cc + 1) * T]),
            reads=[u + "VF"], writes=[u + f"padt2{cp}"], tag=u + "f63")
        for j in range(63):
            P.op("vector", lambda nc, cc=cc, j=j, cp=cp: nc.vector.tensor_scalar_mul(
                DIAGS[cp][:, j * 128:(j + 1) * 128], c.t["IDENT"],
                FWL[:, cc * 63 + j:cc * 63 + j + 1]),
                reads=["IDENT", u + "fwl"], writes=[u + f"diags{cp}"], tag=u + "f63")
        for tq in range(4):
            ps = PSA_ if tq % 2 == 0 else PSB_
            bank = "PSAf" if tq % 2 == 0 else "PSBf"
            fns = []
            for j in range(63):
                fns.append(lambda nc, j=j, tq=tq, ps=ps, cp=cp: nc.tensor.matmul(
                    ps, DIAGS[cp][:, j * 128:(j + 1) * 128],
                    PADT2[cp][:, tq * 512 + j: tq * 512 + j + 512],
                    start=(j == 0), stop=(j == 62)))
            P.matmul_group(fns, reads=[u + f"diags{cp}", u + f"padt2{cp}"],
                           writes=[bank], tag=u + "f63")
            P.op("scalar", lambda nc, cc=cc, tq=tq, ps=ps: nc.scalar.copy(
                LOF[:, cc * T + tq * 512: cc * T + (tq + 1) * 512], ps),
                reads=[bank], writes=[u + "LOF"], tag=u + "f63")
    c.close_scratch()
    STT = c.sb(u + "STT", [64, T], BF16)
    c.open_scratch()
    SHT = c.sb(u + "SHT", [128, D], BF16, scratch=True)
    LOT = c.sb(u + "LOT", [128, D], BF16, scratch=True)
    VTT = c.sb(u + "VTT", [128, D], BF16, scratch=True)
    DLT = c.sb(u + "DLT", [128, D], BF16, scratch=True)
    SQWD = c.sb(u + "SQWD", [128, D], F32, scratch=True)
    STB = c.sb(u + "STB", [128, 64], F32, scratch=True)
    STBB = c.sb(u + "STBB", [128, 64], BF16, scratch=True)
    SUM4 = c.sb(u + "SUM4", [128, 4], F32, scratch=True)
    SSQ4 = c.sb(u + "SSQ4", [128, 4], F32, scratch=True)
    M2W = c.sb(u + "M2W", [128, 4], F32, scratch=True)
    L24 = c.sb(u + "L24", [128, 4], F32, scratch=True)
    for i in range(NT):
        P.dma(lambda nc, i=i: nc.sync.dma_start(
            out=DLT, in_=scrODf[i * 128:(i + 1) * 128, :]),
            reads=[u + f"Dod{i}_0", u + f"Dod{i}_1"], writes=[u + "dlt"],
            queue="sync", tag=u + "st")
        for src, srck, dst, dstk in ((SHF, u + "SHF", SHT, u + "sht"),
                                     (LOF, u + "LOF", LOT, u + "lot"),
                                     (VF, u + "VF", VTT, u + "vtt")):
            pairs = []
            for cd in range(ND):
                pairs.append((src[:, cd * T + i * 128: cd * T + (i + 1) * 128],
                              dst[:, cd * 128:(cd + 1) * 128]))
            emit_transpose_tiles(c, srck, dstk, pairs)
        for b, (SRC, sk) in enumerate(((SHT, u + "sht"), (LOT, u + "lot"),
                                       (DLT, u + "dlt"), (VTT, u + "vtt"))):
            v3 = SRC.rearrange("p (h d) -> p h d", d=256)
            stv = STB[:, b * 16:(b + 1) * 16].rearrange("p (h s) -> p h s", s=4)
            P.op("vector", lambda nc, v3=v3: nc.vector.reduce_sum(
                SUM4.rearrange("p (h o) -> p h o", o=1), v3, axis=AX.X),
                reads=[sk], writes=[u + "sum4"], tag=u + "st")
            P.op("scalar", lambda nc, SRC=SRC: nc.scalar.activation(
                SQWD, SRC, AF.Square), reads=[sk], writes=[u + "sqwd"], tag=u + "st")
            P.op("vector", lambda nc: nc.vector.reduce_sum(
                SSQ4.rearrange("p (h o) -> p h o", o=1),
                SQWD.rearrange("p (h d) -> p h d", d=256), axis=AX.X),
                reads=[u + "sqwd"], writes=[u + "ssq4"], tag=u + "st")
            P.op("vector", lambda nc, v3=v3, stv=stv: nc.vector.reduce_max(
                stv[:, :, 2:3], v3, axis=AX.X),
                reads=[sk], writes=[u + "stb"], tag=u + "st")
            P.op("vector", lambda nc, stv=stv: nc.vector.tensor_scalar_mul(
                stv[:, :, 0:1], SUM4.rearrange("p (h o) -> p h o", o=1), 1.0 / 256),
                reads=[u + "sum4"], writes=[u + "stb"], tag=u + "st")
            P.op("vector", lambda nc: nc.vector.tensor_mul(M2W, SUM4, SUM4),
                 reads=[u + "sum4"], writes=[u + "m2w"], tag=u + "st")
            P.op("vector", lambda nc: nc.vector.tensor_scalar_mul(
                SSQ4, SSQ4, 1.0 / 256),
                reads=[u + "ssq4"], writes=[u + "ssq4"], tag=u + "st")
            P.op("vector", lambda nc, stv=stv: nc.vector.scalar_tensor_tensor(
                stv[:, :, 1:2], M2W.rearrange("p (h o) -> p h o", o=1),
                -1.0 / 65536, SSQ4.rearrange("p (h o) -> p h o", o=1),
                ALU.mult, ALU.add),
                reads=[u + "m2w", u + "ssq4"], writes=[u + "stb"], tag=u + "st")
            P.op("scalar", lambda nc: nc.scalar.activation(
                L24, SSQ4, AF.Sqrt, scale=256.0),
                reads=[u + "ssq4"], writes=[u + "l24"], tag=u + "st")
            P.op("vector", lambda nc, stv=stv: nc.vector.tensor_copy(
                stv[:, :, 3:4], L24.rearrange("p (h o) -> p h o", o=1)),
                reads=[u + "l24"], writes=[u + "stb"], tag=u + "st")
        P.op("vector", lambda nc: nc.vector.tensor_copy(STBB, STB),
             reads=[u + "stb"], writes=[u + "stbb"], tag=u + "st")
        P.op("tensor", lambda nc: nc.tensor.transpose(PST[0:64, 0:128], STBB, ID),
             reads=[u + "stbb", "IDENT"], writes=["PST_0"], tag=u + "st")
        P.op("vector", lambda nc, i=i: nc.vector.tensor_copy(
            STT[:, i * 128:(i + 1) * 128], PST[0:64, 0:128]),
            reads=["PST_0"], writes=[u + "stt"], tag=u + "st")
    c.close_scratch()
    c.open_scratch()
    load_weight_bf16(c, "L3a_W1h", u + "W1H", D, 2048)
    W1Hv = c.t[u + "W1H"].rearrange("p (cd f) -> p cd f", f=2048)
    W1S = c.sb(u + "W1S", [64, 2048], BF16)
    P.dma(lambda nc: nc.gpsimd.dma_start(out=W1S, in_=c.t["L3a_W1s"]),
          writes=[u + "w1s"], queue="gpsimd", tag=u + "g")
    B1C = load_chunkvec_f32(c, "L3a_b1", u + "B1C", 2048)
    load_weight_bf16(c, "L3a_W2", u + "W2", 2048, 16)
    W2v = c.t[u + "W2"].rearrange("p (cd f) -> p cd f", f=16)
    B2R = load_row_f32(c, "L3a_b2", u + "B2R", 16)
    emit_bcast_row(c, B2R, u + "B2R", u + "B2B", 16, dtype=F32)
    B2B = c.t[u + "B2B"]
    FLR = load_row_f32(c, "L3a_floor", u + "FLR", 16)
    P.op("scalar", lambda nc: nc.scalar.activation(FLR, FLR, AF.Sigmoid),
         reads=[u + "FLR"], writes=[u + "FLR"], tag=u + "g")
    P.op("vector", lambda nc: nc.vector.tensor_scalar_mul(FLR, FLR, 0.075),
         reads=[u + "FLR"], writes=[u + "FLR"], tag=u + "g")
    emit_bcast_row(c, FLR, u + "FLR", u + "FLB", 16, dtype=F32)
    FLB = c.t[u + "FLB"]
    TRW = load_row_f32(c, "L3a_logtemp", u + "TRW", 4)
    P.op("scalar", lambda nc: nc.scalar.activation(TRW, TRW, AF.Exp),
         reads=[u + "TRW"], writes=[u + "TRW"], tag=u + "g")
    P.op("vector", lambda nc: nc.vector.reciprocal(TRW, TRW),
         reads=[u + "TRW"], writes=[u + "TRW"], tag=u + "g")
    emit_bcast_row(c, TRW, u + "TRW", u + "TIN4", 4, dtype=F32)
    TIN4 = c.t[u + "TIN4"]
    TINV16 = c.sb(u + "TINV16", [128, 16], F32)
    ONE4 = c.sb(u + "ONE4", [128, 4], F32)
    P.op("vector", lambda nc: nc.vector.memset(ONE4, 1.0), writes=[u + "one4"], tag=u + "g")
    for h in range(H):
        P.op("vector", lambda nc, h=h: nc.vector.tensor_scalar_mul(
            TINV16[:, 4 * h:4 * (h + 1)], ONE4, TIN4[:, h:h + 1]),
            reads=[u + "one4", u + "TIN4"], writes=[u + "tinv"], tag=u + "g")
    GMF = c.sb(u + "GMF", [128, 16 * 512], BF16)
    LGT = c.sb(u + "LGT", [128, 16], F32)
    MX4 = c.sb(u + "MX4", [128, 4], F32)
    for tb in range(4):
        for oc in range(16):
            ps = [PSA, PSB][oc % 2]
            bank = ["PSAg", "PSBg"][oc % 2]
            fns = []
            for cd in range(ND):
                fns.append(lambda nc, cd=cd, oc=oc, tb=tb, ps=ps: nc.tensor.matmul(
                    ps, W1Hv[:, cd, oc * 128:(oc + 1) * 128],
                    c.t["XHT"][:, cd * T + tb * 512: cd * T + (tb + 1) * 512],
                    start=(cd == 0), stop=False))
            fns.append(lambda nc, oc=oc, tb=tb, ps=ps: nc.tensor.matmul(
                ps, W1S[0:64, oc * 128:(oc + 1) * 128],
                STT[0:64, tb * 512:(tb + 1) * 512], start=False, stop=True))
            P.matmul_group(fns, reads=["XHT", u + "stt", u + "w1s"] + c.t[u + "W1H_keys"],
                           writes=[bank], tag=u + "g1")
            P.op("scalar", lambda nc, oc=oc, ps=ps: nc.scalar.activation(
                GMF[:, oc * 512:(oc + 1) * 512], ps, AF.Gelu, bias=B1C[:, oc:oc + 1]),
                reads=[bank, u + "B1C"], writes=[u + "gmf"], tag=u + "g1")
        for tl in range(4):
            i = tb * 4 + tl
            fns = []
            for oc in range(16):
                fns.append(lambda nc, oc=oc, tl=tl: nc.tensor.matmul(
                    PSC[:, 0:16], GMF[:, oc * 512 + tl * 128: oc * 512 + (tl + 1) * 128],
                    W2v[:, oc, :], start=(oc == 0), stop=(oc == 15)))
            P.matmul_group(fns, reads=[u + "gmf"] + c.t[u + "W2_keys"],
                           writes=[u + "psL"], tag=u + "g2")
            P.op("vector", lambda nc: nc.vector.tensor_add(LGT, PSC[:, 0:16], B2B),
                 reads=[u + "psL", u + "B2B"], writes=[u + "lgt"], tag=u + "g2")
            P.op("vector", lambda nc: nc.vector.tensor_mul(LGT, LGT, TINV16),
                 reads=[u + "lgt", u + "tinv"], writes=[u + "lgt"], tag=u + "g2")
            lgv = LGT.rearrange("p (h s) -> p h s", s=4)
            P.op("vector", lambda nc, lgv=lgv: nc.vector.reduce_max(
                MX4.rearrange("p (h o) -> p h o", o=1), lgv, axis=AX.X),
                reads=[u + "lgt"], writes=[u + "mx4"], tag=u + "g2")
            for h in range(H):
                P.op("vector", lambda nc, h=h: nc.vector.tensor_single_scalar(
                    LGT[:, 4 * h:4 * (h + 1)], LGT[:, 4 * h:4 * (h + 1)],
                    MX4[:, h:h + 1], ALU.subtract),
                    reads=[u + "lgt", u + "mx4"], writes=[u + "lgt"], tag=u + "g2")
            P.op("scalar", lambda nc: nc.scalar.activation(LGT, LGT, AF.Exp),
                 reads=[u + "lgt"], writes=[u + "lgt"], tag=u + "g2")
            P.op("vector", lambda nc, lgv=lgv: nc.vector.reduce_sum(
                MX4.rearrange("p (h o) -> p h o", o=1), lgv, axis=AX.X),
                reads=[u + "lgt"], writes=[u + "mx4"], tag=u + "g2")
            P.op("vector", lambda nc: nc.vector.reciprocal(MX4, MX4),
                 reads=[u + "mx4"], writes=[u + "mx4"], tag=u + "g2")
            for h in range(H):
                P.op("vector", lambda nc, h=h: nc.vector.tensor_scalar_mul(
                    LGT[:, 4 * h:4 * (h + 1)], LGT[:, 4 * h:4 * (h + 1)],
                    MX4[:, h:h + 1]),
                    reads=[u + "lgt", u + "mx4"], writes=[u + "lgt"], tag=u + "g2")
            P.op("vector", lambda nc: nc.vector.tensor_tensor(LGT, LGT, FLB, ALU.max),
                 reads=[u + "lgt", u + "FLB"], writes=[u + "lgt"], tag=u + "g2")
            P.op("vector", lambda nc, lgv=lgv: nc.vector.reduce_sum(
                MX4.rearrange("p (h o) -> p h o", o=1), lgv, axis=AX.X),
                reads=[u + "lgt"], writes=[u + "mx4"], tag=u + "g2")
            P.op("vector", lambda nc: nc.vector.reciprocal(MX4, MX4),
                 reads=[u + "mx4"], writes=[u + "mx4"], tag=u + "g2")
            for h in range(H):
                P.op("vector", lambda nc, h=h, i=i: nc.vector.tensor_scalar_mul(
                    PALL[:, i * 16 + 4 * h: i * 16 + 4 * (h + 1)],
                    LGT[:, 4 * h:4 * (h + 1)], MX4[:, h:h + 1]),
                    reads=[u + "lgt", u + "mx4"], writes=[u + "pall"], tag=u + "g2")
    c.close_scratch()
    c.open_scratch()
    load_weight_bf16(c, "L3a_Wo", u + "WO", D, D)
    WOv = c.t[u + "WO"].rearrange("p (cd f) -> p cd f", f=D)
    NWR = load_row_bf16(c, "L3a_normw", u + "NWR", 256)
    emit_bcast_row(c, NWR, u + "NWR", u + "NWB", 256, dtype=BF16)
    NWB = c.t[u + "NWB"]
    E5 = c.sb(u + "E5", [128, 1], F32)
    P.op("vector", lambda nc: nc.vector.memset(E5, 1e-5), writes=[u + "e5"], tag=u)
    OT = c.sb(u + "OT", [128, D], BF16)
    SHT = c.sb(u + "SHTb", [128, D], BF16)
    LOT = c.sb(u + "LOTb", [128, D], BF16)
    VTT = c.sb(u + "VTTb", [128, D], BF16)
    DLT = c.sb(u + "DLTb", [128, D], BF16)
    ACC4 = c.sb(u + "ACC4", [128, 256], F32)
    OFN = c.sb(u + "OFN", [128, 256], F32)
    OFU = c.sb(u + "OFU", [128, D], BF16)
    TRF = c.sb(u + "TRF", [128, D], BF16)
    SS1 = c.sb(u + "SS1", [128, 1], F32)
    SQV = c.sb(u + "SQV", [128, 256], F32)
    for i in range(NT):
        P.dma(lambda nc, i=i: nc.sync.dma_start(
            out=DLT, in_=scrODf[i * 128:(i + 1) * 128, :]),
            reads=[u + f"Dod{i}_0", u + f"Dod{i}_1"], writes=[u + "dltb"],
            queue="sync", tag=u + "bl")
        for src, srck, dst, dstk in ((SHF, u + "SHF", SHT, u + "shtb"),
                                     (LOF, u + "LOF", LOT, u + "lotb"),
                                     (VF, u + "VF", VTT, u + "vttb")):
            pairs = []
            for cd in range(ND):
                pairs.append((src[:, cd * T + i * 128: cd * T + (i + 1) * 128],
                              dst[:, cd * 128:(cd + 1) * 128]))
            emit_transpose_tiles(c, srck, dstk, pairs)
        for h in range(H):
            sl = slice(h * 256, (h + 1) * 256)
            P.op("vector", lambda nc, h=h, i=i, sl=sl: nc.vector.tensor_scalar_mul(
                ACC4, SHT[:, sl], PALL[:, i * 16 + 4 * h: i * 16 + 4 * h + 1]),
                reads=[u + "shtb", u + "pall"], writes=[u + "acc4"], tag=u + "bl")
            for bsrc, bk, off in ((LOT, u + "lotb", 1), (DLT, u + "dltb", 2),
                                  (VTT, u + "vttb", 3)):
                P.op("vector", lambda nc, bsrc=bsrc, h=h, i=i, off=off, sl=sl:
                     nc.vector.scalar_tensor_tensor(
                         ACC4, bsrc[:, sl],
                         PALL[:, i * 16 + 4 * h + off: i * 16 + 4 * h + off + 1],
                         ACC4, ALU.mult, ALU.add),
                     reads=[bk, u + "pall", u + "acc4"], writes=[u + "acc4"],
                     tag=u + "bl")
            P.op("scalar", lambda nc: nc.scalar.activation(
                SQV, ACC4, AF.Square, accum_out=SS1),
                reads=[u + "acc4"], writes=[u + "sqv", u + "ss1"], tag=u + "bl")
            P.op("scalar", lambda nc: nc.scalar.activation(
                SS1, SS1, AF.Sqrt, scale=1.0 / 256, bias=E5),
                reads=[u + "ss1", u + "e5"], writes=[u + "ss1"], tag=u + "bl")
            P.op("vector", lambda nc: nc.vector.reciprocal(SS1, SS1),
                 reads=[u + "ss1"], writes=[u + "ss1"], tag=u + "bl")
            P.op("vector", lambda nc: nc.vector.tensor_scalar_mul(OFN, ACC4, SS1),
                 reads=[u + "acc4", u + "ss1"], writes=[u + "ofn"], tag=u + "bl")
            P.op("vector", lambda nc, sl=sl: nc.vector.tensor_mul(
                OFU[:, sl], OFN, NWB),
                reads=[u + "ofn", u + "NWB"], writes=[u + "ofu"], tag=u + "bl")
        pairs = []
        for cd in range(ND):
            pairs.append((OFU[:, cd * 128:(cd + 1) * 128],
                          TRF[:, cd * 128:(cd + 1) * 128]))
        emit_transpose_tiles(c, u + "ofu", u + "trf", pairs)
        for hf in range(2):
            ps = [PSA, PSB][hf]
            bank = ["PSAo", "PSBo"][hf]
            fns = []
            for cd in range(ND):
                fns.append(lambda nc, cd=cd, hf=hf, ps=ps: nc.tensor.matmul(
                    ps, TRF[:, cd * 128:(cd + 1) * 128],
                    WOv[:, cd, hf * 512:(hf + 1) * 512],
                    start=(cd == 0), stop=(cd == ND - 1)))
            P.matmul_group(fns, reads=[u + "trf"] + c.t[u + "WO_keys"],
                           writes=[bank], tag=u + "bl")
            P.op("scalar", lambda nc, hf=hf, ps=ps: nc.scalar.copy(
                OT[:, hf * 512:(hf + 1) * 512], ps),
                reads=[bank], writes=[u + "ot"], tag=u + "bl")
        xt = c.t[f"XT{i % 2}"]
        xtk = f"XT{i % 2}"
        xv = c.t[c.xin_name].rearrange("(n p) d -> p n d", p=128)
        ov = c.t[c.xout_name].rearrange("(n p) d -> p n d", p=128)
        P.dma(lambda nc, i=i, xt=xt, xv=xv: nc.sync.dma_start(out=xt, in_=xv[:, i, :]),
              reads=[f"D{c.xin_name}_{i}"], writes=[xtk], queue="sync", tag=u + "res")
        P.op("vector", lambda nc, xt=xt: nc.vector.tensor_add(xt, xt, OT),
             reads=[u + "ot", xtk], writes=[xtk], tag=u + "res")
        P.dma(lambda nc, i=i, xt=xt, ov=ov: nc.sync.dma_start(out=ov[:, i, :], in_=xt),
              reads=[xtk], writes=[f"D{c.xout_name}_{i}"], queue="sync", tag=u + "res")
    c.close_scratch()
    c.close_scratch()


def stage_L0_attn_a(inputs):
    d = {}
    d["L0a_Win"] = np.ascontiguousarray(np.asarray(inputs["L0_attn_a_W_in"], np.float32))
    d["L0a_convw"] = np.ascontiguousarray(
        np.asarray(inputs["L0_attn_a_conv_w"], np.float32).reshape(2048, 4))
    d["L0a_convb"] = np.ascontiguousarray(np.asarray(inputs["L0_attn_a_conv_b"], np.float32))
    d["L0a_Wx"] = np.ascontiguousarray(np.asarray(inputs["L0_attn_a_W_x"], np.float32))
    d["L0a_Wdt"] = np.ascontiguousarray(np.asarray(inputs["L0_attn_a_W_dt"], np.float32))
    d["L0a_bdt"] = np.ascontiguousarray(np.asarray(inputs["L0_attn_a_b_dt"], np.float32))
    d["L0a_Alog"] = np.ascontiguousarray(np.asarray(inputs["L0_attn_a_A_log"], np.float32))
    d["L0a_Dvec"] = np.ascontiguousarray(np.asarray(inputs["L0_attn_a_D"], np.float32))
    d["L0a_Wout"] = np.ascontiguousarray(np.asarray(inputs["L0_attn_a_W_out"], np.float32))
    oh = np.zeros((32, 32, 128), np.float32)
    oh[np.arange(32), np.arange(32), :] = 1.0
    d["c_oh32"] = np.ascontiguousarray(oh.reshape(32, 32 * 128))
    return d


def declare_L0_attn_a(c: Ctx):
    c.dram("L0a_Win", [D, 2048], F32, kind="ExternalInput")
    c.dram("L0a_convw", [2048, 4], F32, kind="ExternalInput")
    c.dram("L0a_convb", [2048], F32, kind="ExternalInput")
    c.dram("L0a_Wx", [2048, 96], F32, kind="ExternalInput")
    c.dram("L0a_Wdt", [64, 2048], F32, kind="ExternalInput")
    c.dram("L0a_bdt", [2048], F32, kind="ExternalInput")
    c.dram("L0a_Alog", [2048, 16], F32, kind="ExternalInput")
    c.dram("L0a_Dvec", [2048], F32, kind="ExternalInput")
    c.dram("L0a_Wout", [2048, D], F32, kind="ExternalInput")
    c.dram("c_oh32", [32, 32 * 128], F32, kind="ExternalInput")


def emit_L0_attn_a(c: Ctx, uid="s00"):
    u = uid
    P = c.P
    DI = 2048
    NDI = 16
    psf = [c.t["PSA"], c.t["PSB"]]
    psfk = ["PSA", "PSB"]
    scrDel = c.dram(u + "scrDel", [NDI, 128, T], F32)
    scrDXV = c.dram(u + "scrDXV", [NDI, 128, T], BF16)
    scrXV = c.dram(u + "scrXV", [NDI, 128, T], BF16)
    c.open_scratch()
    SSMtm = c.sb(u + "SSMtm", [128, NT * 96], F32)
    AN = c.sb(u + "AN", [128, 256], F32)
    DCOL = load_chunkvec_f32(c, "L0a_Dvec", u + "DCOL", DI)
    OCOL = c.sb(u + "OCOL", [128, 1], F32)
    P.op("vector", lambda nc: nc.vector.memset(OCOL, 1.0), writes=[u + "ocol"], tag=u)

    c.open_scratch()
    XV = c.sb(u + "XV", [128, NDI * T], BF16)
    c.open_scratch()
    load_weight_bf16(c, "L0a_Win", u + "WIN", D, DI, scratch=True)
    emit_proj(c, "XHT", u + "WIN", u + "XV", DI, out_major="f")
    c.close_scratch()
    c.open_scratch()
    CW = c.sb(u + "CW", [128, 64], F32, scratch=True)
    CB = load_chunkvec_f32(c, "L0a_convb", u + "CB", DI, scratch=True)
    cwv = c.t["L0a_convw"].rearrange("(cc p) j -> cc p j", p=128)
    for cc in range(NDI):
        P.dma(lambda nc, cc=cc: nc.sync.dma_start(
            out=CW[:, cc * 4:(cc + 1) * 4], in_=cwv[cc]),
            writes=[u + "cw"], queue="sync", tag=u + "cv")
    XVP = c.sb(u + "XVP", [128, T + 4], BF16, scratch=True)
    ACC = c.sb(u + "ACC", [128, T], F32, scratch=True)
    for cc in range(NDI):
        P.op("vector", lambda nc: nc.vector.memset(XVP[:, 0:3], 0.0),
             writes=[u + "xvp"], tag=u + "cv")
        P.op("gpsimd", lambda nc, cc=cc: nc.gpsimd.tensor_copy(
            XVP[:, 3:T + 3], XV[:, cc * T:(cc + 1) * T]),
            reads=[u + "XV"], writes=[u + "xvp"], tag=u + "cv")
        P.op("vector", lambda nc, cc=cc: nc.vector.tensor_scalar_mul(
            ACC, XVP[:, 0:T], CW[:, cc * 4:cc * 4 + 1]),
            reads=[u + "xvp", u + "cw"], writes=[u + "acc"], tag=u + "cv")
        for j in range(1, 4):
            P.op("vector", lambda nc, cc=cc, j=j: nc.vector.scalar_tensor_tensor(
                ACC, XVP[:, j:T + j], CW[:, cc * 4 + j:cc * 4 + j + 1], ACC,
                ALU.mult, ALU.add),
                reads=[u + "xvp", u + "cw", u + "acc"], writes=[u + "acc"], tag=u + "cv")
        P.op("scalar", lambda nc, cc=cc: nc.scalar.activation(
            XV[:, cc * T:(cc + 1) * T], ACC, AF.Silu, bias=CB[:, cc:cc + 1]),
            reads=[u + "acc", u + "CB"], writes=[u + "XV"], tag=u + "cv")
    c.close_scratch()
    c.open_scratch()
    load_weight_bf16(c, "L0a_Wx", u + "WX", DI, 96, scratch=True)
    Wxv = c.t[u + "WX"].rearrange("p (cd f) -> p cd f", f=96)
    for i in range(NT):
        ps = psf[i % 2]
        bank = psfk[i % 2]
        fns = []
        for cd in range(NDI):
            fns.append(lambda nc, cd=cd, i=i, ps=ps: nc.tensor.matmul(
                ps[:, 0:96], XV[:, cd * T + i * 128: cd * T + (i + 1) * 128],
                Wxv[:, cd, :], start=(cd == 0), stop=(cd == NDI - 1)))
        P.matmul_group(fns, reads=[u + "XV"] + c.t[u + "WX_keys"],
                       writes=[bank], tag=u + "ssm")
        P.op("vector", lambda nc, i=i, ps=ps: nc.vector.tensor_copy(
            SSMtm[:, i * 96:(i + 1) * 96], ps[:, 0:96]),
            reads=[bank], writes=[u + "ssmtm"], tag=u + "ssm")
    DTT = c.sb(u + "DTT", [64, T], BF16, scratch=True)
    PSC = c.t["PSC"]
    for i in range(NT):
        P.op("tensor", lambda nc, i=i: nc.tensor.transpose(
            PSC[0:64, 0:128], SSMtm[:, i * 96:i * 96 + 64], c.t["IDENTF"]),
            reads=[u + "ssmtm", "IDENTF"], writes=["PSCt"], tag=u + "dtt")
        P.op("vector", lambda nc, i=i: nc.vector.tensor_copy(
            DTT[0:64, i * 128:(i + 1) * 128], PSC[0:64, 0:128]),
            reads=["PSCt"], writes=[u + "dtt"], tag=u + "dtt")
    ALG = c.t["L0a_Alog"].rearrange("(cc p) n -> cc p n", p=128)
    for cc in range(NDI):
        P.dma(lambda nc, cc=cc: nc.sync.dma_start(
            out=AN[:, cc * 16:(cc + 1) * 16], in_=ALG[cc]),
            writes=[u + f"anl{cc}"], queue="sync", tag=u + "an")
    P.op("scalar", lambda nc: nc.scalar.activation(AN, AN, AF.Exp),
         reads=[u + f"anl{cc2}" for cc2 in range(NDI)], writes=[u + "an"], tag=u + "an")
    P.op("vector", lambda nc: nc.vector.tensor_scalar_mul(AN, AN, -1.0),
         reads=[u + "an"], writes=[u + "an"], tag=u + "an")
    WDT = c.sb(u + "WDT", [64, DI], BF16, scratch=True)
    P.dma(lambda nc: nc.gpsimd.dma_start(out=WDT, in_=c.t["L0a_Wdt"]),
          writes=[u + "wdt"], queue="gpsimd", tag=u + "dl")
    BDT = load_chunkvec_f32(c, "L0a_bdt", u + "BDT", DI, scratch=True)
    DELC = c.sb(u + "DELC", [128, T], F32, scratch=True)
    DXVC = c.sb(u + "DXVC", [128, T], BF16, scratch=True)
    NEG = c.sb(u + "NEG", [128, 512], F32, scratch=True)
    EW = c.sb(u + "EW", [128, 512], F32, scratch=True)
    for cc in range(NDI):
        for tb in range(4):
            ps = psf[tb % 2]
            bank = psfk[tb % 2]
            dsl = DELC[:, tb * 512:(tb + 1) * 512]
            P.matmul_group([lambda nc, cc=cc, tb=tb, ps=ps: nc.tensor.matmul(
                ps, WDT[0:64, cc * 128:(cc + 1) * 128],
                DTT[0:64, tb * 512:(tb + 1) * 512], start=True, stop=True)],
                reads=[u + "wdt", u + "dtt"], writes=[bank], tag=u + "dl")
            P.op("vector", lambda nc, cc=cc, dsl=dsl, ps=ps: nc.vector.tensor_scalar_add(
                dsl, ps, BDT[:, cc:cc + 1]),
                reads=[bank, u + "BDT"], writes=[u + "delc"], tag=u + "dl")
            P.op("vector", lambda nc, dsl=dsl: nc.vector.tensor_scalar_mul(NEG, dsl, -1.0),
                 reads=[u + "delc"], writes=[u + "neg"], tag=u + "dl")
            P.op("vector", lambda nc, dsl=dsl: nc.vector.tensor_tensor(
                NEG, dsl, NEG, ALU.max),
                reads=[u + "delc", u + "neg"], writes=[u + "neg"], tag=u + "dl")
            P.op("scalar", lambda nc: nc.scalar.activation(EW, NEG, AF.Exp, scale=-1.0),
                 reads=[u + "neg"], writes=[u + "ew"], tag=u + "dl")
            P.op("scalar", lambda nc: nc.scalar.activation(NEG, EW, AF.Ln, bias=OCOL),
                 reads=[u + "ew", u + "ocol"], writes=[u + "neg"], tag=u + "dl")
            P.op("scalar", lambda nc, dsl=dsl: nc.scalar.activation(EW, dsl, AF.Relu),
                 reads=[u + "delc"], writes=[u + "ew"], tag=u + "dl")
            P.op("vector", lambda nc, dsl=dsl: nc.vector.tensor_tensor(
                dsl, NEG, EW, ALU.add),
                reads=[u + "neg", u + "ew"], writes=[u + "delc"], tag=u + "dl")
        P.dma(lambda nc, cc=cc: nc.sync.dma_start(out=scrDel[cc], in_=DELC),
              reads=[u + "delc"], writes=[u + f"Ddel{cc}"], queue="sync", tag=u + "sp")
        P.op("vector", lambda nc, cc=cc: nc.vector.tensor_mul(
            DXVC, DELC, XV[:, cc * T:(cc + 1) * T]),
            reads=[u + "delc", u + "XV"], writes=[u + "dxvc"], tag=u + "sp")
        P.dma(lambda nc, cc=cc: nc.sync.dma_start(out=scrDXV[cc], in_=DXVC),
              reads=[u + "dxvc"], writes=[u + f"Ddxv{cc}"], queue="sync", tag=u + "sp")
        P.dma(lambda nc, cc=cc: nc.sync.dma_start(
            out=scrXV[cc], in_=XV[:, cc * T:(cc + 1) * T]),
            reads=[u + "XV"], writes=[u + f"Dxv{cc}"], queue="sync", tag=u + "sp")
    c.close_scratch()
    c.close_scratch()

    c.open_scratch()
    OH = c.sb(u + "OH", [32, 32 * 128], BF16)
    P.dma(lambda nc: nc.gpsimd.dma_start(out=OH, in_=c.t["c_oh32"]),
          writes=[u + "oh"], queue="gpsimd", tag=u + "p3")
    load_weight_bf16(c, "L0a_Wout", u + "WOUT", DI, D)
    WOv = c.t[u + "WOUT"].rearrange("p (cd f) -> p cd f", f=D)
    CAR = c.sb(u + "CAR", [128, 256], F32)
    P.op("vector", lambda nc: nc.vector.memset(CAR, 0.0), writes=[u + "car"], tag=u + "p3")
    BPB = c.sb(u + "BPB", [128, 16 * 512], BF16)
    CPB = c.sb(u + "CPB", [128, 16 * 512], BF16)
    YB = c.sb(u + "YB", [128, 16 * 512], BF16)
    SSB = c.sb(u + "SSB", [32, 128], BF16)
    DELW = [c.sb(u + f"DELW{j}", [128, 512], F32) for j in range(2)]
    DXW = [c.sb(u + f"DXW{j}", [128, 512], BF16) for j in range(2)]
    XVW = [c.sb(u + f"XVW{j}", [128, 512], BF16) for j in range(2)]
    DA = [c.sb(u + f"DA{j}", [128, 512], BF16) for j in range(4)]
    DBX = [c.sb(u + f"DBX{j}", [128, 512], BF16) for j in range(4)]
    HS = [c.sb(u + f"HS{j}", [128, 512], F32) for j in range(4)]
    TMP = [c.sb(u + f"TMP{j}", [128, 512], BF16) for j in range(4)]
    YW = [c.sb(u + f"YW{j}", [128, 512], F32) for j in range(4)]
    TYW = c.sb(u + "TYW", [128, 512], F32)
    PSD = c.t["PSD"]
    bpv = BPB.rearrange("p (n f) -> p n f", f=512)
    cpv = CPB.rearrange("p (n f) -> p n f", f=512)
    pdv = PSD.rearrange("p (q f) -> p q f", f=128)
    for tb in range(4):
        for tl in range(4):
            i = tb * 4 + tl
            P.op("tensor", lambda nc, i=i: nc.tensor.transpose(
                PSC[0:32, 0:128], SSMtm[:, i * 96 + 64:i * 96 + 96], c.t["IDENTF"]),
                reads=[u + "ssmtm", "IDENTF"], writes=["PSCt"], tag=u + "bc")
            P.op("vector", lambda nc: nc.vector.tensor_copy(SSB, PSC[0:32, 0:128]),
                 reads=["PSCt"], writes=[u + "ssb"], tag=u + "bc")
            for gq in range(8):
                fns = []
                for q in range(4):
                    n0 = gq * 4 + q
                    fns.append(lambda nc, n0=n0, q=q: nc.tensor.matmul(
                        PSD[:, q * 128:(q + 1) * 128],
                        OH[:, n0 * 128:(n0 + 1) * 128], SSB,
                        start=True, stop=True))
                P.matmul_group(fns, reads=[u + "oh", u + "ssb"], writes=["PSDb"],
                               tag=u + "bc")
                dstv = bpv if gq < 4 else cpv
                dk = u + ("BPB" if gq < 4 else "CPB")
                g4 = (gq % 4) * 4
                P.op("vector", lambda nc, dstv=dstv, g4=g4, tl=tl: nc.vector.tensor_copy(
                    dstv[:, g4:g4 + 4, tl * 128:(tl + 1) * 128], pdv[:, 0:4, :]),
                    reads=["PSDb"], writes=[dk], tag=u + "bc")
        for cc in range(NDI):
            cp = cc % 2
            P.dma(lambda nc, cc=cc, tb=tb, cp=cp: nc.sync.dma_start(
                out=DELW[cp], in_=scrDel[cc][:, tb * 512:(tb + 1) * 512]),
                reads=[u + f"Ddel{cc}"], writes=[u + f"delw{cp}"], queue="sync",
                tag=u + "sc")
            P.dma(lambda nc, cc=cc, tb=tb, cp=cp: nc.sync.dma_start(
                out=DXW[cp], in_=scrDXV[cc][:, tb * 512:(tb + 1) * 512]),
                reads=[u + f"Ddxv{cc}"], writes=[u + f"dxw{cp}"], queue="sync",
                tag=u + "sc")
            P.dma(lambda nc, cc=cc, tb=tb, cp=cp: nc.sync.dma_start(
                out=XVW[cp], in_=scrXV[cc][:, tb * 512:(tb + 1) * 512]),
                reads=[u + f"Dxv{cc}"], writes=[u + f"xvw{cp}"], queue="sync",
                tag=u + "sc")
            import os as _os
            _ncut = 0 if _os.environ.get("L0A_CUT") == "1" else 16
            for n in range(_ncut):
                np_ = n % 4
                col = cc * 16 + n
                P.op("scalar", lambda nc, col=col, cp=cp, np_=np_: nc.scalar.activation(
                    DA[np_], DELW[cp], AF.Exp, scale=AN[:, col:col + 1]),
                    reads=[u + f"delw{cp}", u + "an"], writes=[u + f"da{np_}"],
                    tag=u + "sc")
                P.op("gpsimd", lambda nc, n=n, cp=cp, np_=np_: nc.gpsimd.tensor_mul(
                    DBX[np_], DXW[cp], BPB[:, n * 512:(n + 1) * 512]),
                    reads=[u + f"dxw{cp}", u + "BPB"], writes=[u + f"dbx{np_}"],
                    tag=u + "sc")
                P.op("vector", lambda nc, col=col, np_=np_: nc.vector.tensor_tensor_scan(
                    HS[np_], DA[np_], DBX[np_], CAR[:, col:col + 1], ALU.mult, ALU.add),
                    reads=[u + f"da{np_}", u + f"dbx{np_}", u + f"car{col}"],
                    writes=[u + f"hs{np_}"], tag=u + "sc")
                P.op("scalar", lambda nc, col=col, np_=np_: nc.scalar.copy(
                    CAR[:, col:col + 1], HS[np_][:, 511:512]),
                    reads=[u + f"hs{np_}"], writes=[u + f"car{col}"], tag=u + "sc")
                if n < 2:
                    P.op("gpsimd", lambda nc, n=n, np_=np_: nc.gpsimd.tensor_mul(
                        YW[np_], HS[np_], CPB[:, n * 512:(n + 1) * 512]),
                        reads=[u + f"hs{np_}", u + "CPB"], writes=[u + f"yw{np_}"],
                        tag=u + "sc")
                else:
                    P.op("gpsimd", lambda nc, n=n, np_=np_: nc.gpsimd.tensor_mul(
                        TMP[np_], HS[np_], CPB[:, n * 512:(n + 1) * 512]),
                        reads=[u + f"hs{np_}", u + "CPB"], writes=[u + f"tmp{np_}"],
                        tag=u + "sc")
                    P.op("vector", lambda nc, np_=np_: nc.vector.tensor_add(
                        YW[np_], YW[np_], TMP[np_]),
                        reads=[u + f"tmp{np_}", u + f"yw{np_}"],
                        writes=[u + f"yw{np_}"], tag=u + "sc")
            if _ncut:
                P.op("vector", lambda nc: nc.vector.tensor_add(TYW, YW[0], YW[1]),
                     reads=[u + "yw0", u + "yw1"], writes=[u + "tyw"], tag=u + "sc")
            else:
                P.op("vector", lambda nc: nc.vector.memset(TYW, 0.0),
                     writes=[u + "tyw"], tag=u + "sc")
            P.op("vector", lambda nc, cc=cc, cp=cp: nc.vector.scalar_tensor_tensor(
                YB[:, cc * 512:(cc + 1) * 512], XVW[cp], DCOL[:, cc:cc + 1], TYW,
                ALU.mult, ALU.add),
                reads=[u + f"xvw{cp}", u + "DCOL", u + "tyw"], writes=[u + "yb"],
                tag=u + "sc")
        xv_res = c.t[c.xin_name].rearrange("(n p) d -> p n d", p=128)
        ov_res = c.t[c.xout_name].rearrange("(n p) d -> p n d", p=128)
        for tl in range(4):
            i_t = tb * 4 + tl
            xt = c.t[f"XT{i_t % 2}"]
            xtk = f"XT{i_t % 2}"
            P.dma(lambda nc, i_t=i_t, xt=xt, xv_res=xv_res: nc.sync.dma_start(
                out=xt, in_=xv_res[:, i_t, :]),
                reads=[f"D{c.xin_name}_{i_t}"], writes=[xtk], queue="sync",
                tag=u + "res")
            for hf in range(2):
                ps = psf[hf]
                bank = psfk[hf]
                fns = []
                for cc in range(NDI):
                    fns.append(lambda nc, cc=cc, tl=tl, hf=hf, ps=ps: nc.tensor.matmul(
                        ps, YB[:, cc * 512 + tl * 128: cc * 512 + (tl + 1) * 128],
                        WOv[:, cc, hf * 512:(hf + 1) * 512],
                        start=(cc == 0), stop=(cc == NDI - 1)))
                P.matmul_group(fns, reads=[u + "yb"] + c.t[u + "WOUT_keys"],
                               writes=[bank], tag=u + "o")
                P.op("vector", lambda nc, hf=hf, ps=ps, xt=xt: nc.vector.tensor_add(
                    xt[:, hf * 512:(hf + 1) * 512], xt[:, hf * 512:(hf + 1) * 512], ps),
                    reads=[bank, xtk], writes=[xtk], tag=u + "res")
            P.dma(lambda nc, i_t=i_t, xt=xt, ov_res=ov_res: nc.sync.dma_start(
                out=ov_res[:, i_t, :], in_=xt),
                reads=[xtk], writes=[f"D{c.xout_name}_{i_t}"], queue="sync",
                tag=u + "res")
    c.close_scratch()
    c.close_scratch()


SUB_REGISTRY = {
    "L0_ffn": (declare_L0_ffn, stage_L0_ffn, emit_L0_ffn),
    "L1_attn_b": (declare_L1_attn_b, stage_L1_attn_b, emit_L1_attn_b),
    "L2_attn_a": (declare_L2_attn_a, stage_L2_attn_a, emit_L2_attn_a),
    "L3_ffn": (declare_L3_ffn, stage_L3_ffn, emit_L3_ffn),
    "L1_attn_a": (declare_L1_attn_a, stage_L1_attn_a, emit_L1_attn_a),
    "L0_attn_b": (declare_L0_attn_b, stage_L0_attn_b, emit_L0_attn_b),
    "L2_ffn": (declare_L2_ffn, stage_L2_ffn, emit_L2_ffn),
    "L3_attn_b": (declare_L3_attn_b, stage_L3_attn_b, emit_L3_attn_b),
    "L1_ffn": (declare_L1_ffn, stage_L1_ffn, emit_L1_ffn),
    "L2_attn_b": (declare_L2_attn_b, stage_L2_attn_b, emit_L2_attn_b),
    "L0_attn_a": (declare_L0_attn_a, stage_L0_attn_a, emit_L0_attn_a),
    "L3_attn_a": (declare_L3_attn_a, stage_L3_attn_a, emit_L3_attn_a),
}

CHAIN = [
    ("L0_attn_a", "s00"), ("L0_ffn", "s01"), ("L0_attn_b", "s02"), ("L0_ffn", "s03"),
    ("L1_attn_a", "s04"), ("L1_ffn", "s05"), ("L1_attn_b", "s06"), ("L1_ffn", "s07"),
    ("L2_attn_a", "s08"), ("L2_ffn", "s09"), ("L2_attn_b", "s10"), ("L2_ffn", "s11"),
    ("L3_attn_a", "s12"), ("L3_ffn", "s13"), ("L3_attn_b", "s14"), ("L3_ffn", "s15"),
]


def build_chain_program(names):
    nc = make_nc()
    P = Prog(nc)
    st = ExitStack()
    c = Ctx(nc, P, st)
    setup_common(c)
    declared = set()
    for name, uid in names:
        if name not in declared:
            SUB_REGISTRY[name][0](c)
            declared.add(name)
    if len(names) > 1:
        c.dram("xping", [T, D], F32)
        c.dram("xpong", [T, D], F32)
    for k, (name, uid) in enumerate(names):
        c.xin_name = "xin" if k == 0 else ("xping" if k % 2 == 1 else "xpong")
        c.xout_name = "xout" if k == len(names) - 1 else ("xping" if k % 2 == 0 else "xpong")
        keys_before = set(c.t.keys())
        c.open_scratch()
        if name == "L2_ffn":
            if (uid + "XC") not in c.t:
                c.sb(uid + "XC", [128, NT * D], BF16)
            emit_ln_fmajor(c, ln_id=k, out_bf16=uid + "XC")
        else:
            emit_ln_fmajor(c, ln_id=k)
        SUB_REGISTRY[name][2](c, uid=uid)
        c.close_scratch()
    P.build(st)
    return nc, c


def _const_inputs():
    return {"c_ident": np.eye(128, dtype=np.float32)}


def run_chain(x_full, names, inputs, core_ids=None):
    if core_ids is None:
        core_ids = list(range(8))
    nc, c = build_chain_program(names)
    staged = {}
    for name, uid in names:
        staged.update(SUB_REGISTRY[name][1](inputs))
    staged.update(_const_inputs())
    in_maps = []
    for k in range(len(core_ids)):
        m = dict(staged)
        m["xin"] = np.ascontiguousarray(x_full[k])
        in_maps.append(m)
    res = run_bass_kernel_spmd(nc, in_maps, core_ids=core_ids)
    out = np.stack([np.asarray(r["xout"]) for r in res.results], 0)
    c.st.close()
    return out


def kernel(**inputs):
    x = np.ascontiguousarray(np.asarray(inputs["L0_attn_a_x"], np.float32))
    return run_chain(x, CHAIN, inputs)
